# version 16
# baseline (speedup 1.0000x reference)
import math
import numpy as np
import ml_dtypes

import concourse.bass as bass
import concourse.mybir as mybir
from concourse.bass_utils import run_bass_kernel_spmd

F32 = mybir.dt.float32
BF16 = mybir.dt.bfloat16
AF = mybir.ActivationFunctionType
OP = mybir.AluOpType
AX = mybir.AxisListType

NCORES = 8

NF4_CODE = np.array([
    -1.0, -0.6961928009986877, -0.5250730514526367, -0.39491748809814453,
    -0.28444138169288635, -0.18477343022823334, -0.09105003625154495, 0.0,
    0.07958029955625534, 0.16093020141124725, 0.24611230194568634,
    0.33791524171829224, 0.44070982933044434, 0.5626170039176941,
    0.7229568362236023, 1.0], dtype=np.float32)
NF4_BOUNDS = ((NF4_CODE[:-1] + NF4_CODE[1:]) * np.float32(0.5)).astype(np.float32)
BIG = float(np.float32(1e30))
NF4_HDELTA = [float(np.float32(np.float64(NF4_CODE[j + 1]) - np.float64(NF4_CODE[j])) / np.float32(2.0))
              for j in range(15)]
AM_FLOOR = float(np.float32(1e-30))


def cfg_full():
    return dict(D=4096, HFF=11008, T=1024)


def derived(c):
    d = dict(c)
    d["ND"] = c["D"] // 128
    d["NT"] = c["T"] // 128
    d["NH"] = c["HFF"] // 128
    d["NHEADS"] = c["D"] // 128
    d["HT_SH"] = (d["NH"] + NCORES - 1) // NCORES
    d["DSH"] = c["D"] // NCORES
    d["DSHT"] = d["DSH"] // 128
    d["CSH"] = c["D"] // NCORES // 128
    nh = c["HFF"] // 128
    ch3 = []
    t0 = 0
    while t0 < nh:
        nt_ = min(32, nh - t0)
        ch3.append((t0, nt_))
        t0 += nt_
    d["CH3L"] = ch3
    d["MAXC3"] = max(nt_ for (_, nt_) in ch3) * 128
    nh = d["NH"]
    base = [nh // 4] * 4
    for i in range(nh % 4):
        base[i] += 1
    cuts = [0]
    for b in base:
        cuts.append(cuts[-1] + b)
    d["HCHUNKS"] = [(cuts[i], cuts[i + 1]) for i in range(4)]
    d["MAXCH"] = max(b for b in base)
    return d


SM_SCALE = float(np.float32(1.0) / np.sqrt(np.float32(128.0)))


class Walk:

    def __init__(self, role, eng, sems):
        self.role = role
        self.eng = eng
        self.sems = sems
        self.c = {k: 0 for k in sems}

    def wait(self, wrole, sem, val=None):
        if val is None:
            val = self.c[sem]
        if self.role == wrole:
            self.eng.wait_ge(self.sems[sem], val)

    def dma(self, qrole, sem, dst, src):
        self.c[sem] += 16
        if self.role == qrole:
            self.eng.dma_start(out=dst, in_=src).then_inc(self.sems[sem], 16)

    def _ser(self, role_, sem, opname, a, kw):
        tgt = self.c[sem]
        self.c[sem] += 1
        if self.role == role_:
            self.eng.wait_ge(self.sems[sem], tgt)
            getattr(self.eng, opname)(*a, **kw).then_inc(self.sems[sem], 1)

    def v(self, *a, **kw):
        self._ser("vector", "sV", a[0], a[1:], kw)

    def s(self, *a, **kw):
        self._ser("scalar", "sA", a[0], a[1:], kw)

    def p(self, *a, **kw):
        self._ser("gpsimd", "sP", a[0], a[1:], kw)

    def t(self, *a, inc=False, **kw):
        if inc:
            self.c["sT"] += 1
        if self.role == "tensor":
            inst = getattr(self.eng, a[0])(*a[1:], **kw)
            if inc:
                inst.then_inc(self.sems["sT"], 1)

    def cc(self, sem, ins, outs):
        self.cc2("gpsimd", sem, ins, outs)

    def cc2(self, role, sem, ins, outs):
        self.c[sem] += 1
        if self.role == role:
            self.eng.collective_compute(
                "AllGather", OP.bypass,
                replica_groups=[list(range(NCORES))],
                ins=ins, outs=outs,
            ).then_inc(self.sems[sem])


def build_program(cfg):
    c = derived(cfg)
    D, T, HFF = c["D"], c["T"], c["HFF"]
    ND, NT, NH = c["ND"], c["NT"], c["NH"]
    NHEADS, HT_SH, DSH, DSHT = c["NHEADS"], c["HT_SH"], c["DSH"], c["DSHT"]
    CSH = c["CSH"]
    HPAD = HT_SH * NCORES * 128
    SH_ROWS = HT_SH * 128

    nc = bass.Bass("TRN2", target_bir_lowering=False)

    x_in = nc.dram_tensor("x_in", [T, D], F32, kind="ExternalInput").ap()
    wq_in = nc.dram_tensor("wq_in", [DSH, D], F32, kind="ExternalInput").ap()
    wk_in = nc.dram_tensor("wk_in", [DSH, 128], F32, kind="ExternalInput").ap()
    wv_in = nc.dram_tensor("wv_in", [DSH, 128], F32, kind="ExternalInput").ap()
    bqT_in = nc.dram_tensor("bqT_in", [NHEADS, 128], F32, kind="ExternalInput").ap()
    bk_in = nc.dram_tensor("bk_in", [1, 128], F32, kind="ExternalInput").ap()
    bv_in = nc.dram_tensor("bv_in", [1, 128], F32, kind="ExternalInput").ap()
    w1_in = nc.dram_tensor("w1_in", [SH_ROWS, D], F32, kind="ExternalInput").ap()
    w2_in = nc.dram_tensor("w2_in", [SH_ROWS, D], F32, kind="ExternalInput").ap()
    w3_in = nc.dram_tensor("w3_in", [DSH, HFF], F32, kind="ExternalInput").ap()
    idb_in = nc.dram_tensor("idb_in", [128, 128], BF16, kind="ExternalInput").ap()
    idf_in = nc.dram_tensor("idf_in", [128, 128], F32, kind="ExternalInput").ap()
    cst_in = nc.dram_tensor("cst_in", [128, 40], F32, kind="ExternalInput").ap()
    out_t = nc.dram_tensor("out", [T, D], F32, kind="ExternalOutput").ap()

    ctrb_q = nc.dram_tensor("ctrb_q", [DSH, 384], BF16).ap()
    gath_q = nc.dram_tensor("gath_q", [D, 384], BF16).ap()
    W12G = [(g0, min(4, HT_SH - g0)) for g0 in range(0, HT_SH, 4)]
    c1g = [nc.dram_tensor("c1g%d" % i, [gs * D, 128], BF16).ap()
           for i, (g0, gs) in enumerate(W12G)]
    g1g = [nc.dram_tensor("g1g%d" % i, [NCORES * gs * D, 128], BF16).ap()
           for i, (g0, gs) in enumerate(W12G)]
    c2g = [nc.dram_tensor("c2g%d" % i, [gs * D, 128], BF16).ap()
           for i, (g0, gs) in enumerate(W12G)]
    g2g = [nc.dram_tensor("g2g%d" % i, [NCORES * gs * D, 128], BF16).ap()
           for i, (g0, gs) in enumerate(W12G)]
    c3g = [nc.dram_tensor("c3g%d" % rt, [HFF, 128], BF16).ap()
           for rt in range(DSHT)]
    g3g = [nc.dram_tensor("g3g%d" % rt, [NCORES * HFF, 128], BF16).ap()
           for rt in range(DSHT)]
    x2sp = nc.dram_tensor("x2sp", [D, T], F32).ap()
    part = nc.dram_tensor("part", [D, T], F32).ap()

    from contextlib import ExitStack
    with ExitStack() as ctx:
        def sb(name, shape, dt):
            return ctx.enter_context(nc.sbuf_tensor(name, shape, dt))

        W_all = sb("W_all", [128, ND * 384], BF16)
        ln2T = sb("ln2T", [128, ND * T], BF16)
        bias_row = sb("bias_row", [1, 384], F32)
        ones1 = sb("ones1", [1, 128], F32)
        ones32 = sb("ones32", [NHEADS, 1], F32)
        bqTs = sb("bqTs", [NHEADS, 128], F32)
        idnb = sb("idnb", [128, 128], BF16)
        idnf = sb("idnf", [128, 128], F32)
        cst = sb("cst", [128, 40], F32)
        AR_BYTES = (nc.sbuf_bytes_remaining - 64) & ~63
        arena = sb("arena", [128, AR_BYTES], mybir.dt.int8)

        def carve(off, nelem, dt, name=None):
            nb = nelem * (4 if dt == F32 else 2)
            assert off % 4 == 0 and off + nb <= AR_BYTES, (off, nb)
            return arena[:, off:off + nb].bitcast(dt), off + nb

        MAXC3 = c["MAXC3"]
        o = 0
        SCR = min(max(D, MAXC3), 2048)
        wfb = []
        for i in range(2):
            a, o = carve(o, SCR, F32)
            wfb.append(a)
        wn_, o = carve(o, SCR, F32)
        wnB_, o = carve(o, SCR, F32)
        wnP = [wn_, wnB_]
        tmpA, o = carve(o, SCR, F32)
        tmpB, o = carve(o, SCR, F32)
        accB_, o = carve(o, SCR, F32)
        accB2_, o = carve(o, SCR, F32)
        accP = [accB_, accB2_]
        dqb, o = carve(o, max(MAXC3, D), BF16)
        trb, o = carve(o, max(MAXC3, D), BF16)
        amb_, o = carve(o, 64, F32)
        amb2_, o = carve(o, 64, F32)
        ambP = [amb_, amb2_]
        recb_, o = carve(o, 64, F32)
        recb2_, o = carve(o, 64, F32)
        recbP = [recb_, recb2_]
        qcon, o = carve(o, DSHT * 384, BF16)
        tmpC, o = carve(o, SCR, F32)
        tmpD, o = carve(o, SCR, F32)
        dqbB, o = carve(o, max(MAXC3, D), BF16)
        trbB, o = carve(o, max(MAXC3, D), BF16)
        B_END = o
        o = 0
        xt = []
        for i in range(2):
            a, o = carve(o, D, F32)
            xt.append(a)
        attnA, o = carve(o, 32 * 128, F32)
        attnB_off = o
        attnB, o = carve(o, 32 * 128, F32)
        x2Tb, _ = carve(attnB_off, D, F32)
        ln1b, o = carve(o, D, BF16)
        ln2b, o = carve(o, D, BF16)
        ln1Tb, o = carve(o, D, BF16)
        x2t, o = carve(o, D, F32)
        qkv, o = carve(o, 384, F32)
        s_sb, o = carve(o, 128, F32)
        o_sb, o = carve(o, 128, F32)
        o2_sb, o = carve(o, 128, F32)
        rs_sb, o = carve(o, 128, F32)
        mx, o = carve(o, 128, F32)
        m1s, o = carve(o, 128, F32)
        m2s, o = carve(o, 128, F32)
        sm1, o = carve(o, 1, F32)
        sm2, o = carve(o, 1, F32)
        mmean, o = carve(o, 1, F32)
        ex2, o = carve(o, 1, F32)
        msq, o = carve(o, 1, F32)
        varb, o = carve(o, 1, F32)
        sdv, o = carve(o, 1, F32)
        rstd, o = carve(o, 1, F32)
        kmx, o = carve(o, 1, F32)
        kmn, o = carve(o, 1, F32)
        C_END = o
        o = 0
        hT, o = carve(o, c["MAXCH"] * T, BF16)
        st1 = []
        st2 = []
        st3 = []
        for i in range(2):
            a, o = carve(o, ND * 128, BF16)
            st1.append(a)
        for i in range(2):
            a, o = carve(o, ND * 128, BF16)
            st2.append(a)
        for i in range(2):
            a, o = carve(o, c["MAXCH"] * 128, BF16)
            st3.append(a)
        sgb, o = carve(o, T, F32)
        ptl = []
        for i in range(2):
            a, o = carve(o, T, F32)
            ptl.append(a)
        x2tt = []
        for i in range(2):
            a, o = carve(o, T, F32)
            x2tt.append(a)
        oev = []
        for i in range(2):
            a, o = carve(o, T, F32)
            oev.append(a)
        F_END = o
        assert max(B_END, C_END, F_END) <= AR_BYTES

        banks = []
        for i in range(8):
            banks.append(ctx.enter_context(
                nc.psum_tensor("bank%d" % i, [128, 512], F32)))

        NG12 = len([1 for g0 in range(0, HT_SH, 4)])
        NUNITS = 2 * NG12 + DSHT
        sem_names = ["sIN", "sWALL", "sBF0", "sBF1", "sX0", "sX1",
                     "sST0", "sST1", "sS30", "sS31", "sPT0", "sPT1",
                     "sV", "sA", "sT", "sG",
                     "sCCQ", "sOUT0", "sOUT1",
                     "sPW0", "sPW1", "sP", "sBF2_0", "sBF2_1", "sCT"]
        sem_names += ["sU%d" % i for i in range(NUNITS)]
        sems = {n: ctx.enter_context(nc.semaphore(n)) for n in sem_names}
        block = ctx.enter_context(nc.Block())

        def script(role, eng):
            w = Walk(role, eng, sems)
            pb16 = [b.ap().bitcast(BF16) for b in banks]
            pf32 = [b.ap() for b in banks]

            w.dma("sync", "sIN", cst[:], cst_in)
            w.dma("sync", "sIN", idnb[:], idb_in)
            w.dma("sync", "sIN", idnf[:], idf_in)
            w.dma("sync", "sIN", bqTs[:], bqT_in)
            w.dma("sync", "sIN", bias_row[0:1, 128:256], bk_in)
            w.dma("sync", "sIN", bias_row[0:1, 256:384], bv_in)
            setup_in = w.c["sIN"]
            w.wait("vector", "sIN", setup_in)
            w.v("memset", ones1[:], 1.0)
            w.v("memset", ones32[:], 1.0)
            wq_done = [0, 0]
            NQH = max(1, (D * 4) // (2048 * 4))
            QH = D // NQH
            for i in range(DSHT):
                for h2 in range(NQH):
                    sc = i * NQH + h2
                    w.wait("sync", "sV", wq_done[sc % 2])
                    w.dma("sync", "sBF%d" % (sc % 2), wfb[sc % 2][:, 0:QH],
                          wq_in.rearrange("(i p) d -> i p d", p=128)[i][
                              :, h2 * QH:(h2 + 1) * QH])
                    w.wait("vector", "sBF%d" % (sc % 2))
                    w.v("tensor_reduce", wn_[:, 0:QH // NHEADS],
                        wfb[sc % 2][:, 0:QH].rearrange(
                            "p (d h) -> p d h", h=NHEADS),
                        axis=AX.X, op=OP.add)
                    wq_done[sc % 2] = w.c["sV"]
                    w.v("tensor_copy",
                        qcon[:, i * 384 + h2 * (QH // NHEADS):
                             i * 384 + (h2 + 1) * (QH // NHEADS)],
                        wn_[:, 0:QH // NHEADS])
            qred_done = w.c["sV"]
            w.wait("gpsimd", "sIN", setup_in)
            for i in range(DSHT):
                w.dma("gpsimd", "sG", qcon[:, i * 384 + 128:i * 384 + 256],
                      wk_in.rearrange("(i p) e -> i p e", p=128)[i])
                w.dma("gpsimd", "sG", qcon[:, i * 384 + 256:i * 384 + 384],
                      wv_in.rearrange("(i p) e -> i p e", p=128)[i])
            w.wait("gpsimd", "sV", qred_done)
            w.wait("gpsimd", "sG")
            w.dma("gpsimd", "sG", ctrb_q.rearrange("(i p) q -> p i q", p=128),
                  qcon[:])
            w.wait("gpsimd", "sG")
            w.cc("sCCQ", [ctrb_q.opt()], [gath_q.opt()])
            w.wait("tensor", "sIN", setup_in)
            w.wait("tensor", "sV", qred_done)
            w.t("matmul", pf32[2][0:1, 0:128], ones32[:], bqTs[:],
                start=True, stop=True, inc=True)
            tb = w.c["sT"]
            w.wait("vector", "sT", tb)
            w.v("tensor_copy", bias_row[0:1, 0:128], pf32[2][0:1, 0:128])

            units = []
            for wi in range(HT_SH):
                units.append(("w1", wi))
            for wi in range(HT_SH):
                units.append(("w2", wi))
            for rt in range(DSHT):
                for chi in range(len(c["CH3L"])):
                    units.append(("w3", (rt, chi)))

            normdone = {}
            dq_done = {}
            tr_done = {}
            ev_done = {}
            ct_done = {}
            bank_ev = [0, 0]

            def subsplit(width, scr):
                sub = []
                off = 0
                while off < width:
                    rem = width - off
                    if rem <= scr:
                        sw = rem
                    elif rem >= 2 * scr:
                        sw = scr
                    else:
                        sw = 64 * ((rem // 2 + 63) // 64)
                    sub.append((off, sw))
                    off += sw
                return sub

            def transposes(ntiles, unit_key, dqsrc, trdst, psem, pcnt):
                ngr = (ntiles + 7) // 8
                w.wait("tensor", psem, pcnt)
                for g in range(ngr):
                    t0 = g * 8
                    tn = min(8, ntiles - t0)
                    bk = g % 2
                    w.wait("tensor", "sA", bank_ev[bk])
                    for tt in range(tn):
                        w.t("transpose", pb16[bk][:, (tt) * 128:(tt + 1) * 128],
                            dqsrc[:, (t0 + tt) * 128:(t0 + tt + 1) * 128],
                            idnb[:], inc=True)
                    w.wait("scalar", "sT")
                    w.s("copy", trdst[:, t0 * 128:(t0 + tn) * 128],
                        pb16[bk][:, 0:tn * 128])
                    bank_ev[bk] = w.c["sA"]
                tr_done[unit_key] = w.c["sT"]
                ev_done[unit_key] = w.c["sA"]

            def unit_src(kind, meta):
                if kind == "w1":
                    return (w1_in.rearrange("(i p) d -> i p d", p=128)[meta],
                            D, ND)
                if kind == "w2":
                    return (w2_in.rearrange("(i p) d -> i p d", p=128)[meta],
                            D, ND)
                rt, chi = meta
                cs, nt_ = c["CH3L"][chi]
                return (w3_in.rearrange("(i p) h -> i p h", p=128)[rt][
                    :, cs * 128:(cs + nt_) * 128], nt_ * 128, nt_)

            ccq = [0]

            def emit_cc(sem, ins_, outs_):
                ccq[0] += 1
                w.wait("gpsimd", "sCT")
                w.cc2("gpsimd", sem, ins_, outs_)

            def emit_contrib(kind, meta, trsrc, uidx):
                if kind in ("w1", "w2"):
                    ht = meta
                    g = ht // 4
                    g0, gs = W12G[g]
                    ctr = c1g[g] if kind == "w1" else c2g[g]
                    gt = g1g[g] if kind == "w1" else g2g[g]
                    w.dma("scalar", "sCT",
                          ctr[(ht - g0) * D:(ht - g0 + 1) * D, :].rearrange(
                              "(j d) h -> d j h", d=128),
                          trsrc[:, 0:ND * 128])
                    if ht == g0 + gs - 1:
                        sidx = g if kind == "w1" else NG12 + g
                        emit_cc("sU%d" % sidx, [ctr.opt()], [gt.opt()])
                else:
                    rt, chi = meta
                    cs, nt_ = c["CH3L"][chi]
                    ctr = c3g[rt]
                    gt = g3g[rt]
                    w.dma("scalar", "sCT",
                          ctr[cs * 128:(cs + nt_) * 128, :].rearrange(
                              "(j h) c -> h j c", h=128),
                          trsrc[:, 0:nt_ * 128])
                    if chi == len(c["CH3L"]) - 1:
                        emit_cc("sU%d" % (2 * NG12 + rt),
                                [ctr.opt()], [gt.opt()])

            ch_state = {"sign": 0, "tr": 0}
            pending = [None]

            def flush_pending():
                if pending[0] is None:
                    return
                (pk, pm, pntt, pdq, pdqbuf, ptrbuf, puidx) = pending[0]
                pending[0] = None
                w.wait("scalar", "sCT", w.c["sCT"])
                transposes(pntt, (pk, pm), pdqbuf, ptrbuf, pdq[0], pdq[1])
                ch_state["tr"] = tr_done[(pk, pm)]
                w.wait("scalar", "sA", ev_done[(pk, pm)])
                emit_contrib(pk, pm, ptrbuf, puidx)

            gends = set()
            for g0 in range(0, HT_SH, 4):
                ge = min(g0 + 4, HT_SH) - 1
                gends.add(ge)
                gends.add(HT_SH + ge)
            nw3c = len(c["CH3L"])
            for rt in range(DSHT):
                gends.add(2 * HT_SH + rt * nw3c + (nw3c - 1))
            pool_free = set()
            for ge in gends:
                pool_free.add(ge + 1)
                pool_free.add(ge + 2)

            subq = []
            for u, (kind, meta) in enumerate(units):
                srcap, width, ntt = unit_src(kind, meta)
                dqd = dqb if u % 2 == 0 else dqbB
                trd = trb if u % 2 == 0 else trbB
                subs = subsplit(width, 2048)
                for si, (soff, sw) in enumerate(subs):
                    subq.append(dict(u=u, kind=kind, meta=meta, ntt=ntt,
                                     srcap=srcap, soff=soff, sw=sw,
                                     dqd=dqd, trd=trd, np=(u in pool_free),
                                     first=(si == 0),
                                     last=(si == len(subs) - 1)))
            NSQ = len(subq)
            marks = {"norm": {}, "fin": {}, "sign": {},
                     "memset": {}, "stt14": {}}

            def b_dma(i):
                e = subq[i]
                if i >= 2:
                    ms, mv = marks["norm"][i - 2]
                    w.wait("sync", ms, mv)
                w.dma("sync", "sBF%d" % (i % 2), wfb[i % 2][:, 0:e["sw"]],
                      e["srcap"][:, e["soff"]:e["soff"] + e["sw"]])

            def b_prep(i):
                e = subq[i]
                par = i % 2
                sw = e["sw"]
                nb = sw // 64
                sv = wfb[par][:, 0:sw]
                w.wait("vector", "sBF%d" % par)
                if i >= 2:
                    fs, fv = marks["fin"].get(i - 2, ("sP", 0))
                    w.wait("vector", fs, fv)
                w.v("tensor_reduce", ambP[par][:, 0:nb],
                    sv.rearrange("p (b k) -> p b k", k=64),
                    axis=AX.X, op=OP.max, apply_absolute_value=True)
                w.v("tensor_scalar", ambP[par][:, 0:nb], ambP[par][:, 0:nb],
                    AM_FLOOR, None, op0=OP.max)
                w.v("reciprocal", recbP[par][:, 0:nb], ambP[par][:, 0:nb])
                role = "vector" if e["np"] else "gpsimd"
                emit = w.v if e["np"] else w.p
                if not e["np"]:
                    w.wait("gpsimd", "sV")
                if e["first"]:
                    w.wait(role, "sT", ch_state["tr"])
                if i >= 2:
                    w.wait(role, "sA", marks["sign"].get(i - 2, 0))
                emit("tensor_tensor", wnP[par][:, 0:sw], sv,
                     recbP[par][:, 0:nb].unsqueeze(2).broadcast_to(
                         [128, nb, 64]),
                     op=OP.mult)
                marks["norm"][i] = ("sV", w.c["sV"]) if e["np"] else (
                    "sP", w.c["sP"])
                emit("memset", accP[par][:, 0:sw], 0.0)
                marks["memset"][i] = ("sV", w.c["sV"]) if e["np"] else (
                    "sP", w.c["sP"])

            def b_steps(i):
                e = subq[i]
                par = i % 2
                sw = e["sw"]
                v0 = w.c["sV"]
                stt_cnt = {-4: v0, -3: v0, -2: v0, -1: v0}
                masks = [tmpA, tmpB, tmpC, tmpD]
                for j in range(15):
                    tb_ = masks[j % 4]
                    w.wait("scalar", "sV", stt_cnt[j - 4])
                    nmsem, nmval = marks["norm"][i]
                    w.wait("scalar", nmsem, nmval)
                    w.s("activation", tb_[:, 0:sw], wnP[par][:, 0:sw],
                        AF.Sign, bias=cst[:, 8 + j:9 + j])
                    marks["sign"][i] = w.c["sA"]
                    if j == 0:
                        mssem, msval = marks["memset"][i]
                        w.wait("vector", mssem, msval)
                    w.wait("vector", "sA")
                    w.v("scalar_tensor_tensor", accP[par][:, 0:sw],
                        tb_[:, 0:sw], NF4_HDELTA[j], accP[par][:, 0:sw],
                        op0=OP.mult, op1=OP.add)
                    stt_cnt[j] = w.c["sV"]
                marks["stt14"][i] = w.c["sV"]

            def b_fin(i):
                e = subq[i]
                par = i % 2
                sw = e["sw"]
                nb = sw // 64
                emit = w.v if e["np"] else w.p
                if not e["np"]:
                    w.wait("gpsimd", "sV", marks["stt14"][i])
                emit("tensor_tensor",
                     e["dqd"][:, e["soff"]:e["soff"] + e["sw"]],
                     accP[par][:, 0:sw],
                     ambP[par][:, 0:nb].unsqueeze(2).broadcast_to(
                         [128, nb, 64]),
                     op=OP.mult)
                marks["fin"][i] = ("sV", w.c["sV"]) if e["np"] else (
                    "sP", w.c["sP"])
                if e["last"]:
                    flush_pending()
                    pending[0] = (e["kind"], e["meta"], e["ntt"],
                                  marks["fin"][i], e["dqd"], e["trd"], e["u"])

            w.wait("sync", "sV", qred_done)
            b_dma(0)
            if NSQ > 1:
                b_dma(1)
            b_prep(0)
            for i in range(NSQ):
                if i + 2 < NSQ:
                    b_dma(i + 2)
                if i + 1 < NSQ:
                    b_prep(i + 1)
                b_steps(i)
                b_fin(i)
            flush_pending()
            stageB_v = w.c["sV"]
            stageB_t = w.c["sT"]
            stageB_a = w.c["sA"]

            w.wait("sync", "sCCQ", 1)
            w.wait("sync", "sV", stageB_v)
            w.wait("sync", "sT", stageB_t)
            w.wait("sync", "sA", stageB_a)
            w.wait("sync", "sG")
            w.wait("sync", "sCT")
            w.dma("sync", "sWALL", W_all[:],
                  gath_q.rearrange("(k d) q -> d k q", d=128))

            def layernorm(src, dstb):
                w.v("tensor_reduce", sm1[:, 0:1], src, axis=AX.X, op=OP.add)
                w.v("scalar_tensor_tensor", attnA[:, 0:D], src, 1.0, src,
                    op0=OP.mult, op1=OP.mult, accum_out=sm2[:, 0:1])
                w.v("tensor_scalar", mmean[:, 0:1], sm1[:, 0:1],
                    1.0 / D, None, op0=OP.mult)
                w.v("tensor_scalar", ex2[:, 0:1], sm2[:, 0:1],
                    1.0 / D, None, op0=OP.mult)
                w.v("tensor_scalar", msq[:, 0:1], mmean[:, 0:1],
                    mmean[:, 0:1], None, op0=OP.mult)
                w.v("tensor_tensor", varb[:, 0:1], ex2[:, 0:1], msq[:, 0:1],
                    op=OP.subtract)
                w.wait("scalar", "sV")
                w.s("activation", sdv[:, 0:1], varb[:, 0:1], AF.Ln,
                    bias=cst[:, 0:1])
                w.s("activation", rstd[:, 0:1], sdv[:, 0:1], AF.Exp,
                    scale=-0.5)
                w.wait("vector", "sA")
                w.v("tensor_scalar", dstb, src, mmean[:, 0:1], rstd[:, 0:1],
                    op0=OP.subtract, op1=OP.mult)

            def tpose_128(srcap, ntiles, ident, pbanks, dst_write, fp32=False):
                per = 4 if fp32 else 8
                ngr = (ntiles + per - 1) // per
                for g in range(ngr):
                    t0 = g * per
                    tn = min(per, ntiles - t0)
                    bk = pbanks[g % 2]
                    w.wait("tensor", "sA", bank_ev[g % 2])
                    for tt in range(tn):
                        dst = (pf32[bk][:, tt * 128:(tt + 1) * 128] if fp32
                               else pb16[bk][:, tt * 128:(tt + 1) * 128])
                        w.t("transpose", dst,
                            srcap[:, (t0 + tt) * 128:(t0 + tt + 1) * 128],
                            ident, inc=True)
                    w.wait("scalar", "sT")
                    dst_write(g, t0, tn, bk)
                    bank_ev[g % 2] = w.c["sA"]

            xt_free = [stageB_v, stageB_v]
            ln1_tpose_prev = stageB_t
            ln2_tpose = {-1: stageB_t}
            x2_tpose = {-1: stageB_t}
            qkv_evac = {-2: stageB_v, -1: stageB_v}
            qkv_mm = {}

            def ev_ln1(g, t0, tn, bk):
                w.s("copy", ln1Tb[:, t0 * 128:(t0 + tn) * 128],
                    pb16[bk][:, 0:tn * 128])

            def c_front(tau):
                nonlocal ln1_tpose_prev
                buf = xt[tau % 2]
                w.wait("sync", "sV", xt_free[tau % 2])
                w.dma("sync", "sX%d" % (tau % 2), buf[:, 0:D],
                      x_in.rearrange("(i p) d -> i p d", p=128)[tau])
                w.wait("vector", "sX%d" % (tau % 2))
                if tau == 0:
                    w.wait("vector", "sIN", setup_in)
                w.wait("vector", "sT", ln1_tpose_prev)
                layernorm(buf[:, 0:D], ln1b[:, 0:D])
                ln1_done = w.c["sV"]
                w.wait("tensor", "sV", ln1_done)
                tpose_128(ln1b, ND, idnb[:], [0, 1], ev_ln1)
                ln1_tpose_prev = w.c["sT"]
                ln1T_evac = w.c["sA"]
                bank = 2 + (tau % 2)
                w.wait("tensor", "sA", ln1T_evac)
                if tau == 0:
                    w.wait("tensor", "sWALL", 16)
                    w.wait("tensor", "sV")
                else:
                    w.wait("tensor", "sV", qkv_evac[tau - 2])
                for dt in range(ND):
                    w.t("matmul", pf32[bank][:, 0:384],
                        ln1Tb[:, dt * 128:(dt + 1) * 128],
                        W_all[:, dt * 384:(dt + 1) * 384],
                        start=(dt == 0), stop=False)
                w.t("matmul", pf32[bank][:, 0:384], ones1[:], bias_row[:],
                    start=False, stop=True, inc=True)
                qkv_mm[tau] = w.c["sT"]

            def c_back(tau):
                buf = xt[tau % 2]
                bank = 2 + (tau % 2)
                w.wait("vector", "sT", qkv_mm[tau])
                w.v("tensor_copy", qkv[:, 0:384], pf32[bank][:, 0:384])
                qkv_evac[tau] = w.c["sV"]
                w.v("tensor_reduce", kmx[:, 0:1], qkv[:, 128:256],
                    axis=AX.X, op=OP.max)
                w.v("tensor_reduce", kmn[:, 0:1], qkv[:, 128:256],
                    axis=AX.X, op=OP.min)
                w.v("tensor_scalar", m1s[:, 0:128], qkv[:, 0:128],
                    kmx[:, 0:1], None, op0=OP.mult)
                w.v("tensor_scalar", m2s[:, 0:128], qkv[:, 0:128],
                    kmn[:, 0:1], None, op0=OP.mult)
                w.v("tensor_tensor", mx[:, 0:128], m1s[:, 0:128],
                    m2s[:, 0:128], op=OP.max)
                exp_prev = None
                for q in range(4):
                    for dd in range(32):
                        d = 32 * q + dd
                        if dd == 0 and exp_prev is not None:
                            w.wait("vector", "sA", exp_prev)
                        w.v("tensor_scalar", attnA[:, dd * 128:(dd + 1) * 128],
                            qkv[:, 128:256], qkv[:, d:d + 1], mx[:, d:d + 1],
                            op0=OP.mult, op1=OP.subtract)
                    w.wait("scalar", "sV")
                    if q == 0:
                        w.wait("scalar", "sCT")
                    w.s("activation", attnB[:, 0:4096], attnA[:, 0:4096],
                        AF.Exp, scale=SM_SCALE)
                    exp_prev = w.c["sA"]
                    w.wait("vector", "sA", exp_prev)
                    w.v("tensor_reduce", s_sb[:, 32 * q:32 * q + 32],
                        attnB[:, 0:4096].rearrange("p (d e) -> p d e", e=128),
                        axis=AX.X, op=OP.add)
                    w.v("scalar_tensor_tensor", attnA[:, 0:4096],
                        attnB[:, 0:4096], 1.0,
                        qkv[:, 256:384].unsqueeze(1).broadcast_to(
                            [128, 32, 128]),
                        op0=OP.mult, op1=OP.mult)
                    w.v("tensor_reduce", o_sb[:, 32 * q:32 * q + 32],
                        attnA[:, 0:4096].rearrange("p (d e) -> p d e", e=128),
                        axis=AX.X, op=OP.add)
                w.v("reciprocal", rs_sb[:, 0:128], s_sb[:, 0:128])
                w.v("tensor_tensor", o2_sb[:, 0:128], o_sb[:, 0:128],
                    rs_sb[:, 0:128], op=OP.mult)
                w.wait("vector", "sT", x2_tpose[tau - 1])
                w.v("tensor_tensor", x2t[:, 0:D], buf[:, 0:D],
                    o2_sb[:, 0:128].unsqueeze(2).broadcast_to(
                        [128, 128, NHEADS]),
                    op=OP.add)
                xt_free[tau % 2] = w.c["sV"]
                w.wait("vector", "sT", ln2_tpose[tau - 1])
                layernorm(x2t[:, 0:D], ln2b[:, 0:D])
                ln2_done = w.c["sV"]
                w.wait("tensor", "sV", ln2_done)

                def ev_ln2(g, t0, tn, bk):
                    for tt in range(t0, t0 + tn):
                        w.s("copy", ln2T[:, tt * T + tau * 128:
                                         tt * T + (tau + 1) * 128],
                            pb16[bk][:, (tt - t0) * 128:(tt - t0 + 1) * 128])
                tpose_128(ln2b, ND, idnb[:], [0, 1], ev_ln2)
                ln2_tpose[tau] = w.c["sT"]
                x2sp_prev = w.c["sCT"]
                w.wait("tensor", "sCT", x2sp_prev)

                def ev_x2(g, t0, tn, bk):
                    w.s("copy", x2Tb[:, t0 * 128:(t0 + tn) * 128],
                        pf32[bk][:, 0:tn * 128])
                tpose_128(x2t, ND, idnf[:], [0, 1], ev_x2, fp32=True)
                x2_tpose[tau] = w.c["sT"]
                w.wait("scalar", "sA", w.c["sA"])
                w.dma("scalar", "sCT",
                      x2sp[:, tau * 128:(tau + 1) * 128].rearrange(
                          "(j c) t -> c j t", c=128),
                      x2Tb[:, 0:D])

            c_front(0)
            if NT > 1:
                c_front(1)
            for tau in range(NT):
                c_back(tau)
                if tau + 2 < NT:
                    c_front(tau + 2)
            stageC_v = w.c["sV"]
            stageC_t = w.c["sT"]
            stageC_a = w.c["sA"]
            stageC_g = w.c["sCT"]
            ln2T_done = stageC_a

            hmap = [(t // HT_SH, t % HT_SH) for t in range(NH)]
            for i in range(2 * NG12):
                w.wait("sync", "sU%d" % i, 1)
            w.wait("sync", "sG")
            w.wait("sync", "sCT")
            w.wait("sync", "sV", stageC_v)
            w.wait("sync", "sA", stageC_a)
            w.wait("sync", "sT", stageC_t)
            st_used = [0, 0]
            sg_used = stageC_v
            pend_out = []
            out_tp = {0: 0, 1: 0}

            def flush_out(ci, cb, sum_done):
                ob = oev[ci % 2]
                w.wait("tensor", "sV", sum_done)
                w.wait("tensor", "sOUT%d" % (ci % 2))

                def ev_out(g, t0, tn, bk):
                    w.s("copy", ob[:, t0 * 128:(t0 + tn) * 128],
                        pf32[bk][:, 0:tn * 128])
                tpose_128(ptl[cb], NT, idnf[:], [4, 5], ev_out, fp32=True)
                out_tp[cb] = w.c["sT"]
                w.wait("sync", "sA")
                w.dma("sync", "sOUT%d" % (ci % 2),
                      out_t[:, ci * 128:(ci + 1) * 128].rearrange(
                          "(i p) c -> p i c", p=128),
                      ob[:, 0:NT * 128])
            gu_mult = [0, 0]
            hT_free = stageC_v
            down_prev_t = stageC_t
            TH = min(512, T)
            NTH = T // TH
            for k, (k0, k1) in enumerate(c["HCHUNKS"]):
                nk = k1 - k0
                for jl, t in enumerate(range(k0, k1)):
                    own, idx = hmap[t]
                    jb = jl % 2
                    w.wait("sync", "sT", st_used[jb])
                    g = idx // 4
                    g0, gs = W12G[g]
                    r0 = (own * gs + (idx - g0)) * D
                    w.dma("sync", "sST%d" % jb, st1[jb][:, 0:ND * 128],
                          g1g[g][r0:r0 + D, :].rearrange(
                              "(j d) h -> d j h", d=128))
                    w.dma("sync", "sST%d" % jb, st2[jb][:, 0:ND * 128],
                          g2g[g][r0:r0 + D, :].rearrange(
                              "(j d) h -> d j h", d=128))
                    w.wait("tensor", "sST%d" % jb)
                    if jl == 0 and k == 0:
                        w.wait("tensor", "sA", ln2T_done)
                    if jl == 0:
                        w.wait("tensor", "sV", hT_free)
                    jp = jl % 2
                    GB = [0, 1][:NTH] if jp == 0 else [4, 5][:NTH]
                    UB = [2, 3][:NTH] if jp == 0 else [6, 7][:NTH]
                    if jl >= 2:
                        w.wait("tensor", "sV", gu_mult[jl % 2])
                    for dt in range(ND):
                        st = (dt == 0)
                        sp = (dt == ND - 1)
                        for hh in range(NTH):
                            rh = ln2T[:, dt * T + hh * TH:
                                      dt * T + (hh + 1) * TH]
                            w.t("matmul", pf32[GB[hh]][:, 0:TH],
                                st1[jb][:, dt * 128:(dt + 1) * 128],
                                rh, start=st, stop=sp, inc=sp)
                            w.t("matmul", pf32[UB[hh]][:, 0:TH],
                                st2[jb][:, dt * 128:(dt + 1) * 128],
                                rh, start=st, stop=sp, inc=sp)
                    st_used[jb] = w.c["sT"]
                    w.wait("scalar", "sT")
                    w.wait("scalar", "sV", sg_used)
                    for hh in range(NTH):
                        w.s("activation", sgb[:, hh * TH:(hh + 1) * TH],
                            pf32[GB[hh]][:, 0:TH], AF.Silu)
                    w.wait("vector", "sA")
                    if jl == 0:
                        w.wait("vector", "sT", down_prev_t)
                    for hh in range(NTH):
                        w.v("tensor_tensor",
                            hT[:, jl * T + hh * TH:jl * T + (hh + 1) * TH],
                            sgb[:, hh * TH:(hh + 1) * TH],
                            pf32[UB[hh]][:, 0:TH], op=OP.mult)
                    sg_used = w.c["sV"]
                    gu_mult[jl % 2] = w.c["sV"]
                hT_ready = w.c["sV"]
                dn_used = [0, 0]
                x2_used = [0, 0]
                dn_evac = [0, 0]
                for ci in range(ND):
                    own3 = ci // CSH
                    cloc = ci % CSH
                    cb = ci % 2
                    if k == 0 and ci == 0:
                        for rt2 in range(DSHT):
                            w.wait("sync", "sU%d" % (2 * NG12 + rt2), 1)
                    w.wait("sync", "sT", dn_used[cb])
                    r0 = own3 * HFF + k0 * 128
                    w.dma("sync", "sS3%d" % cb, st3[cb][:, 0:nk * 128],
                          g3g[cloc][r0:r0 + nk * 128, :].rearrange(
                              "(j h) c -> h j c", h=128))
                    if k > 0:
                        w.wait("sync", "sPW%d" % cb)
                        if k == len(c["HCHUNKS"]) - 1:
                            w.wait("sync", "sT", out_tp[cb])
                        w.dma("sync", "sPT%d" % cb, ptl[cb][:, 0:T],
                              part[ci * 128:(ci + 1) * 128, :])
                    last = (k == len(c["HCHUNKS"]) - 1)
                    if last:
                        w.wait("sync", "sCT", stageC_g)
                        w.wait("sync", "sV", x2_used[cb])
                        w.dma("sync", "sPT%d" % cb, x2tt[cb][:, 0:T],
                              x2sp[ci * 128:(ci + 1) * 128, :])
                    w.wait("tensor", "sS3%d" % cb)
                    w.wait("tensor", "sPT%d" % cb)
                    if ci == 0:
                        w.wait("tensor", "sV", hT_ready)
                    DB = [0, 1][:NTH] if cb == 0 else [2, 3][:NTH]
                    if ci >= 2:
                        w.wait("tensor", "sV", dn_evac[cb])
                    for jl in range(nk):
                        st = (jl == 0)
                        sp = (jl == nk - 1)
                        for hh in range(NTH):
                            w.t("matmul", pf32[DB[hh]][:, 0:TH],
                                st3[cb][:, jl * 128:(jl + 1) * 128],
                                hT[:, jl * T + hh * TH:
                                   jl * T + (hh + 1) * TH],
                                start=st, stop=sp, inc=sp)
                    dn_used[cb] = w.c["sT"]
                    w.wait("vector", "sT")
                    w.wait("vector", "sPW%d" % cb)
                    if not last:
                        for hh in range(NTH):
                            hs = slice(hh * TH, (hh + 1) * TH)
                            if k == 0:
                                w.v("tensor_copy", ptl[cb][:, hs],
                                    pf32[DB[hh]][:, 0:TH])
                            else:
                                w.v("tensor_tensor", ptl[cb][:, hs],
                                    ptl[cb][:, hs], pf32[DB[hh]][:, 0:TH],
                                    op=OP.add)
                        dn_evac[cb] = w.c["sV"]
                        w.wait("gpsimd", "sV")
                        w.dma("gpsimd", "sPW%d" % cb,
                              part[ci * 128:(ci + 1) * 128, :],
                              ptl[cb][:, 0:T])
                    else:
                        for hh in range(NTH):
                            hs = slice(hh * TH, (hh + 1) * TH)
                            w.v("tensor_tensor", ptl[cb][:, hs], ptl[cb][:, hs],
                                pf32[DB[hh]][:, 0:TH], op=OP.add)
                        w.v("tensor_tensor", ptl[cb][:, 0:T], ptl[cb][:, 0:T],
                            x2tt[cb][:, 0:T], op=OP.add)
                        x2_used[cb] = w.c["sV"]
                        dn_evac[cb] = w.c["sV"]
                        pend_out.append((ci, cb, w.c["sV"]))
                        if len(pend_out) > 1:
                            flush_out(*pend_out.pop(0))
                if last:
                    while pend_out:
                        flush_out(*pend_out.pop(0))
                hT_free = w.c["sV"]
                down_prev_t = w.c["sT"]
            w.wait("sync", "sOUT0")
            w.wait("sync", "sOUT1")
            return w

        @block.sync
        def _(eng):
            script("sync", eng)

        @block.vector
        def _(eng):
            script("vector", eng)

        @block.scalar
        def _(eng):
            script("scalar", eng)

        @block.tensor
        def _(eng):
            script("tensor", eng)

        @block.gpsimd
        def _(eng):
            script("gpsimd", eng)

    return nc


def shard_inputs(inputs, cfg):
    c = derived(cfg)
    D, T, HFF = c["D"], c["T"], c["HFF"]
    SH_ROWS = c["HT_SH"] * 128
    bf = ml_dtypes.bfloat16

    x = np.ascontiguousarray(np.asarray(
        inputs["L0_attn_x"], dtype=np.float32)).reshape(-1, D)
    wq = np.asarray(inputs["L0_attn_Wq"], dtype=np.float32).reshape(D, -1)
    bq = np.asarray(inputs["L0_attn_bq"], dtype=np.float32)
    wk = np.asarray(inputs["L0_attn_Wk"], dtype=np.float32).reshape(D, 128)
    bk = np.asarray(inputs["L0_attn_bk"], dtype=np.float32).reshape(128)
    wv = np.asarray(inputs["L0_attn_Wv"], dtype=np.float32).reshape(D, 128)
    bv = np.asarray(inputs["L0_attn_bv"], dtype=np.float32).reshape(128)
    w1 = np.asarray(inputs["L0_ffn_w1"], dtype=np.float32)
    w2 = np.asarray(inputs["L0_ffn_w2"], dtype=np.float32)
    w3 = np.asarray(inputs["L0_ffn_w3"], dtype=np.float32)

    w1p = np.zeros((SH_ROWS * NCORES, D), dtype=np.float32)
    w1p[:HFF] = w1
    w2p = np.zeros((SH_ROWS * NCORES, D), dtype=np.float32)
    w2p[:HFF] = w2

    bqT = np.ascontiguousarray(bq.T)
    idb = np.eye(128, dtype=np.float32).astype(bf)
    idf = np.eye(128, dtype=np.float32)
    cstv = np.zeros((128, 40), dtype=np.float32)
    cstv[:, 0] = np.float32(1e-5)
    for j in range(15):
        cstv[:, 8 + j] = -NF4_BOUNDS[j]
        cstv[:, 24 + j] = NF4_HDELTA[j]
    cstv[:, 39] = AM_FLOOR

    dsh = D // NCORES
    in_maps = []
    for r in range(NCORES):
        in_maps.append(dict(
            x_in=np.ascontiguousarray(x[r * T:(r + 1) * T]),
            wq_in=np.ascontiguousarray(wq[r * dsh:(r + 1) * dsh]),
            wk_in=np.ascontiguousarray(wk[r * dsh:(r + 1) * dsh]),
            wv_in=np.ascontiguousarray(wv[r * dsh:(r + 1) * dsh]),
            bqT_in=bqT,
            bk_in=np.ascontiguousarray(bk.reshape(1, 128)),
            bv_in=np.ascontiguousarray(bv.reshape(1, 128)),
            w1_in=np.ascontiguousarray(w1p[r * SH_ROWS:(r + 1) * SH_ROWS]),
            w2_in=np.ascontiguousarray(w2p[r * SH_ROWS:(r + 1) * SH_ROWS]),
            w3_in=np.ascontiguousarray(w3[r * dsh:(r + 1) * dsh]),
            idb_in=idb, idf_in=idf, cst_in=cstv,
        ))
    return in_maps


def kernel(**inputs):
    cfg = cfg_full()
    in_maps = shard_inputs(inputs, cfg)
    nc = build_program(cfg)
    res = run_bass_kernel_spmd(nc, in_maps, core_ids=list(range(NCORES)))
    out = np.concatenate([np.asarray(r["out"]) for r in res.results], axis=0)
    return out.reshape(4, 2048, 4096)


# version 17
# speedup vs baseline: 1.1018x; 1.1018x over previous
import math
import numpy as np
import ml_dtypes

import concourse.bass as bass
import concourse.mybir as mybir
from concourse.bass_utils import run_bass_kernel_spmd

F32 = mybir.dt.float32
BF16 = mybir.dt.bfloat16
AF = mybir.ActivationFunctionType
OP = mybir.AluOpType
AX = mybir.AxisListType

NCORES = 8

NF4_CODE = np.array([
    -1.0, -0.6961928009986877, -0.5250730514526367, -0.39491748809814453,
    -0.28444138169288635, -0.18477343022823334, -0.09105003625154495, 0.0,
    0.07958029955625534, 0.16093020141124725, 0.24611230194568634,
    0.33791524171829224, 0.44070982933044434, 0.5626170039176941,
    0.7229568362236023, 1.0], dtype=np.float32)
NF4_BOUNDS = ((NF4_CODE[:-1] + NF4_CODE[1:]) * np.float32(0.5)).astype(np.float32)
BIG = float(np.float32(1e30))
NF4_HDELTA = [float(np.float32(np.float64(NF4_CODE[j + 1]) - np.float64(NF4_CODE[j])) / np.float32(2.0))
              for j in range(15)]
AM_FLOOR = float(np.float32(1e-30))


def cfg_full():
    return dict(D=4096, HFF=11008, T=1024)


def derived(c):
    d = dict(c)
    d["ND"] = c["D"] // 128
    d["NT"] = c["T"] // 128
    d["NH"] = c["HFF"] // 128
    d["NHEADS"] = c["D"] // 128
    d["HT_SH"] = (d["NH"] + NCORES - 1) // NCORES
    d["DSH"] = c["D"] // NCORES
    d["DSHT"] = d["DSH"] // 128
    d["CSH"] = c["D"] // NCORES // 128
    nh = c["HFF"] // 128
    ch3 = []
    t0 = 0
    while t0 < nh:
        nt_ = min(32, nh - t0)
        ch3.append((t0, nt_))
        t0 += nt_
    d["CH3L"] = ch3
    d["MAXC3"] = max(nt_ for (_, nt_) in ch3) * 128
    nh = d["NH"]
    base = [nh // 4] * 4
    for i in range(nh % 4):
        base[i] += 1
    cuts = [0]
    for b in base:
        cuts.append(cuts[-1] + b)
    d["HCHUNKS"] = [(cuts[i], cuts[i + 1]) for i in range(4)]
    d["MAXCH"] = max(b for b in base)
    return d


SM_SCALE = float(np.float32(1.0) / np.sqrt(np.float32(128.0)))


class Walk:

    def __init__(self, role, eng, sems):
        self.role = role
        self.eng = eng
        self.sems = sems
        self.c = {k: 0 for k in sems}

    def wait(self, wrole, sem, val=None):
        if val is None:
            val = self.c[sem]
        if self.role == wrole:
            self.eng.wait_ge(self.sems[sem], val)

    def dma(self, qrole, sem, dst, src):
        self.c[sem] += 16
        if self.role == qrole:
            self.eng.dma_start(out=dst, in_=src).then_inc(self.sems[sem], 16)

    def _ser(self, role_, sem, opname, a, kw):
        tgt = self.c[sem]
        self.c[sem] += 1
        if self.role == role_:
            self.eng.wait_ge(self.sems[sem], tgt)
            getattr(self.eng, opname)(*a, **kw).then_inc(self.sems[sem], 1)

    def v(self, *a, **kw):
        self._ser("vector", "sV", a[0], a[1:], kw)

    def s(self, *a, **kw):
        self._ser("scalar", "sA", a[0], a[1:], kw)

    def p(self, *a, **kw):
        self._ser("gpsimd", "sP", a[0], a[1:], kw)

    def t(self, *a, inc=False, **kw):
        if inc:
            self.c["sT"] += 1
        if self.role == "tensor":
            inst = getattr(self.eng, a[0])(*a[1:], **kw)
            if inc:
                inst.then_inc(self.sems["sT"], 1)

    def cc(self, sem, ins, outs):
        self.cc2("gpsimd", sem, ins, outs)

    def cc2(self, role, sem, ins, outs):
        self.c[sem] += 1
        if self.role == role:
            self.eng.collective_compute(
                "AllGather", OP.bypass,
                replica_groups=[list(range(NCORES))],
                ins=ins, outs=outs,
            ).then_inc(self.sems[sem])


def build_program(cfg):
    c = derived(cfg)
    D, T, HFF = c["D"], c["T"], c["HFF"]
    ND, NT, NH = c["ND"], c["NT"], c["NH"]
    NHEADS, HT_SH, DSH, DSHT = c["NHEADS"], c["HT_SH"], c["DSH"], c["DSHT"]
    CSH = c["CSH"]
    HPAD = HT_SH * NCORES * 128
    SH_ROWS = HT_SH * 128

    nc = bass.Bass("TRN2", target_bir_lowering=False)

    x_in = nc.dram_tensor("x_in", [T, D], F32, kind="ExternalInput").ap()
    wq_in = nc.dram_tensor("wq_in", [DSH, D], F32, kind="ExternalInput").ap()
    wk_in = nc.dram_tensor("wk_in", [DSH, 128], F32, kind="ExternalInput").ap()
    wv_in = nc.dram_tensor("wv_in", [DSH, 128], F32, kind="ExternalInput").ap()
    bqT_in = nc.dram_tensor("bqT_in", [NHEADS, 128], F32, kind="ExternalInput").ap()
    bk_in = nc.dram_tensor("bk_in", [1, 128], F32, kind="ExternalInput").ap()
    bv_in = nc.dram_tensor("bv_in", [1, 128], F32, kind="ExternalInput").ap()
    w1_in = nc.dram_tensor("w1_in", [SH_ROWS, D], F32, kind="ExternalInput").ap()
    w2_in = nc.dram_tensor("w2_in", [SH_ROWS, D], F32, kind="ExternalInput").ap()
    w3_in = nc.dram_tensor("w3_in", [DSH, HFF], F32, kind="ExternalInput").ap()
    idb_in = nc.dram_tensor("idb_in", [128, 128], BF16, kind="ExternalInput").ap()
    idf_in = nc.dram_tensor("idf_in", [128, 128], F32, kind="ExternalInput").ap()
    cst_in = nc.dram_tensor("cst_in", [128, 40], F32, kind="ExternalInput").ap()
    out_t = nc.dram_tensor("out", [T, D], F32, kind="ExternalOutput").ap()

    ctrb_q = nc.dram_tensor("ctrb_q", [DSH, 384], BF16).ap()
    gath_q = nc.dram_tensor("gath_q", [D, 384], BF16).ap()
    W12G = [(g0, min(4, HT_SH - g0)) for g0 in range(0, HT_SH, 4)]
    c1g = [nc.dram_tensor("c1g%d" % i, [gs * D, 128], BF16).ap()
           for i, (g0, gs) in enumerate(W12G)]
    g1g = [nc.dram_tensor("g1g%d" % i, [NCORES * gs * D, 128], BF16).ap()
           for i, (g0, gs) in enumerate(W12G)]
    c2g = [nc.dram_tensor("c2g%d" % i, [gs * D, 128], BF16).ap()
           for i, (g0, gs) in enumerate(W12G)]
    g2g = [nc.dram_tensor("g2g%d" % i, [NCORES * gs * D, 128], BF16).ap()
           for i, (g0, gs) in enumerate(W12G)]
    c3g = [nc.dram_tensor("c3g%d" % rt, [HFF, 128], BF16).ap()
           for rt in range(DSHT)]
    g3g = [nc.dram_tensor("g3g%d" % rt, [NCORES * HFF, 128], BF16).ap()
           for rt in range(DSHT)]
    x2sp = nc.dram_tensor("x2sp", [D, T], F32).ap()
    part = nc.dram_tensor("part", [D, T], F32).ap()

    from contextlib import ExitStack
    with ExitStack() as ctx:
        def sb(name, shape, dt):
            return ctx.enter_context(nc.sbuf_tensor(name, shape, dt))

        W_all = sb("W_all", [128, ND * 384], BF16)
        ln2T = sb("ln2T", [128, ND * T], BF16)
        bias_row = sb("bias_row", [1, 384], F32)
        ones1 = sb("ones1", [1, 128], F32)
        ones32 = sb("ones32", [NHEADS, 1], F32)
        bqTs = sb("bqTs", [NHEADS, 128], F32)
        idnb = sb("idnb", [128, 128], BF16)
        idnf = sb("idnf", [128, 128], F32)
        cst = sb("cst", [128, 40], F32)
        AR_BYTES = (nc.sbuf_bytes_remaining - 64) & ~63
        arena = sb("arena", [128, AR_BYTES], mybir.dt.int8)

        def carve(off, nelem, dt, name=None):
            nb = nelem * (4 if dt == F32 else 2)
            assert off % 4 == 0 and off + nb <= AR_BYTES, (off, nb)
            return arena[:, off:off + nb].bitcast(dt), off + nb

        MAXC3 = c["MAXC3"]
        o = 0
        SCR = min(max(D, MAXC3), 2048)
        wfb = []
        for i in range(2):
            a, o = carve(o, SCR, F32)
            wfb.append(a)
        wn_, o = carve(o, SCR, F32)
        wnB_, o = carve(o, SCR, F32)
        wnP = [wn_, wnB_]
        tmpA, o = carve(o, SCR, F32)
        tmpB, o = carve(o, SCR, F32)
        accB_, o = carve(o, SCR, F32)
        accB2_, o = carve(o, SCR, F32)
        accP = [accB_, accB2_]
        dqb, o = carve(o, max(MAXC3, D), BF16)
        trb, o = carve(o, max(MAXC3, D), BF16)
        amb_, o = carve(o, 64, F32)
        amb2_, o = carve(o, 64, F32)
        ambP = [amb_, amb2_]
        recb_, o = carve(o, 64, F32)
        recb2_, o = carve(o, 64, F32)
        recbP = [recb_, recb2_]
        qcon, o = carve(o, DSHT * 384, BF16)
        tmpC, o = carve(o, SCR, F32)
        tmpD, o = carve(o, SCR, F32)
        dqbB, o = carve(o, max(MAXC3, D), BF16)
        trbB, o = carve(o, max(MAXC3, D), BF16)
        B_END = o
        o = 0
        xt = []
        for i in range(2):
            a, o = carve(o, D, F32)
            xt.append(a)
        attnA, o = carve(o, 32 * 128, F32)
        attnB_off = o
        attnB, o = carve(o, 32 * 128, F32)
        x2Tb, _ = carve(attnB_off, D, F32)
        ln1b, o = carve(o, D, BF16)
        ln2b, o = carve(o, D, BF16)
        ln1Tb, o = carve(o, D, BF16)
        x2t, o = carve(o, D, F32)
        qkv, o = carve(o, 384, F32)
        s_sb, o = carve(o, 128, F32)
        o_sb, o = carve(o, 128, F32)
        o2_sb, o = carve(o, 128, F32)
        rs_sb, o = carve(o, 128, F32)
        mx, o = carve(o, 128, F32)
        m1s, o = carve(o, 128, F32)
        m2s, o = carve(o, 128, F32)
        sm1, o = carve(o, 1, F32)
        sm2, o = carve(o, 1, F32)
        mmean, o = carve(o, 1, F32)
        ex2, o = carve(o, 1, F32)
        msq, o = carve(o, 1, F32)
        varb, o = carve(o, 1, F32)
        sdv, o = carve(o, 1, F32)
        rstd, o = carve(o, 1, F32)
        kmx, o = carve(o, 1, F32)
        kmn, o = carve(o, 1, F32)
        C_END = o
        o = 0
        hT, o = carve(o, c["MAXCH"] * T, BF16)
        st1 = []
        st2 = []
        st3 = []
        for i in range(2):
            a, o = carve(o, ND * 128, BF16)
            st1.append(a)
        for i in range(2):
            a, o = carve(o, ND * 128, BF16)
            st2.append(a)
        for i in range(2):
            a, o = carve(o, c["MAXCH"] * 128, BF16)
            st3.append(a)
        sgb, o = carve(o, T, F32)
        ptl = []
        for i in range(2):
            a, o = carve(o, T, F32)
            ptl.append(a)
        x2tt = []
        for i in range(2):
            a, o = carve(o, T, F32)
            x2tt.append(a)
        oev = []
        for i in range(2):
            a, o = carve(o, T, F32)
            oev.append(a)
        F_END = o
        assert max(B_END, C_END, F_END) <= AR_BYTES

        banks = []
        for i in range(8):
            banks.append(ctx.enter_context(
                nc.psum_tensor("bank%d" % i, [128, 512], F32)))

        NG12 = len([1 for g0 in range(0, HT_SH, 4)])
        NUNITS = 2 * NG12 + DSHT
        sem_names = ["sIN", "sWALL", "sBF0", "sBF1", "sX0", "sX1",
                     "sST0", "sST1", "sS30", "sS31", "sPT0", "sPT1",
                     "sV", "sA", "sT", "sG",
                     "sCCQ", "sOUT0", "sOUT1",
                     "sPW0", "sPW1", "sP", "sBF2_0", "sBF2_1", "sCT"]
        sem_names += ["sU%d" % i for i in range(NUNITS)]
        sems = {n: ctx.enter_context(nc.semaphore(n)) for n in sem_names}
        block = ctx.enter_context(nc.Block())

        def script(role, eng):
            w = Walk(role, eng, sems)
            pb16 = [b.ap().bitcast(BF16) for b in banks]
            pf32 = [b.ap() for b in banks]

            w.dma("sync", "sIN", cst[:], cst_in)
            w.dma("sync", "sIN", idnb[:], idb_in)
            w.dma("sync", "sIN", idnf[:], idf_in)
            w.dma("sync", "sIN", bqTs[:], bqT_in)
            w.dma("sync", "sIN", bias_row[0:1, 128:256], bk_in)
            w.dma("sync", "sIN", bias_row[0:1, 256:384], bv_in)
            setup_in = w.c["sIN"]
            w.wait("vector", "sIN", setup_in)
            w.v("memset", ones1[:], 1.0)
            w.v("memset", ones32[:], 1.0)
            wq_done = [0, 0]
            NQH = max(1, (D * 4) // (2048 * 4))
            QH = D // NQH
            for i in range(DSHT):
                for h2 in range(NQH):
                    sc = i * NQH + h2
                    w.wait("sync", "sV", wq_done[sc % 2])
                    w.dma("sync", "sBF%d" % (sc % 2), wfb[sc % 2][:, 0:QH],
                          wq_in.rearrange("(i p) d -> i p d", p=128)[i][
                              :, h2 * QH:(h2 + 1) * QH])
                    w.wait("vector", "sBF%d" % (sc % 2))
                    w.v("tensor_reduce", wn_[:, 0:QH // NHEADS],
                        wfb[sc % 2][:, 0:QH].rearrange(
                            "p (d h) -> p d h", h=NHEADS),
                        axis=AX.X, op=OP.add)
                    wq_done[sc % 2] = w.c["sV"]
                    w.v("tensor_copy",
                        qcon[:, i * 384 + h2 * (QH // NHEADS):
                             i * 384 + (h2 + 1) * (QH // NHEADS)],
                        wn_[:, 0:QH // NHEADS])
            qred_done = w.c["sV"]
            w.wait("gpsimd", "sIN", setup_in)
            for i in range(DSHT):
                w.dma("gpsimd", "sG", qcon[:, i * 384 + 128:i * 384 + 256],
                      wk_in.rearrange("(i p) e -> i p e", p=128)[i])
                w.dma("gpsimd", "sG", qcon[:, i * 384 + 256:i * 384 + 384],
                      wv_in.rearrange("(i p) e -> i p e", p=128)[i])
            w.wait("gpsimd", "sV", qred_done)
            w.wait("gpsimd", "sG")
            w.dma("gpsimd", "sG", ctrb_q.rearrange("(i p) q -> p i q", p=128),
                  qcon[:])
            w.wait("gpsimd", "sG")
            w.cc("sCCQ", [ctrb_q.opt()], [gath_q.opt()])
            w.wait("tensor", "sIN", setup_in)
            w.wait("tensor", "sV", qred_done)
            w.t("matmul", pf32[2][0:1, 0:128], ones32[:], bqTs[:],
                start=True, stop=True, inc=True)
            tb = w.c["sT"]
            w.wait("vector", "sT", tb)
            w.v("tensor_copy", bias_row[0:1, 0:128], pf32[2][0:1, 0:128])

            units = []
            for wi in range(HT_SH):
                units.append(("w1", wi))
            for wi in range(HT_SH):
                units.append(("w2", wi))
            for rt in range(DSHT):
                for chi in range(len(c["CH3L"])):
                    units.append(("w3", (rt, chi)))

            normdone = {}
            dq_done = {}
            tr_done = {}
            ev_done = {}
            ct_done = {}
            bank_ev = [0, 0]

            def subsplit(width, scr):
                sub = []
                off = 0
                while off < width:
                    rem = width - off
                    if rem <= scr:
                        sw = rem
                    elif rem >= 2 * scr:
                        sw = scr
                    else:
                        sw = 64 * ((rem // 2 + 63) // 64)
                    sub.append((off, sw))
                    off += sw
                return sub

            def transposes(ntiles, unit_key, dqsrc, trdst, psem, pcnt):
                ngr = (ntiles + 7) // 8
                w.wait("tensor", psem, pcnt)
                for g in range(ngr):
                    t0 = g * 8
                    tn = min(8, ntiles - t0)
                    bk = g % 2
                    w.wait("tensor", "sA", bank_ev[bk])
                    for tt in range(tn):
                        w.t("transpose", pb16[bk][:, (tt) * 128:(tt + 1) * 128],
                            dqsrc[:, (t0 + tt) * 128:(t0 + tt + 1) * 128],
                            idnb[:], inc=True)
                    w.wait("scalar", "sT")
                    w.s("copy", trdst[:, t0 * 128:(t0 + tn) * 128],
                        pb16[bk][:, 0:tn * 128])
                    bank_ev[bk] = w.c["sA"]
                tr_done[unit_key] = w.c["sT"]
                ev_done[unit_key] = w.c["sA"]

            def unit_src(kind, meta):
                if kind == "w1":
                    return (w1_in.rearrange("(i p) d -> i p d", p=128)[meta],
                            D, ND)
                if kind == "w2":
                    return (w2_in.rearrange("(i p) d -> i p d", p=128)[meta],
                            D, ND)
                rt, chi = meta
                cs, nt_ = c["CH3L"][chi]
                return (w3_in.rearrange("(i p) h -> i p h", p=128)[rt][
                    :, cs * 128:(cs + nt_) * 128], nt_ * 128, nt_)

            ccq = [0]

            def emit_cc(sem, ins_, outs_):
                ccq[0] += 1
                w.wait("gpsimd", "sCT")
                w.cc2("gpsimd", sem, ins_, outs_)

            def emit_contrib(kind, meta, trsrc, uidx):
                if kind in ("w1", "w2"):
                    ht = meta
                    g = ht // 4
                    g0, gs = W12G[g]
                    ctr = c1g[g] if kind == "w1" else c2g[g]
                    gt = g1g[g] if kind == "w1" else g2g[g]
                    w.dma("scalar", "sCT",
                          ctr[(ht - g0) * D:(ht - g0 + 1) * D, :].rearrange(
                              "(j d) h -> d j h", d=128),
                          trsrc[:, 0:ND * 128])
                    if ht == g0 + gs - 1:
                        sidx = g if kind == "w1" else NG12 + g
                        emit_cc("sU%d" % sidx, [ctr.opt()], [gt.opt()])
                else:
                    rt, chi = meta
                    cs, nt_ = c["CH3L"][chi]
                    ctr = c3g[rt]
                    gt = g3g[rt]
                    w.dma("scalar", "sCT",
                          ctr[cs * 128:(cs + nt_) * 128, :].rearrange(
                              "(j h) c -> h j c", h=128),
                          trsrc[:, 0:nt_ * 128])
                    if chi == len(c["CH3L"]) - 1:
                        emit_cc("sU%d" % (2 * NG12 + rt),
                                [ctr.opt()], [gt.opt()])

            ch_state = {"sign": 0, "tr": 0}
            pending = [None]

            def flush_pending():
                if pending[0] is None:
                    return
                (pk, pm, pntt, pdq, pdqbuf, ptrbuf, puidx) = pending[0]
                pending[0] = None
                w.wait("scalar", "sCT", w.c["sCT"])
                transposes(pntt, (pk, pm), pdqbuf, ptrbuf, pdq[0], pdq[1])
                ch_state["tr"] = tr_done[(pk, pm)]
                w.wait("scalar", "sA", ev_done[(pk, pm)])
                emit_contrib(pk, pm, ptrbuf, puidx)

            gends = set()
            for g0 in range(0, HT_SH, 4):
                ge = min(g0 + 4, HT_SH) - 1
                gends.add(ge)
                gends.add(HT_SH + ge)
            nw3c = len(c["CH3L"])
            for rt in range(DSHT):
                gends.add(2 * HT_SH + rt * nw3c + (nw3c - 1))
            pool_free = set()

            subq = []
            for u, (kind, meta) in enumerate(units):
                srcap, width, ntt = unit_src(kind, meta)
                dqd = dqb if u % 2 == 0 else dqbB
                trd = trb if u % 2 == 0 else trbB
                subs = subsplit(width, 2048)
                for si, (soff, sw) in enumerate(subs):
                    subq.append(dict(u=u, kind=kind, meta=meta, ntt=ntt,
                                     srcap=srcap, soff=soff, sw=sw,
                                     dqd=dqd, trd=trd, np=(u in pool_free),
                                     first=(si == 0),
                                     last=(si == len(subs) - 1)))
            NSQ = len(subq)
            marks = {"norm": {}, "fin": {}, "sign": {},
                     "memset": {}, "stt14": {}}

            def b_dma(i):
                e = subq[i]
                if i >= 2:
                    ms, mv = marks["norm"][i - 2]
                    w.wait("sync", ms, mv)
                w.dma("sync", "sBF%d" % (i % 2), wfb[i % 2][:, 0:e["sw"]],
                      e["srcap"][:, e["soff"]:e["soff"] + e["sw"]])

            def b_prep(i):
                e = subq[i]
                par = i % 2
                sw = e["sw"]
                nb = sw // 64
                sv = wfb[par][:, 0:sw]
                w.wait("vector", "sBF%d" % par)
                if i >= 2:
                    fs, fv = marks["fin"].get(i - 2, ("sP", 0))
                    w.wait("vector", fs, fv)
                w.v("tensor_reduce", ambP[par][:, 0:nb],
                    sv.rearrange("p (b k) -> p b k", k=64),
                    axis=AX.X, op=OP.max, apply_absolute_value=True)
                w.v("tensor_scalar", ambP[par][:, 0:nb], ambP[par][:, 0:nb],
                    AM_FLOOR, None, op0=OP.max)
                w.v("reciprocal", recbP[par][:, 0:nb], ambP[par][:, 0:nb])
                role = "vector" if e["np"] else "gpsimd"
                emit = w.v if e["np"] else w.p
                if not e["np"]:
                    w.wait("gpsimd", "sV")
                if e["first"]:
                    w.wait(role, "sT", ch_state["tr"])
                if i >= 2:
                    w.wait(role, "sA", marks["sign"].get(i - 2, 0))
                emit("tensor_tensor", wnP[par][:, 0:sw], sv,
                     recbP[par][:, 0:nb].unsqueeze(2).broadcast_to(
                         [128, nb, 64]),
                     op=OP.mult)
                marks["norm"][i] = ("sV", w.c["sV"]) if e["np"] else (
                    "sP", w.c["sP"])
                emit("memset", accP[par][:, 0:sw], 0.0)
                marks["memset"][i] = ("sV", w.c["sV"]) if e["np"] else (
                    "sP", w.c["sP"])

            def b_steps(i):
                e = subq[i]
                par = i % 2
                sw = e["sw"]
                v0 = w.c["sV"]
                stt_cnt = {-4: v0, -3: v0, -2: v0, -1: v0}
                masks = [tmpA, tmpB, tmpC, tmpD]
                for j in range(15):
                    tb_ = masks[j % 4]
                    w.wait("scalar", "sV", stt_cnt[j - 4])
                    nmsem, nmval = marks["norm"][i]
                    w.wait("scalar", nmsem, nmval)
                    w.s("activation", tb_[:, 0:sw], wnP[par][:, 0:sw],
                        AF.Sign, bias=cst[:, 8 + j:9 + j])
                    marks["sign"][i] = w.c["sA"]
                    if j == 0:
                        mssem, msval = marks["memset"][i]
                        w.wait("vector", mssem, msval)
                    w.wait("vector", "sA")
                    w.v("scalar_tensor_tensor", accP[par][:, 0:sw],
                        tb_[:, 0:sw], NF4_HDELTA[j], accP[par][:, 0:sw],
                        op0=OP.mult, op1=OP.add)
                    stt_cnt[j] = w.c["sV"]
                marks["stt14"][i] = w.c["sV"]

            def b_fin(i):
                e = subq[i]
                par = i % 2
                sw = e["sw"]
                nb = sw // 64
                emit = w.v if e["np"] else w.p
                if not e["np"]:
                    w.wait("gpsimd", "sV", marks["stt14"][i])
                emit("tensor_tensor",
                     e["dqd"][:, e["soff"]:e["soff"] + e["sw"]],
                     accP[par][:, 0:sw],
                     ambP[par][:, 0:nb].unsqueeze(2).broadcast_to(
                         [128, nb, 64]),
                     op=OP.mult)
                marks["fin"][i] = ("sV", w.c["sV"]) if e["np"] else (
                    "sP", w.c["sP"])
                if e["last"]:
                    flush_pending()
                    pending[0] = (e["kind"], e["meta"], e["ntt"],
                                  marks["fin"][i], e["dqd"], e["trd"], e["u"])

            w.wait("sync", "sV", qred_done)
            b_dma(0)
            if NSQ > 1:
                b_dma(1)
            b_prep(0)
            for i in range(NSQ):
                if i + 2 < NSQ:
                    b_dma(i + 2)
                if i + 1 < NSQ:
                    b_prep(i + 1)
                b_steps(i)
                b_fin(i)
            flush_pending()
            stageB_v = w.c["sV"]
            stageB_t = w.c["sT"]
            stageB_a = w.c["sA"]

            w.wait("sync", "sCCQ", 1)
            w.wait("sync", "sV", stageB_v)
            w.wait("sync", "sT", stageB_t)
            w.wait("sync", "sA", stageB_a)
            w.wait("sync", "sG")
            w.wait("sync", "sCT")
            w.dma("sync", "sWALL", W_all[:],
                  gath_q.rearrange("(k d) q -> d k q", d=128))

            def layernorm(src, dstb):
                w.v("tensor_reduce", sm1[:, 0:1], src, axis=AX.X, op=OP.add)
                w.v("scalar_tensor_tensor", attnA[:, 0:D], src, 1.0, src,
                    op0=OP.mult, op1=OP.mult, accum_out=sm2[:, 0:1])
                w.v("tensor_scalar", mmean[:, 0:1], sm1[:, 0:1],
                    1.0 / D, None, op0=OP.mult)
                w.v("tensor_scalar", ex2[:, 0:1], sm2[:, 0:1],
                    1.0 / D, None, op0=OP.mult)
                w.v("tensor_scalar", msq[:, 0:1], mmean[:, 0:1],
                    mmean[:, 0:1], None, op0=OP.mult)
                w.v("tensor_tensor", varb[:, 0:1], ex2[:, 0:1], msq[:, 0:1],
                    op=OP.subtract)
                w.wait("scalar", "sV")
                w.s("activation", sdv[:, 0:1], varb[:, 0:1], AF.Ln,
                    bias=cst[:, 0:1])
                w.s("activation", rstd[:, 0:1], sdv[:, 0:1], AF.Exp,
                    scale=-0.5)
                w.wait("vector", "sA")
                w.v("tensor_scalar", dstb, src, mmean[:, 0:1], rstd[:, 0:1],
                    op0=OP.subtract, op1=OP.mult)

            def tpose_128(srcap, ntiles, ident, pbanks, dst_write, fp32=False):
                per = 4 if fp32 else 8
                ngr = (ntiles + per - 1) // per
                for g in range(ngr):
                    t0 = g * per
                    tn = min(per, ntiles - t0)
                    bk = pbanks[g % 2]
                    w.wait("tensor", "sA", bank_ev[g % 2])
                    for tt in range(tn):
                        dst = (pf32[bk][:, tt * 128:(tt + 1) * 128] if fp32
                               else pb16[bk][:, tt * 128:(tt + 1) * 128])
                        w.t("transpose", dst,
                            srcap[:, (t0 + tt) * 128:(t0 + tt + 1) * 128],
                            ident, inc=True)
                    w.wait("scalar", "sT")
                    dst_write(g, t0, tn, bk)
                    bank_ev[g % 2] = w.c["sA"]

            xt_free = [stageB_v, stageB_v]
            ln1_tpose_prev = stageB_t
            ln2_tpose = {-1: stageB_t}
            x2_tpose = {-1: stageB_t}
            qkv_evac = {-2: stageB_v, -1: stageB_v}
            qkv_mm = {}

            def ev_ln1(g, t0, tn, bk):
                w.s("copy", ln1Tb[:, t0 * 128:(t0 + tn) * 128],
                    pb16[bk][:, 0:tn * 128])

            def c_front(tau):
                nonlocal ln1_tpose_prev
                buf = xt[tau % 2]
                w.wait("sync", "sV", xt_free[tau % 2])
                w.dma("sync", "sX%d" % (tau % 2), buf[:, 0:D],
                      x_in.rearrange("(i p) d -> i p d", p=128)[tau])
                w.wait("vector", "sX%d" % (tau % 2))
                if tau == 0:
                    w.wait("vector", "sIN", setup_in)
                w.wait("vector", "sT", ln1_tpose_prev)
                layernorm(buf[:, 0:D], ln1b[:, 0:D])
                ln1_done = w.c["sV"]
                w.wait("tensor", "sV", ln1_done)
                tpose_128(ln1b, ND, idnb[:], [0, 1], ev_ln1)
                ln1_tpose_prev = w.c["sT"]
                ln1T_evac = w.c["sA"]
                bank = 2 + (tau % 2)
                w.wait("tensor", "sA", ln1T_evac)
                if tau == 0:
                    w.wait("tensor", "sWALL", 16)
                    w.wait("tensor", "sV")
                else:
                    w.wait("tensor", "sV", qkv_evac[tau - 2])
                for dt in range(ND):
                    w.t("matmul", pf32[bank][:, 0:384],
                        ln1Tb[:, dt * 128:(dt + 1) * 128],
                        W_all[:, dt * 384:(dt + 1) * 384],
                        start=(dt == 0), stop=False)
                w.t("matmul", pf32[bank][:, 0:384], ones1[:], bias_row[:],
                    start=False, stop=True, inc=True)
                qkv_mm[tau] = w.c["sT"]

            def c_back(tau):
                buf = xt[tau % 2]
                bank = 2 + (tau % 2)
                w.wait("vector", "sT", qkv_mm[tau])
                w.v("tensor_copy", qkv[:, 0:384], pf32[bank][:, 0:384])
                qkv_evac[tau] = w.c["sV"]
                w.v("tensor_reduce", kmx[:, 0:1], qkv[:, 128:256],
                    axis=AX.X, op=OP.max)
                w.v("tensor_reduce", kmn[:, 0:1], qkv[:, 128:256],
                    axis=AX.X, op=OP.min)
                w.v("tensor_scalar", m1s[:, 0:128], qkv[:, 0:128],
                    kmx[:, 0:1], None, op0=OP.mult)
                w.v("tensor_scalar", m2s[:, 0:128], qkv[:, 0:128],
                    kmn[:, 0:1], None, op0=OP.mult)
                w.v("tensor_tensor", mx[:, 0:128], m1s[:, 0:128],
                    m2s[:, 0:128], op=OP.max)
                exp_prev = None
                for q in range(4):
                    for dd in range(32):
                        d = 32 * q + dd
                        if dd == 0 and exp_prev is not None:
                            w.wait("vector", "sA", exp_prev)
                        w.v("tensor_scalar", attnA[:, dd * 128:(dd + 1) * 128],
                            qkv[:, 128:256], qkv[:, d:d + 1], mx[:, d:d + 1],
                            op0=OP.mult, op1=OP.subtract)
                    w.wait("scalar", "sV")
                    if q == 0:
                        w.wait("scalar", "sCT")
                    w.s("activation", attnB[:, 0:4096], attnA[:, 0:4096],
                        AF.Exp, scale=SM_SCALE)
                    exp_prev = w.c["sA"]
                    w.wait("vector", "sA", exp_prev)
                    w.wait("gpsimd", "sA", exp_prev)
                    w.wait("gpsimd", "sV")
                    w.p("tensor_tensor", attnA[:, 0:4096],
                        attnB[:, 0:4096],
                        qkv[:, 256:384].unsqueeze(1).broadcast_to(
                            [128, 32, 128]),
                        op=OP.mult)
                    w.v("tensor_reduce", s_sb[:, 32 * q:32 * q + 32],
                        attnB[:, 0:4096].rearrange("p (d e) -> p d e", e=128),
                        axis=AX.X, op=OP.add)
                    w.wait("vector", "sP")
                    w.v("tensor_reduce", o_sb[:, 32 * q:32 * q + 32],
                        attnA[:, 0:4096].rearrange("p (d e) -> p d e", e=128),
                        axis=AX.X, op=OP.add)
                w.v("reciprocal", rs_sb[:, 0:128], s_sb[:, 0:128])
                w.v("tensor_tensor", o2_sb[:, 0:128], o_sb[:, 0:128],
                    rs_sb[:, 0:128], op=OP.mult)
                w.wait("vector", "sT", x2_tpose[tau - 1])
                w.v("tensor_tensor", x2t[:, 0:D], buf[:, 0:D],
                    o2_sb[:, 0:128].unsqueeze(2).broadcast_to(
                        [128, 128, NHEADS]),
                    op=OP.add)
                xt_free[tau % 2] = w.c["sV"]
                w.wait("vector", "sT", ln2_tpose[tau - 1])
                layernorm(x2t[:, 0:D], ln2b[:, 0:D])
                ln2_done = w.c["sV"]
                w.wait("tensor", "sV", ln2_done)

                def ev_ln2(g, t0, tn, bk):
                    for tt in range(t0, t0 + tn):
                        w.s("copy", ln2T[:, tt * T + tau * 128:
                                         tt * T + (tau + 1) * 128],
                            pb16[bk][:, (tt - t0) * 128:(tt - t0 + 1) * 128])
                tpose_128(ln2b, ND, idnb[:], [0, 1], ev_ln2)
                ln2_tpose[tau] = w.c["sT"]
                x2sp_prev = w.c["sCT"]
                w.wait("tensor", "sCT", x2sp_prev)

                def ev_x2(g, t0, tn, bk):
                    w.s("copy", x2Tb[:, t0 * 128:(t0 + tn) * 128],
                        pf32[bk][:, 0:tn * 128])
                tpose_128(x2t, ND, idnf[:], [0, 1], ev_x2, fp32=True)
                x2_tpose[tau] = w.c["sT"]
                w.wait("scalar", "sA", w.c["sA"])
                w.dma("scalar", "sCT",
                      x2sp[:, tau * 128:(tau + 1) * 128].rearrange(
                          "(j c) t -> c j t", c=128),
                      x2Tb[:, 0:D])

            c_front(0)
            if NT > 1:
                c_front(1)
            for tau in range(NT):
                c_back(tau)
                if tau + 2 < NT:
                    c_front(tau + 2)
            stageC_v = w.c["sV"]
            stageC_t = w.c["sT"]
            stageC_a = w.c["sA"]
            stageC_g = w.c["sCT"]
            ln2T_done = stageC_a

            hmap = [(t // HT_SH, t % HT_SH) for t in range(NH)]
            for i in range(2 * NG12):
                w.wait("sync", "sU%d" % i, 1)
            w.wait("sync", "sG")
            w.wait("sync", "sCT")
            w.wait("sync", "sV", stageC_v)
            w.wait("sync", "sA", stageC_a)
            w.wait("sync", "sT", stageC_t)
            st_used = [0, 0]
            sg_used = stageC_v
            pend_out = []
            out_tp = {0: 0, 1: 0}

            def flush_out(ci, cb, sum_done):
                ob = oev[ci % 2]
                w.wait("tensor", "sV", sum_done)
                w.wait("tensor", "sOUT%d" % (ci % 2))

                def ev_out(g, t0, tn, bk):
                    w.s("copy", ob[:, t0 * 128:(t0 + tn) * 128],
                        pf32[bk][:, 0:tn * 128])
                tpose_128(ptl[cb], NT, idnf[:], [4, 5], ev_out, fp32=True)
                out_tp[cb] = w.c["sT"]
                w.wait("sync", "sA")
                w.dma("sync", "sOUT%d" % (ci % 2),
                      out_t[:, ci * 128:(ci + 1) * 128].rearrange(
                          "(i p) c -> p i c", p=128),
                      ob[:, 0:NT * 128])
            gu_mult = [0, 0]
            hT_free = stageC_v
            down_prev_t = stageC_t
            TH = min(512, T)
            NTH = T // TH
            for k, (k0, k1) in enumerate(c["HCHUNKS"]):
                nk = k1 - k0
                for jl, t in enumerate(range(k0, k1)):
                    own, idx = hmap[t]
                    jb = jl % 2
                    w.wait("sync", "sT", st_used[jb])
                    g = idx // 4
                    g0, gs = W12G[g]
                    r0 = (own * gs + (idx - g0)) * D
                    w.dma("sync", "sST%d" % jb, st1[jb][:, 0:ND * 128],
                          g1g[g][r0:r0 + D, :].rearrange(
                              "(j d) h -> d j h", d=128))
                    w.dma("sync", "sST%d" % jb, st2[jb][:, 0:ND * 128],
                          g2g[g][r0:r0 + D, :].rearrange(
                              "(j d) h -> d j h", d=128))
                    w.wait("tensor", "sST%d" % jb)
                    if jl == 0 and k == 0:
                        w.wait("tensor", "sA", ln2T_done)
                    if jl == 0:
                        w.wait("tensor", "sV", hT_free)
                    jp = jl % 2
                    GB = [0, 1][:NTH] if jp == 0 else [4, 5][:NTH]
                    UB = [2, 3][:NTH] if jp == 0 else [6, 7][:NTH]
                    if jl >= 2:
                        w.wait("tensor", "sV", gu_mult[jl % 2])
                    for dt in range(ND):
                        st = (dt == 0)
                        sp = (dt == ND - 1)
                        for hh in range(NTH):
                            rh = ln2T[:, dt * T + hh * TH:
                                      dt * T + (hh + 1) * TH]
                            w.t("matmul", pf32[GB[hh]][:, 0:TH],
                                st1[jb][:, dt * 128:(dt + 1) * 128],
                                rh, start=st, stop=sp, inc=sp)
                            w.t("matmul", pf32[UB[hh]][:, 0:TH],
                                st2[jb][:, dt * 128:(dt + 1) * 128],
                                rh, start=st, stop=sp, inc=sp)
                    st_used[jb] = w.c["sT"]
                    w.wait("scalar", "sT")
                    w.wait("scalar", "sV", sg_used)
                    for hh in range(NTH):
                        w.s("activation", sgb[:, hh * TH:(hh + 1) * TH],
                            pf32[GB[hh]][:, 0:TH], AF.Silu)
                    w.wait("vector", "sA")
                    if jl == 0:
                        w.wait("vector", "sT", down_prev_t)
                    for hh in range(NTH):
                        w.v("tensor_tensor",
                            hT[:, jl * T + hh * TH:jl * T + (hh + 1) * TH],
                            sgb[:, hh * TH:(hh + 1) * TH],
                            pf32[UB[hh]][:, 0:TH], op=OP.mult)
                    sg_used = w.c["sV"]
                    gu_mult[jl % 2] = w.c["sV"]
                hT_ready = w.c["sV"]
                dn_used = [0, 0]
                x2_used = [0, 0]
                dn_evac = [0, 0]
                for ci in range(ND):
                    own3 = ci // CSH
                    cloc = ci % CSH
                    cb = ci % 2
                    if k == 0 and ci == 0:
                        for rt2 in range(DSHT):
                            w.wait("sync", "sU%d" % (2 * NG12 + rt2), 1)
                    w.wait("sync", "sT", dn_used[cb])
                    r0 = own3 * HFF + k0 * 128
                    w.dma("sync", "sS3%d" % cb, st3[cb][:, 0:nk * 128],
                          g3g[cloc][r0:r0 + nk * 128, :].rearrange(
                              "(j h) c -> h j c", h=128))
                    if k > 0:
                        w.wait("sync", "sPW%d" % cb)
                        if k == len(c["HCHUNKS"]) - 1:
                            w.wait("sync", "sT", out_tp[cb])
                        w.dma("sync", "sPT%d" % cb, ptl[cb][:, 0:T],
                              part[ci * 128:(ci + 1) * 128, :])
                    last = (k == len(c["HCHUNKS"]) - 1)
                    if last:
                        w.wait("sync", "sCT", stageC_g)
                        w.wait("sync", "sV", x2_used[cb])
                        w.dma("sync", "sPT%d" % cb, x2tt[cb][:, 0:T],
                              x2sp[ci * 128:(ci + 1) * 128, :])
                    w.wait("tensor", "sS3%d" % cb)
                    w.wait("tensor", "sPT%d" % cb)
                    if ci == 0:
                        w.wait("tensor", "sV", hT_ready)
                    DB = [0, 1][:NTH] if cb == 0 else [2, 3][:NTH]
                    if ci >= 2:
                        w.wait("tensor", "sV", dn_evac[cb])
                    for jl in range(nk):
                        st = (jl == 0)
                        sp = (jl == nk - 1)
                        for hh in range(NTH):
                            w.t("matmul", pf32[DB[hh]][:, 0:TH],
                                st3[cb][:, jl * 128:(jl + 1) * 128],
                                hT[:, jl * T + hh * TH:
                                   jl * T + (hh + 1) * TH],
                                start=st, stop=sp, inc=sp)
                    dn_used[cb] = w.c["sT"]
                    w.wait("vector", "sT")
                    w.wait("vector", "sPW%d" % cb)
                    if not last:
                        for hh in range(NTH):
                            hs = slice(hh * TH, (hh + 1) * TH)
                            if k == 0:
                                w.v("tensor_copy", ptl[cb][:, hs],
                                    pf32[DB[hh]][:, 0:TH])
                            else:
                                w.v("tensor_tensor", ptl[cb][:, hs],
                                    ptl[cb][:, hs], pf32[DB[hh]][:, 0:TH],
                                    op=OP.add)
                        dn_evac[cb] = w.c["sV"]
                        w.wait("gpsimd", "sV")
                        w.dma("gpsimd", "sPW%d" % cb,
                              part[ci * 128:(ci + 1) * 128, :],
                              ptl[cb][:, 0:T])
                    else:
                        for hh in range(NTH):
                            hs = slice(hh * TH, (hh + 1) * TH)
                            w.v("tensor_tensor", ptl[cb][:, hs], ptl[cb][:, hs],
                                pf32[DB[hh]][:, 0:TH], op=OP.add)
                        w.v("tensor_tensor", ptl[cb][:, 0:T], ptl[cb][:, 0:T],
                            x2tt[cb][:, 0:T], op=OP.add)
                        x2_used[cb] = w.c["sV"]
                        dn_evac[cb] = w.c["sV"]
                        pend_out.append((ci, cb, w.c["sV"]))
                        if len(pend_out) > 1:
                            flush_out(*pend_out.pop(0))
                if last:
                    while pend_out:
                        flush_out(*pend_out.pop(0))
                hT_free = w.c["sV"]
                down_prev_t = w.c["sT"]
            w.wait("sync", "sOUT0")
            w.wait("sync", "sOUT1")
            return w

        @block.sync
        def _(eng):
            script("sync", eng)

        @block.vector
        def _(eng):
            script("vector", eng)

        @block.scalar
        def _(eng):
            script("scalar", eng)

        @block.tensor
        def _(eng):
            script("tensor", eng)

        @block.gpsimd
        def _(eng):
            script("gpsimd", eng)

    return nc


def shard_inputs(inputs, cfg):
    c = derived(cfg)
    D, T, HFF = c["D"], c["T"], c["HFF"]
    SH_ROWS = c["HT_SH"] * 128
    bf = ml_dtypes.bfloat16

    x = np.ascontiguousarray(np.asarray(
        inputs["L0_attn_x"], dtype=np.float32)).reshape(-1, D)
    wq = np.asarray(inputs["L0_attn_Wq"], dtype=np.float32).reshape(D, -1)
    bq = np.asarray(inputs["L0_attn_bq"], dtype=np.float32)
    wk = np.asarray(inputs["L0_attn_Wk"], dtype=np.float32).reshape(D, 128)
    bk = np.asarray(inputs["L0_attn_bk"], dtype=np.float32).reshape(128)
    wv = np.asarray(inputs["L0_attn_Wv"], dtype=np.float32).reshape(D, 128)
    bv = np.asarray(inputs["L0_attn_bv"], dtype=np.float32).reshape(128)
    w1 = np.asarray(inputs["L0_ffn_w1"], dtype=np.float32)
    w2 = np.asarray(inputs["L0_ffn_w2"], dtype=np.float32)
    w3 = np.asarray(inputs["L0_ffn_w3"], dtype=np.float32)

    w1p = np.zeros((SH_ROWS * NCORES, D), dtype=np.float32)
    w1p[:HFF] = w1
    w2p = np.zeros((SH_ROWS * NCORES, D), dtype=np.float32)
    w2p[:HFF] = w2

    bqT = np.ascontiguousarray(bq.T)
    idb = np.eye(128, dtype=np.float32).astype(bf)
    idf = np.eye(128, dtype=np.float32)
    cstv = np.zeros((128, 40), dtype=np.float32)
    cstv[:, 0] = np.float32(1e-5)
    for j in range(15):
        cstv[:, 8 + j] = -NF4_BOUNDS[j]
        cstv[:, 24 + j] = NF4_HDELTA[j]
    cstv[:, 39] = AM_FLOOR

    dsh = D // NCORES
    in_maps = []
    for r in range(NCORES):
        in_maps.append(dict(
            x_in=np.ascontiguousarray(x[r * T:(r + 1) * T]),
            wq_in=np.ascontiguousarray(wq[r * dsh:(r + 1) * dsh]),
            wk_in=np.ascontiguousarray(wk[r * dsh:(r + 1) * dsh]),
            wv_in=np.ascontiguousarray(wv[r * dsh:(r + 1) * dsh]),
            bqT_in=bqT,
            bk_in=np.ascontiguousarray(bk.reshape(1, 128)),
            bv_in=np.ascontiguousarray(bv.reshape(1, 128)),
            w1_in=np.ascontiguousarray(w1p[r * SH_ROWS:(r + 1) * SH_ROWS]),
            w2_in=np.ascontiguousarray(w2p[r * SH_ROWS:(r + 1) * SH_ROWS]),
            w3_in=np.ascontiguousarray(w3[r * dsh:(r + 1) * dsh]),
            idb_in=idb, idf_in=idf, cst_in=cstv,
        ))
    return in_maps


def kernel(**inputs):
    cfg = cfg_full()
    in_maps = shard_inputs(inputs, cfg)
    nc = build_program(cfg)
    res = run_bass_kernel_spmd(nc, in_maps, core_ids=list(range(NCORES)))
    out = np.concatenate([np.asarray(r["out"]) for r in res.results], axis=0)
    return out.reshape(4, 2048, 4096)


# version 18
# speedup vs baseline: 1.1026x; 1.0007x over previous
import math
import numpy as np
import ml_dtypes

import concourse.bass as bass
import concourse.mybir as mybir
from concourse.bass_utils import run_bass_kernel_spmd

F32 = mybir.dt.float32
BF16 = mybir.dt.bfloat16
AF = mybir.ActivationFunctionType
OP = mybir.AluOpType
AX = mybir.AxisListType

NCORES = 8

NF4_CODE = np.array([
    -1.0, -0.6961928009986877, -0.5250730514526367, -0.39491748809814453,
    -0.28444138169288635, -0.18477343022823334, -0.09105003625154495, 0.0,
    0.07958029955625534, 0.16093020141124725, 0.24611230194568634,
    0.33791524171829224, 0.44070982933044434, 0.5626170039176941,
    0.7229568362236023, 1.0], dtype=np.float32)
NF4_BOUNDS = ((NF4_CODE[:-1] + NF4_CODE[1:]) * np.float32(0.5)).astype(np.float32)
BIG = float(np.float32(1e30))
NF4_HDELTA = [float(np.float32(np.float64(NF4_CODE[j + 1]) - np.float64(NF4_CODE[j])) / np.float32(2.0))
              for j in range(15)]
AM_FLOOR = float(np.float32(1e-30))


def cfg_full():
    return dict(D=4096, HFF=11008, T=1024)


def derived(c):
    d = dict(c)
    d["ND"] = c["D"] // 128
    d["NT"] = c["T"] // 128
    d["NH"] = c["HFF"] // 128
    d["NHEADS"] = c["D"] // 128
    d["HT_SH"] = (d["NH"] + NCORES - 1) // NCORES
    d["DSH"] = c["D"] // NCORES
    d["DSHT"] = d["DSH"] // 128
    d["CSH"] = c["D"] // NCORES // 128
    nh = c["HFF"] // 128
    ch3 = []
    t0 = 0
    while t0 < nh:
        nt_ = min(32, nh - t0)
        ch3.append((t0, nt_))
        t0 += nt_
    d["CH3L"] = ch3
    d["MAXC3"] = max(nt_ for (_, nt_) in ch3) * 128
    nh = d["NH"]
    base = [nh // 4] * 4
    for i in range(nh % 4):
        base[i] += 1
    cuts = [0]
    for b in base:
        cuts.append(cuts[-1] + b)
    d["HCHUNKS"] = [(cuts[i], cuts[i + 1]) for i in range(4)]
    d["MAXCH"] = max(b for b in base)
    return d


SM_SCALE = float(np.float32(1.0) / np.sqrt(np.float32(128.0)))


class Walk:

    def __init__(self, role, eng, sems):
        self.role = role
        self.eng = eng
        self.sems = sems
        self.c = {k: 0 for k in sems}

    def wait(self, wrole, sem, val=None):
        if val is None:
            val = self.c[sem]
        if self.role == wrole:
            self.eng.wait_ge(self.sems[sem], val)

    def dma(self, qrole, sem, dst, src):
        self.c[sem] += 16
        if self.role == qrole:
            self.eng.dma_start(out=dst, in_=src).then_inc(self.sems[sem], 16)

    def _ser(self, role_, sem, opname, a, kw):
        tgt = self.c[sem]
        self.c[sem] += 1
        if self.role == role_:
            self.eng.wait_ge(self.sems[sem], tgt)
            getattr(self.eng, opname)(*a, **kw).then_inc(self.sems[sem], 1)

    def v(self, *a, **kw):
        self._ser("vector", "sV", a[0], a[1:], kw)

    def s(self, *a, **kw):
        self._ser("scalar", "sA", a[0], a[1:], kw)

    def p(self, *a, **kw):
        self._ser("gpsimd", "sP", a[0], a[1:], kw)

    def t(self, *a, inc=False, **kw):
        if inc:
            self.c["sT"] += 1
        if self.role == "tensor":
            inst = getattr(self.eng, a[0])(*a[1:], **kw)
            if inc:
                inst.then_inc(self.sems["sT"], 1)

    def cc(self, sem, ins, outs):
        self.cc2("gpsimd", sem, ins, outs)

    def cc2(self, role, sem, ins, outs):
        self.c[sem] += 1
        if self.role == role:
            self.eng.collective_compute(
                "AllGather", OP.bypass,
                replica_groups=[list(range(NCORES))],
                ins=ins, outs=outs,
            ).then_inc(self.sems[sem])


def build_program(cfg):
    c = derived(cfg)
    D, T, HFF = c["D"], c["T"], c["HFF"]
    ND, NT, NH = c["ND"], c["NT"], c["NH"]
    NHEADS, HT_SH, DSH, DSHT = c["NHEADS"], c["HT_SH"], c["DSH"], c["DSHT"]
    CSH = c["CSH"]
    HPAD = HT_SH * NCORES * 128
    SH_ROWS = HT_SH * 128

    nc = bass.Bass("TRN2", target_bir_lowering=False)

    x_in = nc.dram_tensor("x_in", [T, D], F32, kind="ExternalInput").ap()
    wq_in = nc.dram_tensor("wq_in", [DSH, D], F32, kind="ExternalInput").ap()
    wk_in = nc.dram_tensor("wk_in", [DSH, 128], F32, kind="ExternalInput").ap()
    wv_in = nc.dram_tensor("wv_in", [DSH, 128], F32, kind="ExternalInput").ap()
    bqT_in = nc.dram_tensor("bqT_in", [NHEADS, 128], F32, kind="ExternalInput").ap()
    bk_in = nc.dram_tensor("bk_in", [1, 128], F32, kind="ExternalInput").ap()
    bv_in = nc.dram_tensor("bv_in", [1, 128], F32, kind="ExternalInput").ap()
    w1_in = nc.dram_tensor("w1_in", [SH_ROWS, D], F32, kind="ExternalInput").ap()
    w2_in = nc.dram_tensor("w2_in", [SH_ROWS, D], F32, kind="ExternalInput").ap()
    w3_in = nc.dram_tensor("w3_in", [DSH, HFF], F32, kind="ExternalInput").ap()
    idb_in = nc.dram_tensor("idb_in", [128, 128], BF16, kind="ExternalInput").ap()
    idf_in = nc.dram_tensor("idf_in", [128, 128], F32, kind="ExternalInput").ap()
    cst_in = nc.dram_tensor("cst_in", [128, 40], F32, kind="ExternalInput").ap()
    out_t = nc.dram_tensor("out", [T, D], F32, kind="ExternalOutput").ap()

    ctrb_q = nc.dram_tensor("ctrb_q", [DSH, 384], BF16).ap()
    gath_q = nc.dram_tensor("gath_q", [D, 384], BF16).ap()
    W12G = [(g0, min(4, HT_SH - g0)) for g0 in range(0, HT_SH, 4)]
    c1g = [nc.dram_tensor("c1g%d" % i, [gs * D, 128], BF16).ap()
           for i, (g0, gs) in enumerate(W12G)]
    g1g = [nc.dram_tensor("g1g%d" % i, [NCORES * gs * D, 128], BF16).ap()
           for i, (g0, gs) in enumerate(W12G)]
    c2g = [nc.dram_tensor("c2g%d" % i, [gs * D, 128], BF16).ap()
           for i, (g0, gs) in enumerate(W12G)]
    g2g = [nc.dram_tensor("g2g%d" % i, [NCORES * gs * D, 128], BF16).ap()
           for i, (g0, gs) in enumerate(W12G)]
    c3g = [nc.dram_tensor("c3g%d" % rt, [HFF, 128], BF16).ap()
           for rt in range(DSHT)]
    g3g = [nc.dram_tensor("g3g%d" % rt, [NCORES * HFF, 128], BF16).ap()
           for rt in range(DSHT)]
    x2sp = nc.dram_tensor("x2sp", [D, T], F32).ap()
    part = nc.dram_tensor("part", [D, T], F32).ap()

    from contextlib import ExitStack
    with ExitStack() as ctx:
        def sb(name, shape, dt):
            return ctx.enter_context(nc.sbuf_tensor(name, shape, dt))

        W_all = sb("W_all", [128, ND * 384], BF16)
        ln2T = sb("ln2T", [128, ND * T], BF16)
        bias_row = sb("bias_row", [1, 384], F32)
        ones1 = sb("ones1", [1, 128], F32)
        ones32 = sb("ones32", [NHEADS, 1], F32)
        bqTs = sb("bqTs", [NHEADS, 128], F32)
        idnb = sb("idnb", [128, 128], BF16)
        idnf = sb("idnf", [128, 128], F32)
        cst = sb("cst", [128, 40], F32)
        AR_BYTES = (nc.sbuf_bytes_remaining - 64) & ~63
        arena = sb("arena", [128, AR_BYTES], mybir.dt.int8)

        def carve(off, nelem, dt, name=None):
            nb = nelem * (4 if dt == F32 else 2)
            assert off % 4 == 0 and off + nb <= AR_BYTES, (off, nb)
            return arena[:, off:off + nb].bitcast(dt), off + nb

        MAXC3 = c["MAXC3"]
        o = 0
        SCR = min(max(D, MAXC3), 2048)
        wfb = []
        for i in range(2):
            a, o = carve(o, SCR, F32)
            wfb.append(a)
        wn_, o = carve(o, SCR, F32)
        wnB_, o = carve(o, SCR, F32)
        wnP = [wn_, wnB_]
        tmpA, o = carve(o, SCR, F32)
        tmpB, o = carve(o, SCR, F32)
        accB_, o = carve(o, SCR, F32)
        accB2_, o = carve(o, SCR, F32)
        accP = [accB_, accB2_]
        dqb, o = carve(o, max(MAXC3, D), BF16)
        trb, o = carve(o, max(MAXC3, D), BF16)
        amb_, o = carve(o, 64, F32)
        amb2_, o = carve(o, 64, F32)
        ambP = [amb_, amb2_]
        recb_, o = carve(o, 64, F32)
        recb2_, o = carve(o, 64, F32)
        recbP = [recb_, recb2_]
        qcon, o = carve(o, DSHT * 384, BF16)
        tmpC, o = carve(o, SCR, F32)
        tmpD, o = carve(o, SCR, F32)
        dqbB, o = carve(o, max(MAXC3, D), BF16)
        trbB, o = carve(o, max(MAXC3, D), BF16)
        B_END = o
        o = 0
        xt = []
        for i in range(2):
            a, o = carve(o, D, F32)
            xt.append(a)
        attnA, o = carve(o, 32 * 128, F32)
        attnB_off = o
        attnB, o = carve(o, 32 * 128, F32)
        x2Tb, _ = carve(attnB_off, D, F32)
        ln1b, o = carve(o, D, BF16)
        ln2b, o = carve(o, D, BF16)
        ln1Tb, o = carve(o, D, BF16)
        x2t, o = carve(o, D, F32)
        qkv, o = carve(o, 384, F32)
        s_sb, o = carve(o, 128, F32)
        o_sb, o = carve(o, 128, F32)
        o2_sb, o = carve(o, 128, F32)
        rs_sb, o = carve(o, 128, F32)
        mx, o = carve(o, 128, F32)
        m1s, o = carve(o, 128, F32)
        m2s, o = carve(o, 128, F32)
        sm1, o = carve(o, 1, F32)
        sm2, o = carve(o, 1, F32)
        mmean, o = carve(o, 1, F32)
        ex2, o = carve(o, 1, F32)
        msq, o = carve(o, 1, F32)
        varb, o = carve(o, 1, F32)
        sdv, o = carve(o, 1, F32)
        rstd, o = carve(o, 1, F32)
        kmx, o = carve(o, 1, F32)
        kmn, o = carve(o, 1, F32)
        C_END = o
        o = 0
        hT, o = carve(o, c["MAXCH"] * T, BF16)
        st1 = []
        st2 = []
        st3 = []
        for i in range(2):
            a, o = carve(o, ND * 128, BF16)
            st1.append(a)
        for i in range(2):
            a, o = carve(o, ND * 128, BF16)
            st2.append(a)
        for i in range(2):
            a, o = carve(o, c["MAXCH"] * 128, BF16)
            st3.append(a)
        sgb, o = carve(o, T, F32)
        ptl = []
        for i in range(2):
            a, o = carve(o, T, F32)
            ptl.append(a)
        x2tt = []
        for i in range(2):
            a, o = carve(o, T, F32)
            x2tt.append(a)
        oev = []
        for i in range(2):
            a, o = carve(o, T, F32)
            oev.append(a)
        F_END = o
        assert max(B_END, C_END, F_END) <= AR_BYTES

        banks = []
        for i in range(8):
            banks.append(ctx.enter_context(
                nc.psum_tensor("bank%d" % i, [128, 512], F32)))

        NG12 = len([1 for g0 in range(0, HT_SH, 4)])
        NUNITS = 2 * NG12 + DSHT
        sem_names = ["sIN", "sWALL", "sBF0", "sBF1", "sX0", "sX1",
                     "sST0", "sST1", "sS30", "sS31", "sPT0", "sPT1",
                     "sV", "sA", "sT", "sG",
                     "sCCQ", "sOUT0", "sOUT1",
                     "sPW0", "sPW1", "sP", "sBF2_0", "sBF2_1", "sCT"]
        sem_names += ["sU%d" % i for i in range(NUNITS)]
        sems = {n: ctx.enter_context(nc.semaphore(n)) for n in sem_names}
        block = ctx.enter_context(nc.Block())

        def script(role, eng):
            w = Walk(role, eng, sems)
            pb16 = [b.ap().bitcast(BF16) for b in banks]
            pf32 = [b.ap() for b in banks]

            w.dma("sync", "sIN", cst[:], cst_in)
            w.dma("sync", "sIN", idnb[:], idb_in)
            w.dma("sync", "sIN", idnf[:], idf_in)
            w.dma("sync", "sIN", bqTs[:], bqT_in)
            w.dma("sync", "sIN", bias_row[0:1, 128:256], bk_in)
            w.dma("sync", "sIN", bias_row[0:1, 256:384], bv_in)
            setup_in = w.c["sIN"]
            w.wait("vector", "sIN", setup_in)
            w.v("memset", ones1[:], 1.0)
            w.v("memset", ones32[:], 1.0)
            wq_done = [0, 0]
            NQH = max(1, (D * 4) // (2048 * 4))
            QH = D // NQH
            for i in range(DSHT):
                for h2 in range(NQH):
                    sc = i * NQH + h2
                    w.wait("sync", "sV", wq_done[sc % 2])
                    w.dma("sync", "sBF%d" % (sc % 2), wfb[sc % 2][:, 0:QH],
                          wq_in.rearrange("(i p) d -> i p d", p=128)[i][
                              :, h2 * QH:(h2 + 1) * QH])
                    w.wait("vector", "sBF%d" % (sc % 2))
                    w.v("tensor_reduce", wn_[:, 0:QH // NHEADS],
                        wfb[sc % 2][:, 0:QH].rearrange(
                            "p (d h) -> p d h", h=NHEADS),
                        axis=AX.X, op=OP.add)
                    wq_done[sc % 2] = w.c["sV"]
                    w.v("tensor_copy",
                        qcon[:, i * 384 + h2 * (QH // NHEADS):
                             i * 384 + (h2 + 1) * (QH // NHEADS)],
                        wn_[:, 0:QH // NHEADS])
            qred_done = w.c["sV"]
            w.wait("gpsimd", "sIN", setup_in)
            for i in range(DSHT):
                w.dma("gpsimd", "sG", qcon[:, i * 384 + 128:i * 384 + 256],
                      wk_in.rearrange("(i p) e -> i p e", p=128)[i])
                w.dma("gpsimd", "sG", qcon[:, i * 384 + 256:i * 384 + 384],
                      wv_in.rearrange("(i p) e -> i p e", p=128)[i])
            w.wait("gpsimd", "sV", qred_done)
            w.wait("gpsimd", "sG")
            w.dma("gpsimd", "sG", ctrb_q.rearrange("(i p) q -> p i q", p=128),
                  qcon[:])
            w.wait("gpsimd", "sG")
            w.cc("sCCQ", [ctrb_q.opt()], [gath_q.opt()])
            w.wait("tensor", "sIN", setup_in)
            w.wait("tensor", "sV", qred_done)
            w.t("matmul", pf32[2][0:1, 0:128], ones32[:], bqTs[:],
                start=True, stop=True, inc=True)
            tb = w.c["sT"]
            w.wait("vector", "sT", tb)
            w.v("tensor_copy", bias_row[0:1, 0:128], pf32[2][0:1, 0:128])

            units = []
            for wi in range(HT_SH):
                units.append(("w1", wi))
            for wi in range(HT_SH):
                units.append(("w2", wi))
            for rt in range(DSHT):
                for chi in range(len(c["CH3L"])):
                    units.append(("w3", (rt, chi)))

            normdone = {}
            dq_done = {}
            tr_done = {}
            ev_done = {}
            ct_done = {}
            bank_ev = [0, 0]

            def subsplit(width, scr):
                sub = []
                off = 0
                while off < width:
                    rem = width - off
                    if rem <= scr:
                        sw = rem
                    elif rem >= 2 * scr:
                        sw = scr
                    else:
                        sw = 64 * ((rem // 2 + 63) // 64)
                    sub.append((off, sw))
                    off += sw
                return sub

            def transposes(ntiles, unit_key, dqsrc, trdst, psem, pcnt):
                ngr = (ntiles + 7) // 8
                w.wait("tensor", psem, pcnt)
                for g in range(ngr):
                    t0 = g * 8
                    tn = min(8, ntiles - t0)
                    bk = g % 2
                    w.wait("tensor", "sA", bank_ev[bk])
                    for tt in range(tn):
                        w.t("transpose", pb16[bk][:, (tt) * 128:(tt + 1) * 128],
                            dqsrc[:, (t0 + tt) * 128:(t0 + tt + 1) * 128],
                            idnb[:], inc=True)
                    w.wait("scalar", "sT")
                    w.s("copy", trdst[:, t0 * 128:(t0 + tn) * 128],
                        pb16[bk][:, 0:tn * 128])
                    bank_ev[bk] = w.c["sA"]
                tr_done[unit_key] = w.c["sT"]
                ev_done[unit_key] = w.c["sA"]

            def unit_src(kind, meta):
                if kind == "w1":
                    return (w1_in.rearrange("(i p) d -> i p d", p=128)[meta],
                            D, ND)
                if kind == "w2":
                    return (w2_in.rearrange("(i p) d -> i p d", p=128)[meta],
                            D, ND)
                rt, chi = meta
                cs, nt_ = c["CH3L"][chi]
                return (w3_in.rearrange("(i p) h -> i p h", p=128)[rt][
                    :, cs * 128:(cs + nt_) * 128], nt_ * 128, nt_)

            ccq = [0]

            def emit_cc(sem, ins_, outs_):
                ccq[0] += 1
                w.wait("gpsimd", "sCT")
                w.cc2("gpsimd", sem, ins_, outs_)

            def emit_contrib(kind, meta, trsrc, uidx):
                if kind in ("w1", "w2"):
                    ht = meta
                    g = ht // 4
                    g0, gs = W12G[g]
                    ctr = c1g[g] if kind == "w1" else c2g[g]
                    gt = g1g[g] if kind == "w1" else g2g[g]
                    w.dma("scalar", "sCT",
                          ctr[(ht - g0) * D:(ht - g0 + 1) * D, :].rearrange(
                              "(j d) h -> d j h", d=128),
                          trsrc[:, 0:ND * 128])
                    if ht == g0 + gs - 1:
                        sidx = g if kind == "w1" else NG12 + g
                        emit_cc("sU%d" % sidx, [ctr.opt()], [gt.opt()])
                else:
                    rt, chi = meta
                    cs, nt_ = c["CH3L"][chi]
                    ctr = c3g[rt]
                    gt = g3g[rt]
                    w.dma("scalar", "sCT",
                          ctr[cs * 128:(cs + nt_) * 128, :].rearrange(
                              "(j h) c -> h j c", h=128),
                          trsrc[:, 0:nt_ * 128])
                    if chi == len(c["CH3L"]) - 1:
                        emit_cc("sU%d" % (2 * NG12 + rt),
                                [ctr.opt()], [gt.opt()])

            ch_state = {"sign": 0, "tr": 0}
            pending = [None]

            def flush_pending():
                if pending[0] is None:
                    return
                (pk, pm, pntt, pdq, pdqbuf, ptrbuf, puidx) = pending[0]
                pending[0] = None
                w.wait("scalar", "sCT", w.c["sCT"])
                transposes(pntt, (pk, pm), pdqbuf, ptrbuf, pdq[0], pdq[1])
                ch_state["tr"] = tr_done[(pk, pm)]
                w.wait("scalar", "sA", ev_done[(pk, pm)])
                emit_contrib(pk, pm, ptrbuf, puidx)

            gends = set()
            for g0 in range(0, HT_SH, 4):
                ge = min(g0 + 4, HT_SH) - 1
                gends.add(ge)
                gends.add(HT_SH + ge)
            nw3c = len(c["CH3L"])
            for rt in range(DSHT):
                gends.add(2 * HT_SH + rt * nw3c + (nw3c - 1))
            pool_free = set()

            subq = []
            for u, (kind, meta) in enumerate(units):
                srcap, width, ntt = unit_src(kind, meta)
                dqd = dqb if u % 2 == 0 else dqbB
                trd = trb if u % 2 == 0 else trbB
                subs = subsplit(width, 2048)
                for si, (soff, sw) in enumerate(subs):
                    subq.append(dict(u=u, kind=kind, meta=meta, ntt=ntt,
                                     srcap=srcap, soff=soff, sw=sw,
                                     dqd=dqd, trd=trd, np=(u in pool_free),
                                     first=(si == 0),
                                     last=(si == len(subs) - 1)))
            NSQ = len(subq)
            marks = {"norm": {}, "fin": {}, "sign": {},
                     "memset": {}, "stt14": {}}

            def b_dma(i):
                e = subq[i]
                if i >= 2:
                    ms, mv = marks["norm"][i - 2]
                    w.wait("sync", ms, mv)
                w.dma("sync", "sBF%d" % (i % 2), wfb[i % 2][:, 0:e["sw"]],
                      e["srcap"][:, e["soff"]:e["soff"] + e["sw"]])

            def b_prep(i):
                e = subq[i]
                par = i % 2
                sw = e["sw"]
                nb = sw // 64
                sv = wfb[par][:, 0:sw]
                w.wait("vector", "sBF%d" % par)
                if i >= 2:
                    fs, fv = marks["fin"].get(i - 2, ("sP", 0))
                    w.wait("vector", fs, fv)
                w.v("tensor_reduce", ambP[par][:, 0:nb],
                    sv.rearrange("p (b k) -> p b k", k=64),
                    axis=AX.X, op=OP.max, apply_absolute_value=True)
                w.v("tensor_scalar", ambP[par][:, 0:nb], ambP[par][:, 0:nb],
                    AM_FLOOR, None, op0=OP.max)
                w.v("reciprocal", recbP[par][:, 0:nb], ambP[par][:, 0:nb])
                role = "vector" if e["np"] else "gpsimd"
                emit = w.v if e["np"] else w.p
                if not e["np"]:
                    w.wait("gpsimd", "sV")
                if e["first"]:
                    w.wait(role, "sT", ch_state["tr"])
                if i >= 2:
                    w.wait(role, "sA", marks["sign"].get(i - 2, 0))
                emit("tensor_tensor", wnP[par][:, 0:sw], sv,
                     recbP[par][:, 0:nb].unsqueeze(2).broadcast_to(
                         [128, nb, 64]),
                     op=OP.mult)
                marks["norm"][i] = ("sV", w.c["sV"]) if e["np"] else (
                    "sP", w.c["sP"])
                emit("memset", accP[par][:, 0:sw], 0.0)
                marks["memset"][i] = ("sV", w.c["sV"]) if e["np"] else (
                    "sP", w.c["sP"])

            def b_steps(i):
                e = subq[i]
                par = i % 2
                sw = e["sw"]
                v0 = w.c["sV"]
                stt_cnt = {-4: v0, -3: v0, -2: v0, -1: v0}
                masks = [tmpA, tmpB, tmpC, tmpD]
                for j in range(15):
                    tb_ = masks[j % 4]
                    w.wait("scalar", "sV", stt_cnt[j - 4])
                    nmsem, nmval = marks["norm"][i]
                    w.wait("scalar", nmsem, nmval)
                    w.s("activation", tb_[:, 0:sw], wnP[par][:, 0:sw],
                        AF.Sign, bias=cst[:, 8 + j:9 + j])
                    marks["sign"][i] = w.c["sA"]
                    if j == 0:
                        mssem, msval = marks["memset"][i]
                        w.wait("vector", mssem, msval)
                    w.wait("vector", "sA")
                    w.v("scalar_tensor_tensor", accP[par][:, 0:sw],
                        tb_[:, 0:sw], NF4_HDELTA[j], accP[par][:, 0:sw],
                        op0=OP.mult, op1=OP.add)
                    stt_cnt[j] = w.c["sV"]
                marks["stt14"][i] = w.c["sV"]

            def b_fin(i):
                e = subq[i]
                par = i % 2
                sw = e["sw"]
                nb = sw // 64
                emit = w.v if e["np"] else w.p
                if not e["np"]:
                    w.wait("gpsimd", "sV", marks["stt14"][i])
                emit("tensor_tensor",
                     e["dqd"][:, e["soff"]:e["soff"] + e["sw"]],
                     accP[par][:, 0:sw],
                     ambP[par][:, 0:nb].unsqueeze(2).broadcast_to(
                         [128, nb, 64]),
                     op=OP.mult)
                marks["fin"][i] = ("sV", w.c["sV"]) if e["np"] else (
                    "sP", w.c["sP"])
                if e["last"]:
                    flush_pending()
                    pending[0] = (e["kind"], e["meta"], e["ntt"],
                                  marks["fin"][i], e["dqd"], e["trd"], e["u"])

            w.wait("sync", "sV", qred_done)
            b_dma(0)
            if NSQ > 1:
                b_dma(1)
            b_prep(0)
            for i in range(NSQ):
                if i + 2 < NSQ:
                    b_dma(i + 2)
                if i + 1 < NSQ:
                    b_prep(i + 1)
                b_steps(i)
                b_fin(i)
            flush_pending()
            stageB_v = w.c["sV"]
            stageB_t = w.c["sT"]
            stageB_a = w.c["sA"]

            w.wait("sync", "sCCQ", 1)
            w.wait("sync", "sV", stageB_v)
            w.wait("sync", "sT", stageB_t)
            w.wait("sync", "sA", stageB_a)
            w.wait("sync", "sG")
            w.wait("sync", "sCT")
            w.dma("sync", "sWALL", W_all[:],
                  gath_q.rearrange("(k d) q -> d k q", d=128))

            def layernorm(src, dstb):
                w.v("tensor_reduce", sm1[:, 0:1], src, axis=AX.X, op=OP.add)
                w.v("scalar_tensor_tensor", attnA[:, 0:D], src, 1.0, src,
                    op0=OP.mult, op1=OP.mult, accum_out=sm2[:, 0:1])
                w.v("tensor_scalar", mmean[:, 0:1], sm1[:, 0:1],
                    1.0 / D, None, op0=OP.mult)
                w.v("tensor_scalar", ex2[:, 0:1], sm2[:, 0:1],
                    1.0 / D, None, op0=OP.mult)
                w.v("tensor_scalar", msq[:, 0:1], mmean[:, 0:1],
                    mmean[:, 0:1], None, op0=OP.mult)
                w.v("tensor_tensor", varb[:, 0:1], ex2[:, 0:1], msq[:, 0:1],
                    op=OP.subtract)
                w.wait("scalar", "sV")
                w.s("activation", sdv[:, 0:1], varb[:, 0:1], AF.Ln,
                    bias=cst[:, 0:1])
                w.s("activation", rstd[:, 0:1], sdv[:, 0:1], AF.Exp,
                    scale=-0.5)
                w.wait("vector", "sA")
                w.v("tensor_scalar", dstb, src, mmean[:, 0:1], rstd[:, 0:1],
                    op0=OP.subtract, op1=OP.mult)

            def tpose_128(srcap, ntiles, ident, pbanks, dst_write, fp32=False):
                per = 4 if fp32 else 8
                ngr = (ntiles + per - 1) // per
                for g in range(ngr):
                    t0 = g * per
                    tn = min(per, ntiles - t0)
                    bk = pbanks[g % 2]
                    w.wait("tensor", "sA", bank_ev[g % 2])
                    for tt in range(tn):
                        dst = (pf32[bk][:, tt * 128:(tt + 1) * 128] if fp32
                               else pb16[bk][:, tt * 128:(tt + 1) * 128])
                        w.t("transpose", dst,
                            srcap[:, (t0 + tt) * 128:(t0 + tt + 1) * 128],
                            ident, inc=True)
                    w.wait("scalar", "sT")
                    dst_write(g, t0, tn, bk)
                    bank_ev[g % 2] = w.c["sA"]

            xt_free = [stageB_v, stageB_v]
            ln1_tpose_prev = stageB_t
            ln2_tpose = {-1: stageB_t}
            x2_tpose = {-1: stageB_t}
            qkv_evac = {-2: stageB_v, -1: stageB_v}
            qkv_mm = {}

            def ev_ln1(g, t0, tn, bk):
                w.s("copy", ln1Tb[:, t0 * 128:(t0 + tn) * 128],
                    pb16[bk][:, 0:tn * 128])

            def c_front(tau):
                nonlocal ln1_tpose_prev
                buf = xt[tau % 2]
                w.wait("sync", "sV", xt_free[tau % 2])
                w.dma("sync", "sX%d" % (tau % 2), buf[:, 0:D],
                      x_in.rearrange("(i p) d -> i p d", p=128)[tau])
                w.wait("vector", "sX%d" % (tau % 2))
                if tau == 0:
                    w.wait("vector", "sIN", setup_in)
                w.wait("vector", "sT", ln1_tpose_prev)
                layernorm(buf[:, 0:D], ln1b[:, 0:D])
                ln1_done = w.c["sV"]
                w.wait("tensor", "sV", ln1_done)
                tpose_128(ln1b, ND, idnb[:], [0, 1], ev_ln1)
                ln1_tpose_prev = w.c["sT"]
                ln1T_evac = w.c["sA"]
                bank = 2 + (tau % 2)
                w.wait("tensor", "sA", ln1T_evac)
                if tau == 0:
                    w.wait("tensor", "sWALL", 16)
                    w.wait("tensor", "sV")
                else:
                    w.wait("tensor", "sV", qkv_evac[tau - 2])
                for dt in range(ND):
                    w.t("matmul", pf32[bank][:, 0:384],
                        ln1Tb[:, dt * 128:(dt + 1) * 128],
                        W_all[:, dt * 384:(dt + 1) * 384],
                        start=(dt == 0), stop=False)
                w.t("matmul", pf32[bank][:, 0:384], ones1[:], bias_row[:],
                    start=False, stop=True, inc=True)
                qkv_mm[tau] = w.c["sT"]

            def c_back(tau):
                buf = xt[tau % 2]
                bank = 2 + (tau % 2)
                w.wait("vector", "sT", qkv_mm[tau])
                w.v("tensor_copy", qkv[:, 0:384], pf32[bank][:, 0:384])
                qkv_evac[tau] = w.c["sV"]
                w.v("tensor_reduce", kmx[:, 0:1], qkv[:, 128:256],
                    axis=AX.X, op=OP.max)
                w.v("tensor_reduce", kmn[:, 0:1], qkv[:, 128:256],
                    axis=AX.X, op=OP.min)
                w.v("tensor_scalar", m1s[:, 0:128], qkv[:, 0:128],
                    kmx[:, 0:1], None, op0=OP.mult)
                w.v("tensor_scalar", m2s[:, 0:128], qkv[:, 0:128],
                    kmn[:, 0:1], None, op0=OP.mult)
                w.v("tensor_tensor", mx[:, 0:128], m1s[:, 0:128],
                    m2s[:, 0:128], op=OP.max)
                exp_prev = None
                for q in range(4):
                    for dd in range(32):
                        d = 32 * q + dd
                        if dd == 0 and exp_prev is not None:
                            w.wait("vector", "sA", exp_prev)
                        w.v("tensor_scalar", attnA[:, dd * 128:(dd + 1) * 128],
                            qkv[:, 128:256], qkv[:, d:d + 1], mx[:, d:d + 1],
                            op0=OP.mult, op1=OP.subtract)
                    w.wait("scalar", "sV")
                    if q == 0:
                        w.wait("scalar", "sCT")
                    w.s("activation", attnB[:, 0:4096], attnA[:, 0:4096],
                        AF.Exp, scale=SM_SCALE)
                    exp_prev = w.c["sA"]
                    w.wait("vector", "sA", exp_prev)
                    w.v("tensor_reduce", s_sb[:, 32 * q:32 * q + 32],
                        attnB[:, 0:4096].rearrange("p (d e) -> p d e", e=128),
                        axis=AX.X, op=OP.add)
                    w.v("scalar_tensor_tensor", attnA[:, 0:4096],
                        attnB[:, 0:4096], 1.0,
                        qkv[:, 256:384].unsqueeze(1).broadcast_to(
                            [128, 32, 128]),
                        op0=OP.mult, op1=OP.mult)
                    w.v("tensor_reduce", o_sb[:, 32 * q:32 * q + 32],
                        attnA[:, 0:4096].rearrange("p (d e) -> p d e", e=128),
                        axis=AX.X, op=OP.add)
                w.v("reciprocal", rs_sb[:, 0:128], s_sb[:, 0:128])
                w.v("tensor_tensor", o2_sb[:, 0:128], o_sb[:, 0:128],
                    rs_sb[:, 0:128], op=OP.mult)
                w.wait("vector", "sT", x2_tpose[tau - 1])
                w.v("tensor_tensor", x2t[:, 0:D], buf[:, 0:D],
                    o2_sb[:, 0:128].unsqueeze(2).broadcast_to(
                        [128, 128, NHEADS]),
                    op=OP.add)
                xt_free[tau % 2] = w.c["sV"]
                w.wait("vector", "sT", ln2_tpose[tau - 1])
                layernorm(x2t[:, 0:D], ln2b[:, 0:D])
                ln2_done = w.c["sV"]
                w.wait("tensor", "sV", ln2_done)

                def ev_ln2(g, t0, tn, bk):
                    for tt in range(t0, t0 + tn):
                        w.s("copy", ln2T[:, tt * T + tau * 128:
                                         tt * T + (tau + 1) * 128],
                            pb16[bk][:, (tt - t0) * 128:(tt - t0 + 1) * 128])
                tpose_128(ln2b, ND, idnb[:], [0, 1], ev_ln2)
                ln2_tpose[tau] = w.c["sT"]
                x2sp_prev = w.c["sCT"]
                w.wait("tensor", "sCT", x2sp_prev)

                def ev_x2(g, t0, tn, bk):
                    w.s("copy", x2Tb[:, t0 * 128:(t0 + tn) * 128],
                        pf32[bk][:, 0:tn * 128])
                tpose_128(x2t, ND, idnf[:], [0, 1], ev_x2, fp32=True)
                x2_tpose[tau] = w.c["sT"]
                w.wait("scalar", "sA", w.c["sA"])
                w.dma("scalar", "sCT",
                      x2sp[:, tau * 128:(tau + 1) * 128].rearrange(
                          "(j c) t -> c j t", c=128),
                      x2Tb[:, 0:D])

            c_front(0)
            if NT > 1:
                c_front(1)
            for tau in range(NT):
                c_back(tau)
                if tau + 2 < NT:
                    c_front(tau + 2)
            stageC_v = w.c["sV"]
            stageC_t = w.c["sT"]
            stageC_a = w.c["sA"]
            stageC_g = w.c["sCT"]
            ln2T_done = stageC_a

            hmap = [(t // HT_SH, t % HT_SH) for t in range(NH)]
            for i in range(2 * NG12):
                w.wait("sync", "sU%d" % i, 1)
            w.wait("sync", "sG")
            w.wait("sync", "sCT")
            w.wait("sync", "sV", stageC_v)
            w.wait("sync", "sA", stageC_a)
            w.wait("sync", "sT", stageC_t)
            st_used = [0, 0]
            sg_used = stageC_v
            pend_out = []
            out_tp = {0: 0, 1: 0}

            def flush_out(ci, cb, sum_done):
                ob = oev[ci % 2]
                w.wait("tensor", "sV", sum_done)
                w.wait("tensor", "sOUT%d" % (ci % 2))

                def ev_out(g, t0, tn, bk):
                    w.s("copy", ob[:, t0 * 128:(t0 + tn) * 128],
                        pf32[bk][:, 0:tn * 128])
                tpose_128(ptl[cb], NT, idnf[:], [4, 5], ev_out, fp32=True)
                out_tp[cb] = w.c["sT"]
                w.wait("sync", "sA")
                w.dma("sync", "sOUT%d" % (ci % 2),
                      out_t[:, ci * 128:(ci + 1) * 128].rearrange(
                          "(i p) c -> p i c", p=128),
                      ob[:, 0:NT * 128])
            gu_mult = [0, 0]
            hT_free = stageC_v
            down_prev_t = stageC_t
            TH = min(512, T)
            NTH = T // TH
            for k, (k0, k1) in enumerate(c["HCHUNKS"]):
                nk = k1 - k0
                for jl, t in enumerate(range(k0, k1)):
                    own, idx = hmap[t]
                    jb = jl % 2
                    w.wait("sync", "sT", st_used[jb])
                    g = idx // 4
                    g0, gs = W12G[g]
                    r0 = (own * gs + (idx - g0)) * D
                    w.dma("sync", "sST%d" % jb, st1[jb][:, 0:ND * 128],
                          g1g[g][r0:r0 + D, :].rearrange(
                              "(j d) h -> d j h", d=128))
                    w.dma("sync", "sST%d" % jb, st2[jb][:, 0:ND * 128],
                          g2g[g][r0:r0 + D, :].rearrange(
                              "(j d) h -> d j h", d=128))
                    w.wait("tensor", "sST%d" % jb)
                    if jl == 0 and k == 0:
                        w.wait("tensor", "sA", ln2T_done)
                    if jl == 0:
                        w.wait("tensor", "sV", hT_free)
                    jp = jl % 2
                    GB = [0, 1][:NTH] if jp == 0 else [4, 5][:NTH]
                    UB = [2, 3][:NTH] if jp == 0 else [6, 7][:NTH]
                    if jl >= 2:
                        w.wait("tensor", "sV", gu_mult[jl % 2])
                    for dt in range(ND):
                        st = (dt == 0)
                        sp = (dt == ND - 1)
                        for hh in range(NTH):
                            rh = ln2T[:, dt * T + hh * TH:
                                      dt * T + (hh + 1) * TH]
                            w.t("matmul", pf32[GB[hh]][:, 0:TH],
                                st1[jb][:, dt * 128:(dt + 1) * 128],
                                rh, start=st, stop=sp, inc=sp)
                            w.t("matmul", pf32[UB[hh]][:, 0:TH],
                                st2[jb][:, dt * 128:(dt + 1) * 128],
                                rh, start=st, stop=sp, inc=sp)
                    st_used[jb] = w.c["sT"]
                    w.wait("scalar", "sT")
                    w.wait("scalar", "sV", sg_used)
                    for hh in range(NTH):
                        w.s("activation", sgb[:, hh * TH:(hh + 1) * TH],
                            pf32[GB[hh]][:, 0:TH], AF.Silu)
                    w.wait("vector", "sA")
                    if jl == 0:
                        w.wait("vector", "sT", down_prev_t)
                    for hh in range(NTH):
                        w.v("tensor_tensor",
                            hT[:, jl * T + hh * TH:jl * T + (hh + 1) * TH],
                            sgb[:, hh * TH:(hh + 1) * TH],
                            pf32[UB[hh]][:, 0:TH], op=OP.mult)
                    sg_used = w.c["sV"]
                    gu_mult[jl % 2] = w.c["sV"]
                hT_ready = w.c["sV"]
                dn_used = [0, 0]
                x2_used = [0, 0]
                dn_evac = [0, 0]
                for ci in range(ND):
                    own3 = ci // CSH
                    cloc = ci % CSH
                    cb = ci % 2
                    if k == 0 and ci == 0:
                        for rt2 in range(DSHT):
                            w.wait("sync", "sU%d" % (2 * NG12 + rt2), 1)
                    w.wait("sync", "sT", dn_used[cb])
                    r0 = own3 * HFF + k0 * 128
                    w.dma("sync", "sS3%d" % cb, st3[cb][:, 0:nk * 128],
                          g3g[cloc][r0:r0 + nk * 128, :].rearrange(
                              "(j h) c -> h j c", h=128))
                    if k > 0:
                        w.wait("sync", "sPW%d" % cb)
                        if k == len(c["HCHUNKS"]) - 1:
                            w.wait("sync", "sT", out_tp[cb])
                        w.dma("sync", "sPT%d" % cb, ptl[cb][:, 0:T],
                              part[ci * 128:(ci + 1) * 128, :])
                    last = (k == len(c["HCHUNKS"]) - 1)
                    if last:
                        w.wait("sync", "sCT", stageC_g)
                        w.wait("sync", "sV", x2_used[cb])
                        w.dma("sync", "sPT%d" % cb, x2tt[cb][:, 0:T],
                              x2sp[ci * 128:(ci + 1) * 128, :])
                    w.wait("tensor", "sS3%d" % cb)
                    w.wait("tensor", "sPT%d" % cb)
                    if ci == 0:
                        w.wait("tensor", "sV", hT_ready)
                    DB = [0, 1][:NTH] if cb == 0 else [2, 3][:NTH]
                    if ci >= 2:
                        w.wait("tensor", "sV", dn_evac[cb])
                    for jl in range(nk):
                        st = (jl == 0)
                        sp = (jl == nk - 1)
                        for hh in range(NTH):
                            w.t("matmul", pf32[DB[hh]][:, 0:TH],
                                st3[cb][:, jl * 128:(jl + 1) * 128],
                                hT[:, jl * T + hh * TH:
                                   jl * T + (hh + 1) * TH],
                                start=st, stop=sp, inc=sp)
                    dn_used[cb] = w.c["sT"]
                    w.wait("vector", "sT")
                    w.wait("vector", "sPW%d" % cb)
                    if not last:
                        for hh in range(NTH):
                            hs = slice(hh * TH, (hh + 1) * TH)
                            if k == 0:
                                w.v("tensor_copy", ptl[cb][:, hs],
                                    pf32[DB[hh]][:, 0:TH])
                            else:
                                w.v("tensor_tensor", ptl[cb][:, hs],
                                    ptl[cb][:, hs], pf32[DB[hh]][:, 0:TH],
                                    op=OP.add)
                        dn_evac[cb] = w.c["sV"]
                        w.wait("gpsimd", "sV")
                        w.dma("gpsimd", "sPW%d" % cb,
                              part[ci * 128:(ci + 1) * 128, :],
                              ptl[cb][:, 0:T])
                    else:
                        for hh in range(NTH):
                            hs = slice(hh * TH, (hh + 1) * TH)
                            w.v("tensor_tensor", ptl[cb][:, hs], ptl[cb][:, hs],
                                pf32[DB[hh]][:, 0:TH], op=OP.add)
                        w.v("tensor_tensor", ptl[cb][:, 0:T], ptl[cb][:, 0:T],
                            x2tt[cb][:, 0:T], op=OP.add)
                        x2_used[cb] = w.c["sV"]
                        dn_evac[cb] = w.c["sV"]
                        pend_out.append((ci, cb, w.c["sV"]))
                        if len(pend_out) > 1:
                            flush_out(*pend_out.pop(0))
                if last:
                    while pend_out:
                        flush_out(*pend_out.pop(0))
                hT_free = w.c["sV"]
                down_prev_t = w.c["sT"]
            w.wait("sync", "sOUT0")
            w.wait("sync", "sOUT1")
            return w

        @block.sync
        def _(eng):
            script("sync", eng)

        @block.vector
        def _(eng):
            script("vector", eng)

        @block.scalar
        def _(eng):
            script("scalar", eng)

        @block.tensor
        def _(eng):
            script("tensor", eng)

        @block.gpsimd
        def _(eng):
            script("gpsimd", eng)

    return nc


def shard_inputs(inputs, cfg):
    c = derived(cfg)
    D, T, HFF = c["D"], c["T"], c["HFF"]
    SH_ROWS = c["HT_SH"] * 128
    bf = ml_dtypes.bfloat16

    x = np.ascontiguousarray(np.asarray(
        inputs["L0_attn_x"], dtype=np.float32)).reshape(-1, D)
    wq = np.asarray(inputs["L0_attn_Wq"], dtype=np.float32).reshape(D, -1)
    bq = np.asarray(inputs["L0_attn_bq"], dtype=np.float32)
    wk = np.asarray(inputs["L0_attn_Wk"], dtype=np.float32).reshape(D, 128)
    bk = np.asarray(inputs["L0_attn_bk"], dtype=np.float32).reshape(128)
    wv = np.asarray(inputs["L0_attn_Wv"], dtype=np.float32).reshape(D, 128)
    bv = np.asarray(inputs["L0_attn_bv"], dtype=np.float32).reshape(128)
    w1 = np.asarray(inputs["L0_ffn_w1"], dtype=np.float32)
    w2 = np.asarray(inputs["L0_ffn_w2"], dtype=np.float32)
    w3 = np.asarray(inputs["L0_ffn_w3"], dtype=np.float32)

    w1p = np.zeros((SH_ROWS * NCORES, D), dtype=np.float32)
    w1p[:HFF] = w1
    w2p = np.zeros((SH_ROWS * NCORES, D), dtype=np.float32)
    w2p[:HFF] = w2

    bqT = np.ascontiguousarray(bq.T)
    idb = np.eye(128, dtype=np.float32).astype(bf)
    idf = np.eye(128, dtype=np.float32)
    cstv = np.zeros((128, 40), dtype=np.float32)
    cstv[:, 0] = np.float32(1e-5)
    for j in range(15):
        cstv[:, 8 + j] = -NF4_BOUNDS[j]
        cstv[:, 24 + j] = NF4_HDELTA[j]
    cstv[:, 39] = AM_FLOOR

    dsh = D // NCORES
    in_maps = []
    for r in range(NCORES):
        in_maps.append(dict(
            x_in=np.ascontiguousarray(x[r * T:(r + 1) * T]),
            wq_in=np.ascontiguousarray(wq[r * dsh:(r + 1) * dsh]),
            wk_in=np.ascontiguousarray(wk[r * dsh:(r + 1) * dsh]),
            wv_in=np.ascontiguousarray(wv[r * dsh:(r + 1) * dsh]),
            bqT_in=bqT,
            bk_in=np.ascontiguousarray(bk.reshape(1, 128)),
            bv_in=np.ascontiguousarray(bv.reshape(1, 128)),
            w1_in=np.ascontiguousarray(w1p[r * SH_ROWS:(r + 1) * SH_ROWS]),
            w2_in=np.ascontiguousarray(w2p[r * SH_ROWS:(r + 1) * SH_ROWS]),
            w3_in=np.ascontiguousarray(w3[r * dsh:(r + 1) * dsh]),
            idb_in=idb, idf_in=idf, cst_in=cstv,
        ))
    return in_maps


def kernel(**inputs):
    cfg = cfg_full()
    in_maps = shard_inputs(inputs, cfg)
    nc = build_program(cfg)
    res = run_bass_kernel_spmd(nc, in_maps, core_ids=list(range(NCORES)))
    out = np.concatenate([np.asarray(r["out"]) for r in res.results], axis=0)
    return out.reshape(4, 2048, 4096)


# version 19
# speedup vs baseline: 1.1042x; 1.0015x over previous
import math
import numpy as np
import ml_dtypes

import concourse.bass as bass
import concourse.mybir as mybir
from concourse.bass_utils import run_bass_kernel_spmd

F32 = mybir.dt.float32
BF16 = mybir.dt.bfloat16
FP16 = mybir.dt.float16
AF = mybir.ActivationFunctionType
OP = mybir.AluOpType
AX = mybir.AxisListType

NCORES = 8

NF4_CODE = np.array([
    -1.0, -0.6961928009986877, -0.5250730514526367, -0.39491748809814453,
    -0.28444138169288635, -0.18477343022823334, -0.09105003625154495, 0.0,
    0.07958029955625534, 0.16093020141124725, 0.24611230194568634,
    0.33791524171829224, 0.44070982933044434, 0.5626170039176941,
    0.7229568362236023, 1.0], dtype=np.float32)
NF4_BOUNDS = ((NF4_CODE[:-1] + NF4_CODE[1:]) * np.float32(0.5)).astype(np.float32)
BIG = float(np.float32(1e30))
NF4_HDELTA = [float(np.float32(np.float64(NF4_CODE[j + 1]) - np.float64(NF4_CODE[j])) / np.float32(2.0))
              for j in range(15)]
AM_FLOOR = float(np.float32(1e-30))


def cfg_full():
    return dict(D=4096, HFF=11008, T=1024)


def derived(c):
    d = dict(c)
    d["ND"] = c["D"] // 128
    d["NT"] = c["T"] // 128
    d["NH"] = c["HFF"] // 128
    d["NHEADS"] = c["D"] // 128
    d["HT_SH"] = (d["NH"] + NCORES - 1) // NCORES
    d["DSH"] = c["D"] // NCORES
    d["DSHT"] = d["DSH"] // 128
    d["CSH"] = c["D"] // NCORES // 128
    nh = c["HFF"] // 128
    ch3 = []
    t0 = 0
    while t0 < nh:
        nt_ = min(32, nh - t0)
        ch3.append((t0, nt_))
        t0 += nt_
    d["CH3L"] = ch3
    d["MAXC3"] = max(nt_ for (_, nt_) in ch3) * 128
    nh = d["NH"]
    base = [nh // 4] * 4
    for i in range(nh % 4):
        base[i] += 1
    cuts = [0]
    for b in base:
        cuts.append(cuts[-1] + b)
    d["HCHUNKS"] = [(cuts[i], cuts[i + 1]) for i in range(4)]
    d["MAXCH"] = max(b for b in base)
    return d


SM_SCALE = float(np.float32(1.0) / np.sqrt(np.float32(128.0)))


class Walk:

    def __init__(self, role, eng, sems):
        self.role = role
        self.eng = eng
        self.sems = sems
        self.c = {k: 0 for k in sems}

    def wait(self, wrole, sem, val=None):
        if val is None:
            val = self.c[sem]
        if self.role == wrole:
            self.eng.wait_ge(self.sems[sem], val)

    def dma(self, qrole, sem, dst, src):
        self.c[sem] += 16
        if self.role == qrole:
            self.eng.dma_start(out=dst, in_=src).then_inc(self.sems[sem], 16)

    def _ser(self, role_, sem, opname, a, kw):
        tgt = self.c[sem]
        self.c[sem] += 1
        if self.role == role_:
            self.eng.wait_ge(self.sems[sem], tgt)
            getattr(self.eng, opname)(*a, **kw).then_inc(self.sems[sem], 1)

    def v(self, *a, **kw):
        self._ser("vector", "sV", a[0], a[1:], kw)

    def s(self, *a, **kw):
        self._ser("scalar", "sA", a[0], a[1:], kw)

    def p(self, *a, **kw):
        self._ser("gpsimd", "sP", a[0], a[1:], kw)

    def t(self, *a, inc=False, **kw):
        if inc:
            self.c["sT"] += 1
        if self.role == "tensor":
            inst = getattr(self.eng, a[0])(*a[1:], **kw)
            if inc:
                inst.then_inc(self.sems["sT"], 1)

    def cc(self, sem, ins, outs):
        self.cc2("gpsimd", sem, ins, outs)

    def cc2(self, role, sem, ins, outs):
        self.c[sem] += 1
        if self.role == role:
            self.eng.collective_compute(
                "AllGather", OP.bypass,
                replica_groups=[list(range(NCORES))],
                ins=ins, outs=outs,
            ).then_inc(self.sems[sem])


def build_program(cfg):
    c = derived(cfg)
    D, T, HFF = c["D"], c["T"], c["HFF"]
    ND, NT, NH = c["ND"], c["NT"], c["NH"]
    NHEADS, HT_SH, DSH, DSHT = c["NHEADS"], c["HT_SH"], c["DSH"], c["DSHT"]
    CSH = c["CSH"]
    HPAD = HT_SH * NCORES * 128
    SH_ROWS = HT_SH * 128

    nc = bass.Bass("TRN2", target_bir_lowering=False)

    x_in = nc.dram_tensor("x_in", [T, D], F32, kind="ExternalInput").ap()
    wq_in = nc.dram_tensor("wq_in", [DSH, D], F32, kind="ExternalInput").ap()
    wk_in = nc.dram_tensor("wk_in", [DSH, 128], F32, kind="ExternalInput").ap()
    wv_in = nc.dram_tensor("wv_in", [DSH, 128], F32, kind="ExternalInput").ap()
    bqT_in = nc.dram_tensor("bqT_in", [NHEADS, 128], F32, kind="ExternalInput").ap()
    bk_in = nc.dram_tensor("bk_in", [1, 128], F32, kind="ExternalInput").ap()
    bv_in = nc.dram_tensor("bv_in", [1, 128], F32, kind="ExternalInput").ap()
    w1_in = nc.dram_tensor("w1_in", [SH_ROWS, D], F32, kind="ExternalInput").ap()
    w2_in = nc.dram_tensor("w2_in", [SH_ROWS, D], F32, kind="ExternalInput").ap()
    w3_in = nc.dram_tensor("w3_in", [DSH, HFF], F32, kind="ExternalInput").ap()
    idb_in = nc.dram_tensor("idb_in", [128, 128], BF16, kind="ExternalInput").ap()
    idf_in = nc.dram_tensor("idf_in", [128, 128], F32, kind="ExternalInput").ap()
    cst_in = nc.dram_tensor("cst_in", [128, 40], F32, kind="ExternalInput").ap()
    out_t = nc.dram_tensor("out", [T, D], F32, kind="ExternalOutput").ap()

    ctrb_q = nc.dram_tensor("ctrb_q", [DSH, 384], BF16).ap()
    gath_q = nc.dram_tensor("gath_q", [D, 384], BF16).ap()
    W12G = [(g0, min(4, HT_SH - g0)) for g0 in range(0, HT_SH, 4)]
    c1g = [nc.dram_tensor("c1g%d" % i, [gs * D, 128], BF16).ap()
           for i, (g0, gs) in enumerate(W12G)]
    g1g = [nc.dram_tensor("g1g%d" % i, [NCORES * gs * D, 128], BF16).ap()
           for i, (g0, gs) in enumerate(W12G)]
    c2g = [nc.dram_tensor("c2g%d" % i, [gs * D, 128], BF16).ap()
           for i, (g0, gs) in enumerate(W12G)]
    g2g = [nc.dram_tensor("g2g%d" % i, [NCORES * gs * D, 128], BF16).ap()
           for i, (g0, gs) in enumerate(W12G)]
    c3g = [nc.dram_tensor("c3g%d" % rt, [HFF, 128], BF16).ap()
           for rt in range(DSHT)]
    g3g = [nc.dram_tensor("g3g%d" % rt, [NCORES * HFF, 128], BF16).ap()
           for rt in range(DSHT)]
    x2sp = nc.dram_tensor("x2sp", [D, T], F32).ap()
    part = nc.dram_tensor("part", [D, T], F32).ap()

    from contextlib import ExitStack
    with ExitStack() as ctx:
        def sb(name, shape, dt):
            return ctx.enter_context(nc.sbuf_tensor(name, shape, dt))

        W_all = sb("W_all", [128, ND * 384], BF16)
        ln2T = sb("ln2T", [128, ND * T], BF16)
        bias_row = sb("bias_row", [1, 384], F32)
        ones1 = sb("ones1", [1, 128], F32)
        ones32 = sb("ones32", [NHEADS, 1], F32)
        bqTs = sb("bqTs", [NHEADS, 128], F32)
        idnb = sb("idnb", [128, 128], BF16)
        idnf = sb("idnf", [128, 128], F32)
        cst = sb("cst", [128, 40], F32)
        AR_BYTES = (nc.sbuf_bytes_remaining - 64) & ~63
        arena = sb("arena", [128, AR_BYTES], mybir.dt.int8)

        def carve(off, nelem, dt, name=None):
            nb = nelem * (4 if dt == F32 else 2)
            assert off % 4 == 0 and off + nb <= AR_BYTES, (off, nb)
            return arena[:, off:off + nb].bitcast(dt), off + nb

        MAXC3 = c["MAXC3"]
        o = 0
        SCR = min(max(D, MAXC3), 2048)
        wfb = []
        for i in range(2):
            a, o = carve(o, SCR, F32)
            wfb.append(a)
        wn_, o = carve(o, SCR, F32)
        wnB_, o = carve(o, SCR, F32)
        wnP = [wn_, wnB_]
        tmpA, o = carve(o, SCR, FP16)
        tmpB, o = carve(o, SCR, FP16)
        accB_, o = carve(o, SCR, FP16)
        accB2_, o = carve(o, SCR, FP16)
        accP = [accB_, accB2_]
        dqb, o = carve(o, max(MAXC3, D), BF16)
        trb, o = carve(o, max(MAXC3, D), BF16)
        amb_, o = carve(o, 64, F32)
        amb2_, o = carve(o, 64, F32)
        ambP = [amb_, amb2_]
        recb_, o = carve(o, 64, F32)
        recb2_, o = carve(o, 64, F32)
        recbP = [recb_, recb2_]
        qcon, o = carve(o, DSHT * 384, BF16)
        tmpC, o = carve(o, SCR, FP16)
        tmpD, o = carve(o, SCR, FP16)
        dqbB, o = carve(o, max(MAXC3, D), BF16)
        trbB, o = carve(o, max(MAXC3, D), BF16)
        B_END = o
        o = 0
        xt = []
        for i in range(2):
            a, o = carve(o, D, F32)
            xt.append(a)
        attnA, o = carve(o, 32 * 128, F32)
        attnB_off = o
        attnB, o = carve(o, 32 * 128, F32)
        x2Tb, _ = carve(attnB_off, D, F32)
        ln1b, o = carve(o, D, BF16)
        ln2b, o = carve(o, D, BF16)
        ln1Tb, o = carve(o, D, BF16)
        x2t, o = carve(o, D, F32)
        qkv, o = carve(o, 384, F32)
        s_sb, o = carve(o, 128, F32)
        o_sb, o = carve(o, 128, F32)
        o2_sb, o = carve(o, 128, F32)
        rs_sb, o = carve(o, 128, F32)
        mx, o = carve(o, 128, F32)
        m1s, o = carve(o, 128, F32)
        m2s, o = carve(o, 128, F32)
        sm1, o = carve(o, 1, F32)
        sm2, o = carve(o, 1, F32)
        mmean, o = carve(o, 1, F32)
        ex2, o = carve(o, 1, F32)
        msq, o = carve(o, 1, F32)
        varb, o = carve(o, 1, F32)
        sdv, o = carve(o, 1, F32)
        rstd, o = carve(o, 1, F32)
        kmx, o = carve(o, 1, F32)
        kmn, o = carve(o, 1, F32)
        C_END = o
        o = 0
        hT, o = carve(o, c["MAXCH"] * T, BF16)
        st1 = []
        st2 = []
        st3 = []
        for i in range(2):
            a, o = carve(o, ND * 128, BF16)
            st1.append(a)
        for i in range(2):
            a, o = carve(o, ND * 128, BF16)
            st2.append(a)
        for i in range(2):
            a, o = carve(o, c["MAXCH"] * 128, BF16)
            st3.append(a)
        sgb, o = carve(o, T, F32)
        ptl = []
        for i in range(2):
            a, o = carve(o, T, F32)
            ptl.append(a)
        x2tt = []
        for i in range(2):
            a, o = carve(o, T, F32)
            x2tt.append(a)
        oev = []
        for i in range(2):
            a, o = carve(o, T, F32)
            oev.append(a)
        F_END = o
        assert max(B_END, C_END, F_END) <= AR_BYTES

        banks = []
        for i in range(8):
            banks.append(ctx.enter_context(
                nc.psum_tensor("bank%d" % i, [128, 512], F32)))

        NG12 = len([1 for g0 in range(0, HT_SH, 4)])
        NUNITS = 2 * NG12 + DSHT
        sem_names = ["sIN", "sWALL", "sBF0", "sBF1", "sX0", "sX1",
                     "sST0", "sST1", "sS30", "sS31", "sPT0", "sPT1",
                     "sV", "sA", "sT", "sG",
                     "sCCQ", "sOUT0", "sOUT1",
                     "sPW0", "sPW1", "sP", "sBF2_0", "sBF2_1", "sCT"]
        sem_names += ["sU%d" % i for i in range(NUNITS)]
        sems = {n: ctx.enter_context(nc.semaphore(n)) for n in sem_names}
        block = ctx.enter_context(nc.Block())

        def script(role, eng):
            w = Walk(role, eng, sems)
            pb16 = [b.ap().bitcast(BF16) for b in banks]
            pf32 = [b.ap() for b in banks]

            w.dma("sync", "sIN", cst[:], cst_in)
            w.dma("sync", "sIN", idnb[:], idb_in)
            w.dma("sync", "sIN", idnf[:], idf_in)
            w.dma("sync", "sIN", bqTs[:], bqT_in)
            w.dma("sync", "sIN", bias_row[0:1, 128:256], bk_in)
            w.dma("sync", "sIN", bias_row[0:1, 256:384], bv_in)
            setup_in = w.c["sIN"]
            w.wait("vector", "sIN", setup_in)
            w.v("memset", ones1[:], 1.0)
            w.v("memset", ones32[:], 1.0)
            wq_done = [0, 0]
            NQH = max(1, (D * 4) // (2048 * 4))
            QH = D // NQH
            for i in range(DSHT):
                for h2 in range(NQH):
                    sc = i * NQH + h2
                    w.wait("sync", "sV", wq_done[sc % 2])
                    w.dma("sync", "sBF%d" % (sc % 2), wfb[sc % 2][:, 0:QH],
                          wq_in.rearrange("(i p) d -> i p d", p=128)[i][
                              :, h2 * QH:(h2 + 1) * QH])
                    w.wait("vector", "sBF%d" % (sc % 2))
                    w.v("tensor_reduce", wn_[:, 0:QH // NHEADS],
                        wfb[sc % 2][:, 0:QH].rearrange(
                            "p (d h) -> p d h", h=NHEADS),
                        axis=AX.X, op=OP.add)
                    wq_done[sc % 2] = w.c["sV"]
                    w.v("tensor_copy",
                        qcon[:, i * 384 + h2 * (QH // NHEADS):
                             i * 384 + (h2 + 1) * (QH // NHEADS)],
                        wn_[:, 0:QH // NHEADS])
            qred_done = w.c["sV"]
            w.wait("gpsimd", "sIN", setup_in)
            for i in range(DSHT):
                w.dma("gpsimd", "sG", qcon[:, i * 384 + 128:i * 384 + 256],
                      wk_in.rearrange("(i p) e -> i p e", p=128)[i])
                w.dma("gpsimd", "sG", qcon[:, i * 384 + 256:i * 384 + 384],
                      wv_in.rearrange("(i p) e -> i p e", p=128)[i])
            w.wait("gpsimd", "sV", qred_done)
            w.wait("gpsimd", "sG")
            w.dma("gpsimd", "sG", ctrb_q.rearrange("(i p) q -> p i q", p=128),
                  qcon[:])
            w.wait("gpsimd", "sG")
            w.cc("sCCQ", [ctrb_q.opt()], [gath_q.opt()])
            w.wait("tensor", "sIN", setup_in)
            w.wait("tensor", "sV", qred_done)
            w.t("matmul", pf32[2][0:1, 0:128], ones32[:], bqTs[:],
                start=True, stop=True, inc=True)
            tb = w.c["sT"]
            w.wait("vector", "sT", tb)
            w.v("tensor_copy", bias_row[0:1, 0:128], pf32[2][0:1, 0:128])

            units = []
            for wi in range(HT_SH):
                units.append(("w1", wi))
            for wi in range(HT_SH):
                units.append(("w2", wi))
            for rt in range(DSHT):
                for chi in range(len(c["CH3L"])):
                    units.append(("w3", (rt, chi)))

            normdone = {}
            dq_done = {}
            tr_done = {}
            ev_done = {}
            ct_done = {}
            bank_ev = [0, 0]

            def subsplit(width, scr):
                sub = []
                off = 0
                while off < width:
                    rem = width - off
                    if rem <= scr:
                        sw = rem
                    elif rem >= 2 * scr:
                        sw = scr
                    else:
                        sw = 64 * ((rem // 2 + 63) // 64)
                    sub.append((off, sw))
                    off += sw
                return sub

            def transposes(ntiles, unit_key, dqsrc, trdst, psem, pcnt):
                ngr = (ntiles + 7) // 8
                w.wait("tensor", psem, pcnt)
                for g in range(ngr):
                    t0 = g * 8
                    tn = min(8, ntiles - t0)
                    bk = g % 2
                    w.wait("tensor", "sA", bank_ev[bk])
                    for tt in range(tn):
                        w.t("transpose", pb16[bk][:, (tt) * 128:(tt + 1) * 128],
                            dqsrc[:, (t0 + tt) * 128:(t0 + tt + 1) * 128],
                            idnb[:], inc=True)
                    w.wait("scalar", "sT")
                    w.s("copy", trdst[:, t0 * 128:(t0 + tn) * 128],
                        pb16[bk][:, 0:tn * 128])
                    bank_ev[bk] = w.c["sA"]
                tr_done[unit_key] = w.c["sT"]
                ev_done[unit_key] = w.c["sA"]

            def unit_src(kind, meta):
                if kind == "w1":
                    return (w1_in.rearrange("(i p) d -> i p d", p=128)[meta],
                            D, ND)
                if kind == "w2":
                    return (w2_in.rearrange("(i p) d -> i p d", p=128)[meta],
                            D, ND)
                rt, chi = meta
                cs, nt_ = c["CH3L"][chi]
                return (w3_in.rearrange("(i p) h -> i p h", p=128)[rt][
                    :, cs * 128:(cs + nt_) * 128], nt_ * 128, nt_)

            ccq = [0]

            def emit_cc(sem, ins_, outs_):
                ccq[0] += 1
                w.wait("gpsimd", "sCT")
                w.cc2("gpsimd", sem, ins_, outs_)

            def emit_contrib(kind, meta, trsrc, uidx):
                if kind in ("w1", "w2"):
                    ht = meta
                    g = ht // 4
                    g0, gs = W12G[g]
                    ctr = c1g[g] if kind == "w1" else c2g[g]
                    gt = g1g[g] if kind == "w1" else g2g[g]
                    w.dma("scalar", "sCT",
                          ctr[(ht - g0) * D:(ht - g0 + 1) * D, :].rearrange(
                              "(j d) h -> d j h", d=128),
                          trsrc[:, 0:ND * 128])
                    if ht == g0 + gs - 1:
                        sidx = g if kind == "w1" else NG12 + g
                        emit_cc("sU%d" % sidx, [ctr.opt()], [gt.opt()])
                else:
                    rt, chi = meta
                    cs, nt_ = c["CH3L"][chi]
                    ctr = c3g[rt]
                    gt = g3g[rt]
                    w.dma("scalar", "sCT",
                          ctr[cs * 128:(cs + nt_) * 128, :].rearrange(
                              "(j h) c -> h j c", h=128),
                          trsrc[:, 0:nt_ * 128])
                    if chi == len(c["CH3L"]) - 1:
                        emit_cc("sU%d" % (2 * NG12 + rt),
                                [ctr.opt()], [gt.opt()])

            ch_state = {"sign": 0, "tr": 0}
            pending = [None]

            def flush_pending():
                if pending[0] is None:
                    return
                (pk, pm, pntt, pdq, pdqbuf, ptrbuf, puidx) = pending[0]
                pending[0] = None
                w.wait("scalar", "sCT", w.c["sCT"])
                transposes(pntt, (pk, pm), pdqbuf, ptrbuf, pdq[0], pdq[1])
                ch_state["tr"] = tr_done[(pk, pm)]
                w.wait("scalar", "sA", ev_done[(pk, pm)])
                emit_contrib(pk, pm, ptrbuf, puidx)

            gends = set()
            for g0 in range(0, HT_SH, 4):
                ge = min(g0 + 4, HT_SH) - 1
                gends.add(ge)
                gends.add(HT_SH + ge)
            nw3c = len(c["CH3L"])
            for rt in range(DSHT):
                gends.add(2 * HT_SH + rt * nw3c + (nw3c - 1))
            pool_free = set()

            subq = []
            for u, (kind, meta) in enumerate(units):
                srcap, width, ntt = unit_src(kind, meta)
                dqd = dqb if u % 2 == 0 else dqbB
                trd = trb if u % 2 == 0 else trbB
                subs = subsplit(width, 2048)
                for si, (soff, sw) in enumerate(subs):
                    subq.append(dict(u=u, kind=kind, meta=meta, ntt=ntt,
                                     srcap=srcap, soff=soff, sw=sw,
                                     dqd=dqd, trd=trd, np=(u in pool_free),
                                     first=(si == 0),
                                     last=(si == len(subs) - 1)))
            NSQ = len(subq)
            marks = {"norm": {}, "fin": {}, "sign": {},
                     "memset": {}, "stt14": {}}

            def b_dma(i):
                e = subq[i]
                if i >= 2:
                    ms, mv = marks["norm"][i - 2]
                    w.wait("sync", ms, mv)
                w.dma("sync", "sBF%d" % (i % 2), wfb[i % 2][:, 0:e["sw"]],
                      e["srcap"][:, e["soff"]:e["soff"] + e["sw"]])

            def b_prep(i):
                e = subq[i]
                par = i % 2
                sw = e["sw"]
                nb = sw // 64
                sv = wfb[par][:, 0:sw]
                w.wait("vector", "sBF%d" % par)
                if i >= 2:
                    fs, fv = marks["fin"].get(i - 2, ("sP", 0))
                    w.wait("vector", fs, fv)
                w.v("tensor_reduce", ambP[par][:, 0:nb],
                    sv.rearrange("p (b k) -> p b k", k=64),
                    axis=AX.X, op=OP.max, apply_absolute_value=True)
                w.v("tensor_scalar", ambP[par][:, 0:nb], ambP[par][:, 0:nb],
                    AM_FLOOR, None, op0=OP.max)
                w.v("reciprocal", recbP[par][:, 0:nb], ambP[par][:, 0:nb])
                role = "vector" if e["np"] else "gpsimd"
                emit = w.v if e["np"] else w.p
                if not e["np"]:
                    w.wait("gpsimd", "sV")
                if e["first"]:
                    w.wait(role, "sT", ch_state["tr"])
                if i >= 2:
                    w.wait(role, "sA", marks["sign"].get(i - 2, 0))
                emit("tensor_tensor", wnP[par][:, 0:sw], sv,
                     recbP[par][:, 0:nb].unsqueeze(2).broadcast_to(
                         [128, nb, 64]),
                     op=OP.mult)
                marks["norm"][i] = ("sV", w.c["sV"]) if e["np"] else (
                    "sP", w.c["sP"])
                emit("memset", accP[par][:, 0:sw], 0.0)
                marks["memset"][i] = ("sV", w.c["sV"]) if e["np"] else (
                    "sP", w.c["sP"])

            def b_steps(i):
                e = subq[i]
                par = i % 2
                sw = e["sw"]
                v0 = w.c["sV"]
                stt_cnt = {-4: v0, -3: v0, -2: v0, -1: v0}
                masks = [tmpA, tmpB, tmpC, tmpD]
                for j in range(15):
                    tb_ = masks[j % 4]
                    w.wait("scalar", "sV", stt_cnt[j - 4])
                    nmsem, nmval = marks["norm"][i]
                    w.wait("scalar", nmsem, nmval)
                    w.s("activation", tb_[:, 0:sw], wnP[par][:, 0:sw],
                        AF.Sign, bias=cst[:, 8 + j:9 + j])
                    marks["sign"][i] = w.c["sA"]
                    if j == 0:
                        mssem, msval = marks["memset"][i]
                        w.wait("vector", mssem, msval)
                    w.wait("vector", "sA")
                    w.v("scalar_tensor_tensor", accP[par][:, 0:sw],
                        tb_[:, 0:sw], NF4_HDELTA[j], accP[par][:, 0:sw],
                        op0=OP.mult, op1=OP.add)
                    stt_cnt[j] = w.c["sV"]
                marks["stt14"][i] = w.c["sV"]

            def b_fin(i):
                e = subq[i]
                par = i % 2
                sw = e["sw"]
                nb = sw // 64
                emit = w.v if e["np"] else w.p
                if not e["np"]:
                    w.wait("gpsimd", "sV", marks["stt14"][i])
                emit("tensor_tensor",
                     e["dqd"][:, e["soff"]:e["soff"] + e["sw"]],
                     accP[par][:, 0:sw],
                     ambP[par][:, 0:nb].unsqueeze(2).broadcast_to(
                         [128, nb, 64]),
                     op=OP.mult)
                marks["fin"][i] = ("sV", w.c["sV"]) if e["np"] else (
                    "sP", w.c["sP"])
                if e["last"]:
                    flush_pending()
                    pending[0] = (e["kind"], e["meta"], e["ntt"],
                                  marks["fin"][i], e["dqd"], e["trd"], e["u"])

            w.wait("sync", "sV", qred_done)
            b_dma(0)
            if NSQ > 1:
                b_dma(1)
            b_prep(0)
            for i in range(NSQ):
                if i + 2 < NSQ:
                    b_dma(i + 2)
                if i + 1 < NSQ:
                    b_prep(i + 1)
                b_steps(i)
                b_fin(i)
            flush_pending()
            stageB_v = w.c["sV"]
            stageB_t = w.c["sT"]
            stageB_a = w.c["sA"]

            w.wait("sync", "sCCQ", 1)
            w.wait("sync", "sV", stageB_v)
            w.wait("sync", "sT", stageB_t)
            w.wait("sync", "sA", stageB_a)
            w.wait("sync", "sG")
            w.wait("sync", "sCT")
            w.dma("sync", "sWALL", W_all[:],
                  gath_q.rearrange("(k d) q -> d k q", d=128))

            def layernorm(src, dstb):
                w.v("tensor_reduce", sm1[:, 0:1], src, axis=AX.X, op=OP.add)
                w.v("scalar_tensor_tensor", attnA[:, 0:D], src, 1.0, src,
                    op0=OP.mult, op1=OP.mult, accum_out=sm2[:, 0:1])
                w.v("tensor_scalar", mmean[:, 0:1], sm1[:, 0:1],
                    1.0 / D, None, op0=OP.mult)
                w.v("tensor_scalar", ex2[:, 0:1], sm2[:, 0:1],
                    1.0 / D, None, op0=OP.mult)
                w.v("tensor_scalar", msq[:, 0:1], mmean[:, 0:1],
                    mmean[:, 0:1], None, op0=OP.mult)
                w.v("tensor_tensor", varb[:, 0:1], ex2[:, 0:1], msq[:, 0:1],
                    op=OP.subtract)
                w.wait("scalar", "sV")
                w.s("activation", sdv[:, 0:1], varb[:, 0:1], AF.Ln,
                    bias=cst[:, 0:1])
                w.s("activation", rstd[:, 0:1], sdv[:, 0:1], AF.Exp,
                    scale=-0.5)
                w.wait("vector", "sA")
                w.v("tensor_scalar", dstb, src, mmean[:, 0:1], rstd[:, 0:1],
                    op0=OP.subtract, op1=OP.mult)

            def tpose_128(srcap, ntiles, ident, pbanks, dst_write, fp32=False):
                per = 4 if fp32 else 8
                ngr = (ntiles + per - 1) // per
                for g in range(ngr):
                    t0 = g * per
                    tn = min(per, ntiles - t0)
                    bk = pbanks[g % 2]
                    w.wait("tensor", "sA", bank_ev[g % 2])
                    for tt in range(tn):
                        dst = (pf32[bk][:, tt * 128:(tt + 1) * 128] if fp32
                               else pb16[bk][:, tt * 128:(tt + 1) * 128])
                        w.t("transpose", dst,
                            srcap[:, (t0 + tt) * 128:(t0 + tt + 1) * 128],
                            ident, inc=True)
                    w.wait("scalar", "sT")
                    dst_write(g, t0, tn, bk)
                    bank_ev[g % 2] = w.c["sA"]

            xt_free = [stageB_v, stageB_v]
            ln1_tpose_prev = stageB_t
            ln2_tpose = {-1: stageB_t}
            x2_tpose = {-1: stageB_t}
            qkv_evac = {-2: stageB_v, -1: stageB_v}
            qkv_mm = {}

            def ev_ln1(g, t0, tn, bk):
                w.s("copy", ln1Tb[:, t0 * 128:(t0 + tn) * 128],
                    pb16[bk][:, 0:tn * 128])

            def c_front(tau):
                nonlocal ln1_tpose_prev
                buf = xt[tau % 2]
                w.wait("sync", "sV", xt_free[tau % 2])
                w.dma("sync", "sX%d" % (tau % 2), buf[:, 0:D],
                      x_in.rearrange("(i p) d -> i p d", p=128)[tau])
                w.wait("vector", "sX%d" % (tau % 2))
                if tau == 0:
                    w.wait("vector", "sIN", setup_in)
                w.wait("vector", "sT", ln1_tpose_prev)
                layernorm(buf[:, 0:D], ln1b[:, 0:D])
                ln1_done = w.c["sV"]
                w.wait("tensor", "sV", ln1_done)
                tpose_128(ln1b, ND, idnb[:], [0, 1], ev_ln1)
                ln1_tpose_prev = w.c["sT"]
                ln1T_evac = w.c["sA"]
                bank = 2 + (tau % 2)
                w.wait("tensor", "sA", ln1T_evac)
                if tau == 0:
                    w.wait("tensor", "sWALL", 16)
                    w.wait("tensor", "sV")
                else:
                    w.wait("tensor", "sV", qkv_evac[tau - 2])
                for dt in range(ND):
                    w.t("matmul", pf32[bank][:, 0:384],
                        ln1Tb[:, dt * 128:(dt + 1) * 128],
                        W_all[:, dt * 384:(dt + 1) * 384],
                        start=(dt == 0), stop=False)
                w.t("matmul", pf32[bank][:, 0:384], ones1[:], bias_row[:],
                    start=False, stop=True, inc=True)
                qkv_mm[tau] = w.c["sT"]

            def c_back(tau):
                buf = xt[tau % 2]
                bank = 2 + (tau % 2)
                w.wait("vector", "sT", qkv_mm[tau])
                w.v("tensor_copy", qkv[:, 0:384], pf32[bank][:, 0:384])
                qkv_evac[tau] = w.c["sV"]
                w.v("tensor_reduce", kmx[:, 0:1], qkv[:, 128:256],
                    axis=AX.X, op=OP.max)
                w.v("tensor_reduce", kmn[:, 0:1], qkv[:, 128:256],
                    axis=AX.X, op=OP.min)
                w.v("tensor_scalar", m1s[:, 0:128], qkv[:, 0:128],
                    kmx[:, 0:1], None, op0=OP.mult)
                w.v("tensor_scalar", m2s[:, 0:128], qkv[:, 0:128],
                    kmn[:, 0:1], None, op0=OP.mult)
                w.v("tensor_tensor", mx[:, 0:128], m1s[:, 0:128],
                    m2s[:, 0:128], op=OP.max)
                exp_prev = None
                for q in range(4):
                    for dd in range(32):
                        d = 32 * q + dd
                        if dd == 0 and exp_prev is not None:
                            w.wait("vector", "sA", exp_prev)
                        w.v("tensor_scalar", attnA[:, dd * 128:(dd + 1) * 128],
                            qkv[:, 128:256], qkv[:, d:d + 1], mx[:, d:d + 1],
                            op0=OP.mult, op1=OP.subtract)
                    w.wait("scalar", "sV")
                    if q == 0:
                        w.wait("scalar", "sCT")
                    w.s("activation", attnB[:, 0:4096], attnA[:, 0:4096],
                        AF.Exp, scale=SM_SCALE)
                    exp_prev = w.c["sA"]
                    w.wait("vector", "sA", exp_prev)
                    w.v("tensor_reduce", s_sb[:, 32 * q:32 * q + 32],
                        attnB[:, 0:4096].rearrange("p (d e) -> p d e", e=128),
                        axis=AX.X, op=OP.add)
                    w.v("scalar_tensor_tensor", attnA[:, 0:4096],
                        attnB[:, 0:4096], 1.0,
                        qkv[:, 256:384].unsqueeze(1).broadcast_to(
                            [128, 32, 128]),
                        op0=OP.mult, op1=OP.mult)
                    w.v("tensor_reduce", o_sb[:, 32 * q:32 * q + 32],
                        attnA[:, 0:4096].rearrange("p (d e) -> p d e", e=128),
                        axis=AX.X, op=OP.add)
                w.v("reciprocal", rs_sb[:, 0:128], s_sb[:, 0:128])
                w.v("tensor_tensor", o2_sb[:, 0:128], o_sb[:, 0:128],
                    rs_sb[:, 0:128], op=OP.mult)
                w.wait("vector", "sT", x2_tpose[tau - 1])
                w.v("tensor_tensor", x2t[:, 0:D], buf[:, 0:D],
                    o2_sb[:, 0:128].unsqueeze(2).broadcast_to(
                        [128, 128, NHEADS]),
                    op=OP.add)
                xt_free[tau % 2] = w.c["sV"]
                w.wait("vector", "sT", ln2_tpose[tau - 1])
                layernorm(x2t[:, 0:D], ln2b[:, 0:D])
                ln2_done = w.c["sV"]
                w.wait("tensor", "sV", ln2_done)

                def ev_ln2(g, t0, tn, bk):
                    for tt in range(t0, t0 + tn):
                        w.s("copy", ln2T[:, tt * T + tau * 128:
                                         tt * T + (tau + 1) * 128],
                            pb16[bk][:, (tt - t0) * 128:(tt - t0 + 1) * 128])
                tpose_128(ln2b, ND, idnb[:], [0, 1], ev_ln2)
                ln2_tpose[tau] = w.c["sT"]
                x2sp_prev = w.c["sCT"]
                w.wait("tensor", "sCT", x2sp_prev)

                def ev_x2(g, t0, tn, bk):
                    w.s("copy", x2Tb[:, t0 * 128:(t0 + tn) * 128],
                        pf32[bk][:, 0:tn * 128])
                tpose_128(x2t, ND, idnf[:], [0, 1], ev_x2, fp32=True)
                x2_tpose[tau] = w.c["sT"]
                w.wait("scalar", "sA", w.c["sA"])
                w.dma("scalar", "sCT",
                      x2sp[:, tau * 128:(tau + 1) * 128].rearrange(
                          "(j c) t -> c j t", c=128),
                      x2Tb[:, 0:D])

            c_front(0)
            if NT > 1:
                c_front(1)
            for tau in range(NT):
                c_back(tau)
                if tau + 2 < NT:
                    c_front(tau + 2)
            stageC_v = w.c["sV"]
            stageC_t = w.c["sT"]
            stageC_a = w.c["sA"]
            stageC_g = w.c["sCT"]
            ln2T_done = stageC_a

            hmap = [(t // HT_SH, t % HT_SH) for t in range(NH)]
            for i in range(2 * NG12):
                w.wait("sync", "sU%d" % i, 1)
            w.wait("sync", "sG")
            w.wait("sync", "sCT")
            w.wait("sync", "sV", stageC_v)
            w.wait("sync", "sA", stageC_a)
            w.wait("sync", "sT", stageC_t)
            st_used = [0, 0]
            sg_used = stageC_v
            pend_out = []
            out_tp = {0: 0, 1: 0}

            def flush_out(ci, cb, sum_done):
                ob = oev[ci % 2]
                w.wait("tensor", "sV", sum_done)
                w.wait("tensor", "sOUT%d" % (ci % 2))

                def ev_out(g, t0, tn, bk):
                    w.s("copy", ob[:, t0 * 128:(t0 + tn) * 128],
                        pf32[bk][:, 0:tn * 128])
                tpose_128(ptl[cb], NT, idnf[:], [4, 5], ev_out, fp32=True)
                out_tp[cb] = w.c["sT"]
                w.wait("sync", "sA")
                w.dma("sync", "sOUT%d" % (ci % 2),
                      out_t[:, ci * 128:(ci + 1) * 128].rearrange(
                          "(i p) c -> p i c", p=128),
                      ob[:, 0:NT * 128])
            gu_mult = [0, 0]
            hT_free = stageC_v
            down_prev_t = stageC_t
            TH = min(512, T)
            NTH = T // TH
            for k, (k0, k1) in enumerate(c["HCHUNKS"]):
                nk = k1 - k0
                for jl, t in enumerate(range(k0, k1)):
                    own, idx = hmap[t]
                    jb = jl % 2
                    w.wait("sync", "sT", st_used[jb])
                    g = idx // 4
                    g0, gs = W12G[g]
                    r0 = (own * gs + (idx - g0)) * D
                    w.dma("sync", "sST%d" % jb, st1[jb][:, 0:ND * 128],
                          g1g[g][r0:r0 + D, :].rearrange(
                              "(j d) h -> d j h", d=128))
                    w.dma("sync", "sST%d" % jb, st2[jb][:, 0:ND * 128],
                          g2g[g][r0:r0 + D, :].rearrange(
                              "(j d) h -> d j h", d=128))
                    w.wait("tensor", "sST%d" % jb)
                    if jl == 0 and k == 0:
                        w.wait("tensor", "sA", ln2T_done)
                    if jl == 0:
                        w.wait("tensor", "sV", hT_free)
                    jp = jl % 2
                    GB = [0, 1][:NTH] if jp == 0 else [4, 5][:NTH]
                    UB = [2, 3][:NTH] if jp == 0 else [6, 7][:NTH]
                    if jl >= 2:
                        w.wait("tensor", "sV", gu_mult[jl % 2])
                    for dt in range(ND):
                        st = (dt == 0)
                        sp = (dt == ND - 1)
                        for hh in range(NTH):
                            rh = ln2T[:, dt * T + hh * TH:
                                      dt * T + (hh + 1) * TH]
                            w.t("matmul", pf32[GB[hh]][:, 0:TH],
                                st1[jb][:, dt * 128:(dt + 1) * 128],
                                rh, start=st, stop=sp, inc=sp)
                            w.t("matmul", pf32[UB[hh]][:, 0:TH],
                                st2[jb][:, dt * 128:(dt + 1) * 128],
                                rh, start=st, stop=sp, inc=sp)
                    st_used[jb] = w.c["sT"]
                    w.wait("scalar", "sT")
                    w.wait("scalar", "sV", sg_used)
                    for hh in range(NTH):
                        w.s("activation", sgb[:, hh * TH:(hh + 1) * TH],
                            pf32[GB[hh]][:, 0:TH], AF.Silu)
                    w.wait("vector", "sA")
                    if jl == 0:
                        w.wait("vector", "sT", down_prev_t)
                    for hh in range(NTH):
                        w.v("tensor_tensor",
                            hT[:, jl * T + hh * TH:jl * T + (hh + 1) * TH],
                            sgb[:, hh * TH:(hh + 1) * TH],
                            pf32[UB[hh]][:, 0:TH], op=OP.mult)
                    sg_used = w.c["sV"]
                    gu_mult[jl % 2] = w.c["sV"]
                hT_ready = w.c["sV"]
                dn_used = [0, 0]
                x2_used = [0, 0]
                dn_evac = [0, 0]
                for ci in range(ND):
                    own3 = ci // CSH
                    cloc = ci % CSH
                    cb = ci % 2
                    if k == 0 and ci == 0:
                        for rt2 in range(DSHT):
                            w.wait("sync", "sU%d" % (2 * NG12 + rt2), 1)
                    w.wait("sync", "sT", dn_used[cb])
                    r0 = own3 * HFF + k0 * 128
                    w.dma("sync", "sS3%d" % cb, st3[cb][:, 0:nk * 128],
                          g3g[cloc][r0:r0 + nk * 128, :].rearrange(
                              "(j h) c -> h j c", h=128))
                    if k > 0:
                        w.wait("sync", "sPW%d" % cb)
                        if k == len(c["HCHUNKS"]) - 1:
                            w.wait("sync", "sT", out_tp[cb])
                        w.dma("sync", "sPT%d" % cb, ptl[cb][:, 0:T],
                              part[ci * 128:(ci + 1) * 128, :])
                    last = (k == len(c["HCHUNKS"]) - 1)
                    if last:
                        w.wait("sync", "sCT", stageC_g)
                        w.wait("sync", "sV", x2_used[cb])
                        w.dma("sync", "sPT%d" % cb, x2tt[cb][:, 0:T],
                              x2sp[ci * 128:(ci + 1) * 128, :])
                    w.wait("tensor", "sS3%d" % cb)
                    w.wait("tensor", "sPT%d" % cb)
                    if ci == 0:
                        w.wait("tensor", "sV", hT_ready)
                    DB = [0, 1][:NTH] if cb == 0 else [2, 3][:NTH]
                    if ci >= 2:
                        w.wait("tensor", "sV", dn_evac[cb])
                    for jl in range(nk):
                        st = (jl == 0)
                        sp = (jl == nk - 1)
                        for hh in range(NTH):
                            w.t("matmul", pf32[DB[hh]][:, 0:TH],
                                st3[cb][:, jl * 128:(jl + 1) * 128],
                                hT[:, jl * T + hh * TH:
                                   jl * T + (hh + 1) * TH],
                                start=st, stop=sp, inc=sp)
                    dn_used[cb] = w.c["sT"]
                    w.wait("vector", "sT")
                    w.wait("vector", "sPW%d" % cb)
                    if not last:
                        for hh in range(NTH):
                            hs = slice(hh * TH, (hh + 1) * TH)
                            if k == 0:
                                w.v("tensor_copy", ptl[cb][:, hs],
                                    pf32[DB[hh]][:, 0:TH])
                            else:
                                w.v("tensor_tensor", ptl[cb][:, hs],
                                    ptl[cb][:, hs], pf32[DB[hh]][:, 0:TH],
                                    op=OP.add)
                        dn_evac[cb] = w.c["sV"]
                        w.wait("gpsimd", "sV")
                        w.dma("gpsimd", "sPW%d" % cb,
                              part[ci * 128:(ci + 1) * 128, :],
                              ptl[cb][:, 0:T])
                    else:
                        for hh in range(NTH):
                            hs = slice(hh * TH, (hh + 1) * TH)
                            w.v("tensor_tensor", ptl[cb][:, hs], ptl[cb][:, hs],
                                pf32[DB[hh]][:, 0:TH], op=OP.add)
                        w.v("tensor_tensor", ptl[cb][:, 0:T], ptl[cb][:, 0:T],
                            x2tt[cb][:, 0:T], op=OP.add)
                        x2_used[cb] = w.c["sV"]
                        dn_evac[cb] = w.c["sV"]
                        pend_out.append((ci, cb, w.c["sV"]))
                        if len(pend_out) > 1:
                            flush_out(*pend_out.pop(0))
                if last:
                    while pend_out:
                        flush_out(*pend_out.pop(0))
                hT_free = w.c["sV"]
                down_prev_t = w.c["sT"]
            w.wait("sync", "sOUT0")
            w.wait("sync", "sOUT1")
            return w

        @block.sync
        def _(eng):
            script("sync", eng)

        @block.vector
        def _(eng):
            script("vector", eng)

        @block.scalar
        def _(eng):
            script("scalar", eng)

        @block.tensor
        def _(eng):
            script("tensor", eng)

        @block.gpsimd
        def _(eng):
            script("gpsimd", eng)

    return nc


def shard_inputs(inputs, cfg):
    c = derived(cfg)
    D, T, HFF = c["D"], c["T"], c["HFF"]
    SH_ROWS = c["HT_SH"] * 128
    bf = ml_dtypes.bfloat16

    x = np.ascontiguousarray(np.asarray(
        inputs["L0_attn_x"], dtype=np.float32)).reshape(-1, D)
    wq = np.asarray(inputs["L0_attn_Wq"], dtype=np.float32).reshape(D, -1)
    bq = np.asarray(inputs["L0_attn_bq"], dtype=np.float32)
    wk = np.asarray(inputs["L0_attn_Wk"], dtype=np.float32).reshape(D, 128)
    bk = np.asarray(inputs["L0_attn_bk"], dtype=np.float32).reshape(128)
    wv = np.asarray(inputs["L0_attn_Wv"], dtype=np.float32).reshape(D, 128)
    bv = np.asarray(inputs["L0_attn_bv"], dtype=np.float32).reshape(128)
    w1 = np.asarray(inputs["L0_ffn_w1"], dtype=np.float32)
    w2 = np.asarray(inputs["L0_ffn_w2"], dtype=np.float32)
    w3 = np.asarray(inputs["L0_ffn_w3"], dtype=np.float32)

    w1p = np.zeros((SH_ROWS * NCORES, D), dtype=np.float32)
    w1p[:HFF] = w1
    w2p = np.zeros((SH_ROWS * NCORES, D), dtype=np.float32)
    w2p[:HFF] = w2

    bqT = np.ascontiguousarray(bq.T)
    idb = np.eye(128, dtype=np.float32).astype(bf)
    idf = np.eye(128, dtype=np.float32)
    cstv = np.zeros((128, 40), dtype=np.float32)
    cstv[:, 0] = np.float32(1e-5)
    for j in range(15):
        cstv[:, 8 + j] = -NF4_BOUNDS[j]
        cstv[:, 24 + j] = NF4_HDELTA[j]
    cstv[:, 39] = AM_FLOOR

    dsh = D // NCORES
    in_maps = []
    for r in range(NCORES):
        in_maps.append(dict(
            x_in=np.ascontiguousarray(x[r * T:(r + 1) * T]),
            wq_in=np.ascontiguousarray(wq[r * dsh:(r + 1) * dsh]),
            wk_in=np.ascontiguousarray(wk[r * dsh:(r + 1) * dsh]),
            wv_in=np.ascontiguousarray(wv[r * dsh:(r + 1) * dsh]),
            bqT_in=bqT,
            bk_in=np.ascontiguousarray(bk.reshape(1, 128)),
            bv_in=np.ascontiguousarray(bv.reshape(1, 128)),
            w1_in=np.ascontiguousarray(w1p[r * SH_ROWS:(r + 1) * SH_ROWS]),
            w2_in=np.ascontiguousarray(w2p[r * SH_ROWS:(r + 1) * SH_ROWS]),
            w3_in=np.ascontiguousarray(w3[r * dsh:(r + 1) * dsh]),
            idb_in=idb, idf_in=idf, cst_in=cstv,
        ))
    return in_maps


def kernel(**inputs):
    cfg = cfg_full()
    in_maps = shard_inputs(inputs, cfg)
    nc = build_program(cfg)
    res = run_bass_kernel_spmd(nc, in_maps, core_ids=list(range(NCORES)))
    out = np.concatenate([np.asarray(r["out"]) for r in res.results], axis=0)
    return out.reshape(4, 2048, 4096)


# version 21
# speedup vs baseline: 1.1475x; 1.0392x over previous
import math
import numpy as np
import ml_dtypes

import concourse.bass as bass
import concourse.mybir as mybir
from concourse.bass_utils import run_bass_kernel_spmd

F32 = mybir.dt.float32
BF16 = mybir.dt.bfloat16
AF = mybir.ActivationFunctionType
OP = mybir.AluOpType
AX = mybir.AxisListType

NCORES = 8

NF4_CODE = np.array([
    -1.0, -0.6961928009986877, -0.5250730514526367, -0.39491748809814453,
    -0.28444138169288635, -0.18477343022823334, -0.09105003625154495, 0.0,
    0.07958029955625534, 0.16093020141124725, 0.24611230194568634,
    0.33791524171829224, 0.44070982933044434, 0.5626170039176941,
    0.7229568362236023, 1.0], dtype=np.float32)
NF4_BOUNDS = ((NF4_CODE[:-1] + NF4_CODE[1:]) * np.float32(0.5)).astype(np.float32)
BIG = float(np.float32(1e30))
NF4_HDELTA = [float(np.float32(np.float64(NF4_CODE[j + 1]) - np.float64(NF4_CODE[j])) / np.float32(2.0))
              for j in range(15)]
AM_FLOOR = float(np.float32(1e-30))


def cfg_full():
    return dict(D=4096, HFF=11008, T=1024)


def derived(c):
    d = dict(c)
    d["ND"] = c["D"] // 128
    d["NT"] = c["T"] // 128
    d["NH"] = c["HFF"] // 128
    d["NHEADS"] = c["D"] // 128
    d["HT_SH"] = (d["NH"] + NCORES - 1) // NCORES
    d["DSH"] = c["D"] // NCORES
    d["DSHT"] = d["DSH"] // 128
    d["CSH"] = c["D"] // NCORES // 128
    nh = c["HFF"] // 128
    ch3 = []
    t0 = 0
    while t0 < nh:
        nt_ = min(32, nh - t0)
        ch3.append((t0, nt_))
        t0 += nt_
    d["CH3L"] = ch3
    d["MAXC3"] = max(nt_ for (_, nt_) in ch3) * 128
    nh = d["NH"]
    base = [nh // 4] * 4
    for i in range(nh % 4):
        base[i] += 1
    cuts = [0]
    for b in base:
        cuts.append(cuts[-1] + b)
    d["HCHUNKS"] = [(cuts[i], cuts[i + 1]) for i in range(4)]
    d["MAXCH"] = max(b for b in base)
    return d


SM_SCALE = float(np.float32(1.0) / np.sqrt(np.float32(128.0)))


class Walk:

    def __init__(self, role, eng, sems):
        self.role = role
        self.eng = eng
        self.sems = sems
        self.c = {k: 0 for k in sems}

    def wait(self, wrole, sem, val=None):
        if val is None:
            val = self.c[sem]
        if self.role == wrole:
            self.eng.wait_ge(self.sems[sem], val)

    def dma(self, qrole, sem, dst, src):
        self.c[sem] += 16
        if self.role == qrole:
            self.eng.dma_start(out=dst, in_=src).then_inc(self.sems[sem], 16)

    def _ser(self, role_, sem, opname, a, kw):
        tgt = self.c[sem]
        self.c[sem] += 1
        if self.role == role_:
            self.eng.wait_ge(self.sems[sem], tgt)
            getattr(self.eng, opname)(*a, **kw).then_inc(self.sems[sem], 1)

    def v(self, *a, **kw):
        self._ser("vector", "sV", a[0], a[1:], kw)

    def s(self, *a, **kw):
        self._ser("scalar", "sA", a[0], a[1:], kw)

    def p(self, *a, **kw):
        self._ser("gpsimd", "sP", a[0], a[1:], kw)

    def t(self, *a, inc=False, **kw):
        if inc:
            self.c["sT"] += 1
        if self.role == "tensor":
            inst = getattr(self.eng, a[0])(*a[1:], **kw)
            if inc:
                inst.then_inc(self.sems["sT"], 1)

    def cc(self, sem, ins, outs):
        self.cc2("gpsimd", sem, ins, outs)

    def cc2(self, role, sem, ins, outs):
        self.c[sem] += 1
        if self.role == role:
            self.eng.collective_compute(
                "AllGather", OP.bypass,
                replica_groups=[list(range(NCORES))],
                ins=ins, outs=outs,
            ).then_inc(self.sems[sem])


def build_program(cfg):
    c = derived(cfg)
    D, T, HFF = c["D"], c["T"], c["HFF"]
    ND, NT, NH = c["ND"], c["NT"], c["NH"]
    NHEADS, HT_SH, DSH, DSHT = c["NHEADS"], c["HT_SH"], c["DSH"], c["DSHT"]
    CSH = c["CSH"]
    HPAD = HT_SH * NCORES * 128
    SH_ROWS = HT_SH * 128

    nc = bass.Bass("TRN2", target_bir_lowering=False)

    x_in = nc.dram_tensor("x_in", [T, D], F32, kind="ExternalInput").ap()
    wq_in = nc.dram_tensor("wq_in", [DSH, D], F32, kind="ExternalInput").ap()
    wk_in = nc.dram_tensor("wk_in", [DSH, 128], F32, kind="ExternalInput").ap()
    wv_in = nc.dram_tensor("wv_in", [DSH, 128], F32, kind="ExternalInput").ap()
    bqT_in = nc.dram_tensor("bqT_in", [NHEADS, 128], F32, kind="ExternalInput").ap()
    bk_in = nc.dram_tensor("bk_in", [1, 128], F32, kind="ExternalInput").ap()
    bv_in = nc.dram_tensor("bv_in", [1, 128], F32, kind="ExternalInput").ap()
    w1_in = nc.dram_tensor("w1_in", [SH_ROWS, D], F32, kind="ExternalInput").ap()
    w2_in = nc.dram_tensor("w2_in", [SH_ROWS, D], F32, kind="ExternalInput").ap()
    w3_in = nc.dram_tensor("w3_in", [DSH, HFF], F32, kind="ExternalInput").ap()
    idb_in = nc.dram_tensor("idb_in", [128, 128], BF16, kind="ExternalInput").ap()
    idf_in = nc.dram_tensor("idf_in", [128, 128], F32, kind="ExternalInput").ap()
    cst_in = nc.dram_tensor("cst_in", [128, 40], F32, kind="ExternalInput").ap()
    out_t = nc.dram_tensor("out", [D, T], F32, kind="ExternalOutput").ap()

    ctrb_q = nc.dram_tensor("ctrb_q", [DSH, 384], BF16).ap()
    gath_q = nc.dram_tensor("gath_q", [D, 384], BF16).ap()
    W12G = [(g0, min(4, HT_SH - g0)) for g0 in range(0, HT_SH, 4)]
    c1g = [nc.dram_tensor("c1g%d" % i, [gs * D, 128], BF16).ap()
           for i, (g0, gs) in enumerate(W12G)]
    g1g = [nc.dram_tensor("g1g%d" % i, [NCORES * gs * D, 128], BF16).ap()
           for i, (g0, gs) in enumerate(W12G)]
    c2g = [nc.dram_tensor("c2g%d" % i, [gs * D, 128], BF16).ap()
           for i, (g0, gs) in enumerate(W12G)]
    g2g = [nc.dram_tensor("g2g%d" % i, [NCORES * gs * D, 128], BF16).ap()
           for i, (g0, gs) in enumerate(W12G)]
    c3g = [nc.dram_tensor("c3g%d" % rt, [HFF, 128], BF16).ap()
           for rt in range(DSHT)]
    g3g = [nc.dram_tensor("g3g%d" % rt, [NCORES * HFF, 128], BF16).ap()
           for rt in range(DSHT)]
    x2sp = nc.dram_tensor("x2sp", [D, T], F32).ap()
    part = nc.dram_tensor("part", [D, T], F32).ap()

    from contextlib import ExitStack
    with ExitStack() as ctx:
        def sb(name, shape, dt):
            return ctx.enter_context(nc.sbuf_tensor(name, shape, dt))

        W_all = sb("W_all", [128, ND * 384], BF16)
        ln2T = sb("ln2T", [128, ND * T], BF16)
        bias_row = sb("bias_row", [1, 384], F32)
        ones1 = sb("ones1", [1, 128], F32)
        ones32 = sb("ones32", [NHEADS, 1], F32)
        bqTs = sb("bqTs", [NHEADS, 128], F32)
        idnb = sb("idnb", [128, 128], BF16)
        idnf = sb("idnf", [128, 128], F32)
        cst = sb("cst", [128, 40], F32)
        AR_BYTES = (nc.sbuf_bytes_remaining - 64) & ~63
        arena = sb("arena", [128, AR_BYTES], mybir.dt.int8)

        def carve(off, nelem, dt, name=None):
            nb = nelem * (4 if dt == F32 else 2)
            assert off % 4 == 0 and off + nb <= AR_BYTES, (off, nb)
            return arena[:, off:off + nb].bitcast(dt), off + nb

        MAXC3 = c["MAXC3"]
        o = 0
        SCR = min(max(D, MAXC3), 2048)
        wfb = []
        for i in range(2):
            a, o = carve(o, SCR, F32)
            wfb.append(a)
        wn_, o = carve(o, SCR, F32)
        wnB_, o = carve(o, SCR, F32)
        wnP = [wn_, wnB_]
        tmpA, o = carve(o, SCR, F32)
        tmpB, o = carve(o, SCR, F32)
        accB_, o = carve(o, SCR, F32)
        accB2_, o = carve(o, SCR, F32)
        accP = [accB_, accB2_]
        dqb, o = carve(o, max(MAXC3, D), BF16)
        trb, o = carve(o, max(MAXC3, D), BF16)
        amb_, o = carve(o, 64, F32)
        amb2_, o = carve(o, 64, F32)
        ambP = [amb_, amb2_]
        recb_, o = carve(o, 64, F32)
        recb2_, o = carve(o, 64, F32)
        recbP = [recb_, recb2_]
        qcon, o = carve(o, DSHT * 384, BF16)
        tmpC, o = carve(o, SCR, F32)
        tmpD, o = carve(o, SCR, F32)
        dqbB, o = carve(o, max(MAXC3, D), BF16)
        trbB, o = carve(o, max(MAXC3, D), BF16)
        B_END = o
        o = 0
        xt = []
        for i in range(2):
            a, o = carve(o, D, F32)
            xt.append(a)
        attnA, o = carve(o, 32 * 128, F32)
        attnB_off = o
        attnB, o = carve(o, 32 * 128, F32)
        x2Tb, _ = carve(attnB_off, D, F32)
        ln1b, o = carve(o, D, BF16)
        ln2b, o = carve(o, D, BF16)
        ln1Tb, o = carve(o, D, BF16)
        x2t, o = carve(o, D, F32)
        qkv, o = carve(o, 384, F32)
        s_sb, o = carve(o, 128, F32)
        o_sb, o = carve(o, 128, F32)
        o2_sb, o = carve(o, 128, F32)
        rs_sb, o = carve(o, 128, F32)
        mx, o = carve(o, 128, F32)
        m1s, o = carve(o, 128, F32)
        m2s, o = carve(o, 128, F32)
        sm1, o = carve(o, 1, F32)
        sm2, o = carve(o, 1, F32)
        mmean, o = carve(o, 1, F32)
        ex2, o = carve(o, 1, F32)
        msq, o = carve(o, 1, F32)
        varb, o = carve(o, 1, F32)
        sdv, o = carve(o, 1, F32)
        rstd, o = carve(o, 1, F32)
        kmx, o = carve(o, 1, F32)
        kmn, o = carve(o, 1, F32)
        C_END = o
        o = 0
        hT, o = carve(o, c["MAXCH"] * T, BF16)
        st1 = []
        st2 = []
        st3 = []
        for i in range(2):
            a, o = carve(o, ND * 128, BF16)
            st1.append(a)
        for i in range(2):
            a, o = carve(o, ND * 128, BF16)
            st2.append(a)
        for i in range(2):
            a, o = carve(o, c["MAXCH"] * 128, BF16)
            st3.append(a)
        sgb, o = carve(o, T, F32)
        ptl = []
        for i in range(2):
            a, o = carve(o, T, F32)
            ptl.append(a)
        x2tt = []
        for i in range(2):
            a, o = carve(o, T, F32)
            x2tt.append(a)
        oev = []
        for i in range(2):
            a, o = carve(o, T, F32)
            oev.append(a)
        F_END = o
        assert max(B_END, C_END, F_END) <= AR_BYTES

        banks = []
        for i in range(8):
            banks.append(ctx.enter_context(
                nc.psum_tensor("bank%d" % i, [128, 512], F32)))

        NG12 = len([1 for g0 in range(0, HT_SH, 4)])
        NUNITS = 2 * NG12 + DSHT
        sem_names = ["sIN", "sWALL", "sBF0", "sBF1", "sX0", "sX1",
                     "sST0", "sST1", "sS30", "sS31", "sPT0", "sPT1",
                     "sV", "sA", "sT", "sG",
                     "sCCQ", "sOUT0", "sOUT1",
                     "sPW0", "sPW1", "sP", "sBF2_0", "sBF2_1", "sCT"]
        sem_names += ["sU%d" % i for i in range(NUNITS)]
        sems = {n: ctx.enter_context(nc.semaphore(n)) for n in sem_names}
        block = ctx.enter_context(nc.Block())

        def script(role, eng):
            w = Walk(role, eng, sems)
            pb16 = [b.ap().bitcast(BF16) for b in banks]
            pf32 = [b.ap() for b in banks]

            w.dma("sync", "sIN", cst[:], cst_in)
            w.dma("sync", "sIN", idnb[:], idb_in)
            w.dma("sync", "sIN", idnf[:], idf_in)
            w.dma("sync", "sIN", bqTs[:], bqT_in)
            w.dma("sync", "sIN", bias_row[0:1, 128:256], bk_in)
            w.dma("sync", "sIN", bias_row[0:1, 256:384], bv_in)
            setup_in = w.c["sIN"]
            w.wait("vector", "sIN", setup_in)
            w.v("memset", ones1[:], 1.0)
            w.v("memset", ones32[:], 1.0)
            wq_done = [0, 0]
            NQH = max(1, (D * 4) // (2048 * 4))
            QH = D // NQH
            for i in range(DSHT):
                for h2 in range(NQH):
                    sc = i * NQH + h2
                    w.wait("sync", "sV", wq_done[sc % 2])
                    w.dma("sync", "sBF%d" % (sc % 2), wfb[sc % 2][:, 0:QH],
                          wq_in.rearrange("(i p) d -> i p d", p=128)[i][
                              :, h2 * QH:(h2 + 1) * QH])
                    w.wait("vector", "sBF%d" % (sc % 2))
                    w.v("tensor_reduce", wn_[:, 0:QH // NHEADS],
                        wfb[sc % 2][:, 0:QH].rearrange(
                            "p (d h) -> p d h", h=NHEADS),
                        axis=AX.X, op=OP.add)
                    wq_done[sc % 2] = w.c["sV"]
                    w.v("tensor_copy",
                        qcon[:, i * 384 + h2 * (QH // NHEADS):
                             i * 384 + (h2 + 1) * (QH // NHEADS)],
                        wn_[:, 0:QH // NHEADS])
            qred_done = w.c["sV"]
            w.wait("gpsimd", "sIN", setup_in)
            for i in range(DSHT):
                w.dma("gpsimd", "sG", qcon[:, i * 384 + 128:i * 384 + 256],
                      wk_in.rearrange("(i p) e -> i p e", p=128)[i])
                w.dma("gpsimd", "sG", qcon[:, i * 384 + 256:i * 384 + 384],
                      wv_in.rearrange("(i p) e -> i p e", p=128)[i])
            w.wait("gpsimd", "sV", qred_done)
            w.wait("gpsimd", "sG")
            w.dma("gpsimd", "sG", ctrb_q.rearrange("(i p) q -> p i q", p=128),
                  qcon[:])
            w.wait("gpsimd", "sG")
            w.cc("sCCQ", [ctrb_q.opt()], [gath_q.opt()])
            w.wait("tensor", "sIN", setup_in)
            w.wait("tensor", "sV", qred_done)
            w.t("matmul", pf32[2][0:1, 0:128], ones32[:], bqTs[:],
                start=True, stop=True, inc=True)
            tb = w.c["sT"]
            w.wait("vector", "sT", tb)
            w.v("tensor_copy", bias_row[0:1, 0:128], pf32[2][0:1, 0:128])

            units = []
            for wi in range(HT_SH):
                units.append(("w1", wi))
            for wi in range(HT_SH):
                units.append(("w2", wi))
            for rt in range(DSHT):
                for chi in range(len(c["CH3L"])):
                    units.append(("w3", (rt, chi)))

            normdone = {}
            dq_done = {}
            tr_done = {}
            ev_done = {}
            ct_done = {}
            bank_ev = [0, 0]

            def subsplit(width, scr):
                sub = []
                off = 0
                while off < width:
                    rem = width - off
                    if rem <= scr:
                        sw = rem
                    elif rem >= 2 * scr:
                        sw = scr
                    else:
                        sw = 64 * ((rem // 2 + 63) // 64)
                    sub.append((off, sw))
                    off += sw
                return sub

            def transposes(ntiles, unit_key, dqsrc, trdst, psem, pcnt):
                ngr = (ntiles + 7) // 8
                w.wait("tensor", psem, pcnt)
                for g in range(ngr):
                    t0 = g * 8
                    tn = min(8, ntiles - t0)
                    bk = g % 2
                    w.wait("tensor", "sA", bank_ev[bk])
                    for tt in range(tn):
                        w.t("transpose", pb16[bk][:, (tt) * 128:(tt + 1) * 128],
                            dqsrc[:, (t0 + tt) * 128:(t0 + tt + 1) * 128],
                            idnb[:], inc=True)
                    w.wait("scalar", "sT")
                    w.s("copy", trdst[:, t0 * 128:(t0 + tn) * 128],
                        pb16[bk][:, 0:tn * 128])
                    bank_ev[bk] = w.c["sA"]
                tr_done[unit_key] = w.c["sT"]
                ev_done[unit_key] = w.c["sA"]

            def unit_src(kind, meta):
                if kind == "w1":
                    return (w1_in.rearrange("(i p) d -> i p d", p=128)[meta],
                            D, ND)
                if kind == "w2":
                    return (w2_in.rearrange("(i p) d -> i p d", p=128)[meta],
                            D, ND)
                rt, chi = meta
                cs, nt_ = c["CH3L"][chi]
                return (w3_in.rearrange("(i p) h -> i p h", p=128)[rt][
                    :, cs * 128:(cs + nt_) * 128], nt_ * 128, nt_)

            ccq = [0]

            def emit_cc(sem, ins_, outs_):
                ccq[0] += 1
                w.wait("gpsimd", "sCT")
                w.cc2("gpsimd", sem, ins_, outs_)

            def emit_contrib(kind, meta, trsrc, uidx):
                if kind in ("w1", "w2"):
                    ht = meta
                    g = ht // 4
                    g0, gs = W12G[g]
                    ctr = c1g[g] if kind == "w1" else c2g[g]
                    gt = g1g[g] if kind == "w1" else g2g[g]
                    w.dma("scalar", "sCT",
                          ctr[(ht - g0) * D:(ht - g0 + 1) * D, :].rearrange(
                              "(j d) h -> d j h", d=128),
                          trsrc[:, 0:ND * 128])
                    if ht == g0 + gs - 1:
                        sidx = g if kind == "w1" else NG12 + g
                        emit_cc("sU%d" % sidx, [ctr.opt()], [gt.opt()])
                else:
                    rt, chi = meta
                    cs, nt_ = c["CH3L"][chi]
                    ctr = c3g[rt]
                    gt = g3g[rt]
                    w.dma("scalar", "sCT",
                          ctr[cs * 128:(cs + nt_) * 128, :].rearrange(
                              "(j h) c -> h j c", h=128),
                          trsrc[:, 0:nt_ * 128])
                    if chi == len(c["CH3L"]) - 1:
                        emit_cc("sU%d" % (2 * NG12 + rt),
                                [ctr.opt()], [gt.opt()])

            ch_state = {"sign": 0, "tr": 0}
            pending = [None]

            def flush_pending():
                if pending[0] is None:
                    return
                (pk, pm, pntt, pdq, pdqbuf, ptrbuf, puidx) = pending[0]
                pending[0] = None
                w.wait("scalar", "sCT", w.c["sCT"])
                transposes(pntt, (pk, pm), pdqbuf, ptrbuf, "sP", pdq)
                ch_state["tr"] = tr_done[(pk, pm)]
                w.wait("scalar", "sA", ev_done[(pk, pm)])
                emit_contrib(pk, pm, ptrbuf, puidx)

            subq = []
            for u, (kind, meta) in enumerate(units):
                srcap, width, ntt = unit_src(kind, meta)
                dqd = dqb if u % 2 == 0 else dqbB
                trd = trb if u % 2 == 0 else trbB
                subs = subsplit(width, 2048)
                for si, (soff, sw) in enumerate(subs):
                    subq.append(dict(u=u, kind=kind, meta=meta, ntt=ntt,
                                     srcap=srcap, soff=soff, sw=sw,
                                     dqd=dqd, trd=trd,
                                     first=(si == 0),
                                     last=(si == len(subs) - 1)))
            NSQ = len(subq)
            marks = {"norm": {}, "fin": {}, "sign": {},
                     "memset": {}, "stt14": {}}

            def b_dma(i):
                e = subq[i]
                if i >= 2:
                    w.wait("sync", "sP", marks["norm"][i - 2])
                w.dma("sync", "sBF%d" % (i % 2), wfb[i % 2][:, 0:e["sw"]],
                      e["srcap"][:, e["soff"]:e["soff"] + e["sw"]])

            def b_prep(i):
                e = subq[i]
                par = i % 2
                sw = e["sw"]
                nb = sw // 64
                sv = wfb[par][:, 0:sw]
                w.wait("vector", "sBF%d" % par)
                if i >= 2:
                    w.wait("vector", "sP", marks["fin"].get(i - 2, 0))
                w.v("tensor_reduce", ambP[par][:, 0:nb],
                    sv.rearrange("p (b k) -> p b k", k=64),
                    axis=AX.X, op=OP.max, apply_absolute_value=True)
                w.v("tensor_scalar", ambP[par][:, 0:nb], ambP[par][:, 0:nb],
                    AM_FLOOR, None, op0=OP.max)
                w.v("reciprocal", recbP[par][:, 0:nb], ambP[par][:, 0:nb])
                w.wait("gpsimd", "sV")
                if e["first"]:
                    w.wait("gpsimd", "sT", ch_state["tr"])
                if i >= 2:
                    w.wait("gpsimd", "sA", marks["sign"].get(i - 2, 0))
                w.p("tensor_tensor", wnP[par][:, 0:sw], sv,
                    recbP[par][:, 0:nb].unsqueeze(2).broadcast_to(
                        [128, nb, 64]),
                    op=OP.mult)
                marks["norm"][i] = w.c["sP"]
                w.p("memset", accP[par][:, 0:sw], 0.0)
                marks["memset"][i] = w.c["sP"]

            def b_steps(i):
                e = subq[i]
                par = i % 2
                sw = e["sw"]
                v0 = w.c["sV"]
                stt_cnt = {-4: v0, -3: v0, -2: v0, -1: v0}
                masks = [tmpA, tmpB, tmpC, tmpD]
                for j in range(15):
                    tb_ = masks[j % 4]
                    w.wait("scalar", "sV", stt_cnt[j - 4])
                    w.wait("scalar", "sP", marks["norm"][i])
                    w.s("activation", tb_[:, 0:sw], wnP[par][:, 0:sw],
                        AF.Sign, bias=cst[:, 8 + j:9 + j])
                    marks["sign"][i] = w.c["sA"]
                    if j == 0:
                        w.wait("vector", "sP", marks["memset"][i])
                    w.wait("vector", "sA")
                    w.v("scalar_tensor_tensor", accP[par][:, 0:sw],
                        tb_[:, 0:sw], NF4_HDELTA[j], accP[par][:, 0:sw],
                        op0=OP.mult, op1=OP.add)
                    stt_cnt[j] = w.c["sV"]
                marks["stt14"][i] = w.c["sV"]

            def b_fin(i):
                e = subq[i]
                par = i % 2
                sw = e["sw"]
                nb = sw // 64
                w.wait("gpsimd", "sV", marks["stt14"][i])
                w.p("tensor_tensor",
                    e["dqd"][:, e["soff"]:e["soff"] + e["sw"]],
                    accP[par][:, 0:sw],
                    ambP[par][:, 0:nb].unsqueeze(2).broadcast_to(
                        [128, nb, 64]),
                    op=OP.mult)
                marks["fin"][i] = w.c["sP"]
                if e["last"]:
                    flush_pending()
                    pending[0] = (e["kind"], e["meta"], e["ntt"],
                                  w.c["sP"], e["dqd"], e["trd"], e["u"])

            w.wait("sync", "sV", qred_done)
            b_dma(0)
            if NSQ > 1:
                b_dma(1)
            b_prep(0)
            for i in range(NSQ):
                if i + 2 < NSQ:
                    b_dma(i + 2)
                if i + 1 < NSQ:
                    b_prep(i + 1)
                b_steps(i)
                b_fin(i)
            flush_pending()
            stageB_v = w.c["sV"]
            stageB_t = w.c["sT"]
            stageB_a = w.c["sA"]

            w.wait("sync", "sCCQ", 1)
            w.wait("sync", "sV", stageB_v)
            w.wait("sync", "sT", stageB_t)
            w.wait("sync", "sA", stageB_a)
            w.wait("sync", "sG")
            w.wait("sync", "sCT")
            w.dma("sync", "sWALL", W_all[:],
                  gath_q.rearrange("(k d) q -> d k q", d=128))

            def layernorm(src, dstb):
                w.v("tensor_reduce", sm1[:, 0:1], src, axis=AX.X, op=OP.add)
                w.v("scalar_tensor_tensor", attnA[:, 0:D], src, 1.0, src,
                    op0=OP.mult, op1=OP.mult, accum_out=sm2[:, 0:1])
                w.v("tensor_scalar", mmean[:, 0:1], sm1[:, 0:1],
                    1.0 / D, None, op0=OP.mult)
                w.v("tensor_scalar", ex2[:, 0:1], sm2[:, 0:1],
                    1.0 / D, None, op0=OP.mult)
                w.v("tensor_scalar", msq[:, 0:1], mmean[:, 0:1],
                    mmean[:, 0:1], None, op0=OP.mult)
                w.v("tensor_tensor", varb[:, 0:1], ex2[:, 0:1], msq[:, 0:1],
                    op=OP.subtract)
                w.wait("scalar", "sV")
                w.s("activation", sdv[:, 0:1], varb[:, 0:1], AF.Ln,
                    bias=cst[:, 0:1])
                w.s("activation", rstd[:, 0:1], sdv[:, 0:1], AF.Exp,
                    scale=-0.5)
                w.wait("vector", "sA")
                w.v("tensor_scalar", dstb, src, mmean[:, 0:1], rstd[:, 0:1],
                    op0=OP.subtract, op1=OP.mult)

            def tpose_128(srcap, ntiles, ident, pbanks, dst_write, fp32=False):
                per = 4 if fp32 else 8
                ngr = (ntiles + per - 1) // per
                for g in range(ngr):
                    t0 = g * per
                    tn = min(per, ntiles - t0)
                    bk = pbanks[g % 2]
                    w.wait("tensor", "sA", bank_ev[g % 2])
                    for tt in range(tn):
                        dst = (pf32[bk][:, tt * 128:(tt + 1) * 128] if fp32
                               else pb16[bk][:, tt * 128:(tt + 1) * 128])
                        w.t("transpose", dst,
                            srcap[:, (t0 + tt) * 128:(t0 + tt + 1) * 128],
                            ident, inc=True)
                    w.wait("scalar", "sT")
                    dst_write(g, t0, tn, bk)
                    bank_ev[g % 2] = w.c["sA"]

            xt_free = [stageB_v, stageB_v]
            ln1_tpose_prev = stageB_t
            ln2_tpose = {-1: stageB_t}
            x2_tpose = {-1: stageB_t}
            qkv_evac = {-2: stageB_v, -1: stageB_v}
            qkv_mm = {}

            def ev_ln1(g, t0, tn, bk):
                w.s("copy", ln1Tb[:, t0 * 128:(t0 + tn) * 128],
                    pb16[bk][:, 0:tn * 128])

            def c_front(tau):
                nonlocal ln1_tpose_prev
                buf = xt[tau % 2]
                w.wait("sync", "sV", xt_free[tau % 2])
                w.dma("sync", "sX%d" % (tau % 2), buf[:, 0:D],
                      x_in.rearrange("(i p) d -> i p d", p=128)[tau])
                w.wait("vector", "sX%d" % (tau % 2))
                if tau == 0:
                    w.wait("vector", "sIN", setup_in)
                w.wait("vector", "sT", ln1_tpose_prev)
                layernorm(buf[:, 0:D], ln1b[:, 0:D])
                ln1_done = w.c["sV"]
                w.wait("tensor", "sV", ln1_done)
                tpose_128(ln1b, ND, idnb[:], [0, 1], ev_ln1)
                ln1_tpose_prev = w.c["sT"]
                ln1T_evac = w.c["sA"]
                bank = 2 + (tau % 2)
                w.wait("tensor", "sA", ln1T_evac)
                if tau == 0:
                    w.wait("tensor", "sWALL", 16)
                    w.wait("tensor", "sV")
                else:
                    w.wait("tensor", "sV", qkv_evac[tau - 2])
                for dt in range(ND):
                    w.t("matmul", pf32[bank][:, 0:384],
                        ln1Tb[:, dt * 128:(dt + 1) * 128],
                        W_all[:, dt * 384:(dt + 1) * 384],
                        start=(dt == 0), stop=False)
                w.t("matmul", pf32[bank][:, 0:384], ones1[:], bias_row[:],
                    start=False, stop=True, inc=True)
                qkv_mm[tau] = w.c["sT"]

            def c_back(tau):
                buf = xt[tau % 2]
                bank = 2 + (tau % 2)
                w.wait("vector", "sT", qkv_mm[tau])
                w.v("tensor_copy", qkv[:, 0:384], pf32[bank][:, 0:384])
                qkv_evac[tau] = w.c["sV"]
                w.v("tensor_reduce", kmx[:, 0:1], qkv[:, 128:256],
                    axis=AX.X, op=OP.max)
                w.v("tensor_reduce", kmn[:, 0:1], qkv[:, 128:256],
                    axis=AX.X, op=OP.min)
                w.v("tensor_scalar", m1s[:, 0:128], qkv[:, 0:128],
                    kmx[:, 0:1], None, op0=OP.mult)
                w.v("tensor_scalar", m2s[:, 0:128], qkv[:, 0:128],
                    kmn[:, 0:1], None, op0=OP.mult)
                w.v("tensor_tensor", mx[:, 0:128], m1s[:, 0:128],
                    m2s[:, 0:128], op=OP.max)
                exp_prev = None
                for q in range(4):
                    for dd in range(32):
                        d = 32 * q + dd
                        if dd == 0 and exp_prev is not None:
                            w.wait("vector", "sA", exp_prev)
                        w.v("tensor_scalar", attnA[:, dd * 128:(dd + 1) * 128],
                            qkv[:, 128:256], qkv[:, d:d + 1], mx[:, d:d + 1],
                            op0=OP.mult, op1=OP.subtract)
                    w.wait("scalar", "sV")
                    if q == 0:
                        w.wait("scalar", "sCT")
                    w.s("activation", attnB[:, 0:4096], attnA[:, 0:4096],
                        AF.Exp, scale=SM_SCALE)
                    exp_prev = w.c["sA"]
                    w.wait("vector", "sA", exp_prev)
                    w.v("tensor_reduce", s_sb[:, 32 * q:32 * q + 32],
                        attnB[:, 0:4096].rearrange("p (d e) -> p d e", e=128),
                        axis=AX.X, op=OP.add)
                    w.v("scalar_tensor_tensor", attnA[:, 0:4096],
                        attnB[:, 0:4096], 1.0,
                        qkv[:, 256:384].unsqueeze(1).broadcast_to(
                            [128, 32, 128]),
                        op0=OP.mult, op1=OP.mult)
                    w.v("tensor_reduce", o_sb[:, 32 * q:32 * q + 32],
                        attnA[:, 0:4096].rearrange("p (d e) -> p d e", e=128),
                        axis=AX.X, op=OP.add)
                w.v("reciprocal", rs_sb[:, 0:128], s_sb[:, 0:128])
                w.v("tensor_tensor", o2_sb[:, 0:128], o_sb[:, 0:128],
                    rs_sb[:, 0:128], op=OP.mult)
                w.wait("vector", "sT", x2_tpose[tau - 1])
                w.v("tensor_tensor", x2t[:, 0:D], buf[:, 0:D],
                    o2_sb[:, 0:128].unsqueeze(2).broadcast_to(
                        [128, 128, NHEADS]),
                    op=OP.add)
                xt_free[tau % 2] = w.c["sV"]
                w.wait("vector", "sT", ln2_tpose[tau - 1])
                layernorm(x2t[:, 0:D], ln2b[:, 0:D])
                ln2_done = w.c["sV"]
                w.wait("tensor", "sV", ln2_done)

                def ev_ln2(g, t0, tn, bk):
                    for tt in range(t0, t0 + tn):
                        w.s("copy", ln2T[:, tt * T + tau * 128:
                                         tt * T + (tau + 1) * 128],
                            pb16[bk][:, (tt - t0) * 128:(tt - t0 + 1) * 128])
                tpose_128(ln2b, ND, idnb[:], [0, 1], ev_ln2)
                ln2_tpose[tau] = w.c["sT"]
                x2sp_prev = w.c["sCT"]
                w.wait("tensor", "sCT", x2sp_prev)

                def ev_x2(g, t0, tn, bk):
                    w.s("copy", x2Tb[:, t0 * 128:(t0 + tn) * 128],
                        pf32[bk][:, 0:tn * 128])
                tpose_128(x2t, ND, idnf[:], [0, 1], ev_x2, fp32=True)
                x2_tpose[tau] = w.c["sT"]
                w.wait("scalar", "sA", w.c["sA"])
                w.dma("scalar", "sCT",
                      x2sp[:, tau * 128:(tau + 1) * 128].rearrange(
                          "(j c) t -> c j t", c=128),
                      x2Tb[:, 0:D])

            c_front(0)
            if NT > 1:
                c_front(1)
            for tau in range(NT):
                c_back(tau)
                if tau + 2 < NT:
                    c_front(tau + 2)
            stageC_v = w.c["sV"]
            stageC_t = w.c["sT"]
            stageC_a = w.c["sA"]
            stageC_g = w.c["sCT"]
            ln2T_done = stageC_a

            hmap = [(t // HT_SH, t % HT_SH) for t in range(NH)]
            for i in range(2 * NG12):
                w.wait("sync", "sU%d" % i, 1)
            w.wait("sync", "sG")
            w.wait("sync", "sCT")
            w.wait("sync", "sV", stageC_v)
            w.wait("sync", "sA", stageC_a)
            w.wait("sync", "sT", stageC_t)
            st_used = [0, 0]
            sg_used = stageC_v
            def flush_out(ci, cb, sum_done):
                w.wait("gpsimd", "sV", sum_done)
                w.dma("gpsimd", "sOUT%d" % (ci % 2),
                      out_t[ci * 128:(ci + 1) * 128, :],
                      ptl[cb][:, 0:T])
            gu_mult = [0, 0]
            hT_free = stageC_v
            down_prev_t = stageC_t
            TH = min(512, T)
            NTH = T // TH
            for k, (k0, k1) in enumerate(c["HCHUNKS"]):
                nk = k1 - k0
                for jl, t in enumerate(range(k0, k1)):
                    own, idx = hmap[t]
                    jb = jl % 2
                    w.wait("sync", "sT", st_used[jb])
                    g = idx // 4
                    g0, gs = W12G[g]
                    r0 = (own * gs + (idx - g0)) * D
                    w.dma("sync", "sST%d" % jb, st1[jb][:, 0:ND * 128],
                          g1g[g][r0:r0 + D, :].rearrange(
                              "(j d) h -> d j h", d=128))
                    w.dma("sync", "sST%d" % jb, st2[jb][:, 0:ND * 128],
                          g2g[g][r0:r0 + D, :].rearrange(
                              "(j d) h -> d j h", d=128))
                    w.wait("tensor", "sST%d" % jb)
                    if jl == 0 and k == 0:
                        w.wait("tensor", "sA", ln2T_done)
                    if jl == 0:
                        w.wait("tensor", "sV", hT_free)
                    jp = jl % 2
                    GB = [0, 1][:NTH] if jp == 0 else [4, 5][:NTH]
                    UB = [2, 3][:NTH] if jp == 0 else [6, 7][:NTH]
                    if jl >= 2:
                        w.wait("tensor", "sV", gu_mult[jl % 2])
                    for dt in range(ND):
                        st = (dt == 0)
                        sp = (dt == ND - 1)
                        for hh in range(NTH):
                            rh = ln2T[:, dt * T + hh * TH:
                                      dt * T + (hh + 1) * TH]
                            w.t("matmul", pf32[GB[hh]][:, 0:TH],
                                st1[jb][:, dt * 128:(dt + 1) * 128],
                                rh, start=st, stop=sp, inc=sp)
                            w.t("matmul", pf32[UB[hh]][:, 0:TH],
                                st2[jb][:, dt * 128:(dt + 1) * 128],
                                rh, start=st, stop=sp, inc=sp)
                    st_used[jb] = w.c["sT"]
                    w.wait("scalar", "sT")
                    w.wait("scalar", "sV", sg_used)
                    for hh in range(NTH):
                        w.s("activation", sgb[:, hh * TH:(hh + 1) * TH],
                            pf32[GB[hh]][:, 0:TH], AF.Silu)
                    w.wait("vector", "sA")
                    if jl == 0:
                        w.wait("vector", "sT", down_prev_t)
                    for hh in range(NTH):
                        w.v("tensor_tensor",
                            hT[:, jl * T + hh * TH:jl * T + (hh + 1) * TH],
                            sgb[:, hh * TH:(hh + 1) * TH],
                            pf32[UB[hh]][:, 0:TH], op=OP.mult)
                    sg_used = w.c["sV"]
                    gu_mult[jl % 2] = w.c["sV"]
                hT_ready = w.c["sV"]
                dn_used = [0, 0]
                x2_used = [0, 0]
                dn_evac = [0, 0]
                for ci in range(ND):
                    own3 = ci // CSH
                    cloc = ci % CSH
                    cb = ci % 2
                    if k == 0 and ci == 0:
                        for rt2 in range(DSHT):
                            w.wait("sync", "sU%d" % (2 * NG12 + rt2), 1)
                    w.wait("sync", "sT", dn_used[cb])
                    r0 = own3 * HFF + k0 * 128
                    w.dma("sync", "sS3%d" % cb, st3[cb][:, 0:nk * 128],
                          g3g[cloc][r0:r0 + nk * 128, :].rearrange(
                              "(j h) c -> h j c", h=128))
                    if k > 0:
                        w.wait("sync", "sPW%d" % cb)
                        if k == len(c["HCHUNKS"]) - 1:
                            w.wait("sync", "sOUT%d" % (ci % 2))
                        w.dma("sync", "sPT%d" % cb, ptl[cb][:, 0:T],
                              part[ci * 128:(ci + 1) * 128, :])
                    last = (k == len(c["HCHUNKS"]) - 1)
                    if last:
                        w.wait("sync", "sCT", stageC_g)
                        w.wait("sync", "sV", x2_used[cb])
                        w.dma("sync", "sPT%d" % cb, x2tt[cb][:, 0:T],
                              x2sp[ci * 128:(ci + 1) * 128, :])
                    w.wait("tensor", "sS3%d" % cb)
                    w.wait("tensor", "sPT%d" % cb)
                    if ci == 0:
                        w.wait("tensor", "sV", hT_ready)
                    DB = [0, 1][:NTH] if cb == 0 else [2, 3][:NTH]
                    if ci >= 2:
                        w.wait("tensor", "sV", dn_evac[cb])
                    for jl in range(nk):
                        st = (jl == 0)
                        sp = (jl == nk - 1)
                        for hh in range(NTH):
                            w.t("matmul", pf32[DB[hh]][:, 0:TH],
                                st3[cb][:, jl * 128:(jl + 1) * 128],
                                hT[:, jl * T + hh * TH:
                                   jl * T + (hh + 1) * TH],
                                start=st, stop=sp, inc=sp)
                    dn_used[cb] = w.c["sT"]
                    w.wait("vector", "sT")
                    w.wait("vector", "sPW%d" % cb)
                    if not last:
                        for hh in range(NTH):
                            hs = slice(hh * TH, (hh + 1) * TH)
                            if k == 0:
                                w.v("tensor_copy", ptl[cb][:, hs],
                                    pf32[DB[hh]][:, 0:TH])
                            else:
                                w.v("tensor_tensor", ptl[cb][:, hs],
                                    ptl[cb][:, hs], pf32[DB[hh]][:, 0:TH],
                                    op=OP.add)
                        dn_evac[cb] = w.c["sV"]
                        w.wait("gpsimd", "sV")
                        w.dma("gpsimd", "sPW%d" % cb,
                              part[ci * 128:(ci + 1) * 128, :],
                              ptl[cb][:, 0:T])
                    else:
                        for hh in range(NTH):
                            hs = slice(hh * TH, (hh + 1) * TH)
                            w.v("tensor_tensor", ptl[cb][:, hs], ptl[cb][:, hs],
                                pf32[DB[hh]][:, 0:TH], op=OP.add)
                        w.v("tensor_tensor", ptl[cb][:, 0:T], ptl[cb][:, 0:T],
                            x2tt[cb][:, 0:T], op=OP.add)
                        x2_used[cb] = w.c["sV"]
                        dn_evac[cb] = w.c["sV"]
                        flush_out(ci, cb, w.c["sV"])
                hT_free = w.c["sV"]
                down_prev_t = w.c["sT"]
            w.wait("sync", "sOUT0")
            w.wait("sync", "sOUT1")
            return w

        @block.sync
        def _(eng):
            script("sync", eng)

        @block.vector
        def _(eng):
            script("vector", eng)

        @block.scalar
        def _(eng):
            script("scalar", eng)

        @block.tensor
        def _(eng):
            script("tensor", eng)

        @block.gpsimd
        def _(eng):
            script("gpsimd", eng)

    return nc


def shard_inputs(inputs, cfg):
    c = derived(cfg)
    D, T, HFF = c["D"], c["T"], c["HFF"]
    SH_ROWS = c["HT_SH"] * 128
    bf = ml_dtypes.bfloat16

    x = np.ascontiguousarray(np.asarray(
        inputs["L0_attn_x"], dtype=np.float32)).reshape(-1, D)
    wq = np.asarray(inputs["L0_attn_Wq"], dtype=np.float32).reshape(D, -1)
    bq = np.asarray(inputs["L0_attn_bq"], dtype=np.float32)
    wk = np.asarray(inputs["L0_attn_Wk"], dtype=np.float32).reshape(D, 128)
    bk = np.asarray(inputs["L0_attn_bk"], dtype=np.float32).reshape(128)
    wv = np.asarray(inputs["L0_attn_Wv"], dtype=np.float32).reshape(D, 128)
    bv = np.asarray(inputs["L0_attn_bv"], dtype=np.float32).reshape(128)
    w1 = np.asarray(inputs["L0_ffn_w1"], dtype=np.float32)
    w2 = np.asarray(inputs["L0_ffn_w2"], dtype=np.float32)
    w3 = np.asarray(inputs["L0_ffn_w3"], dtype=np.float32)

    w1p = np.zeros((SH_ROWS * NCORES, D), dtype=np.float32)
    w1p[:HFF] = w1
    w2p = np.zeros((SH_ROWS * NCORES, D), dtype=np.float32)
    w2p[:HFF] = w2

    bqT = np.ascontiguousarray(bq.T)
    idb = np.eye(128, dtype=np.float32).astype(bf)
    idf = np.eye(128, dtype=np.float32)
    cstv = np.zeros((128, 40), dtype=np.float32)
    cstv[:, 0] = np.float32(1e-5)
    for j in range(15):
        cstv[:, 8 + j] = -NF4_BOUNDS[j]
        cstv[:, 24 + j] = NF4_HDELTA[j]
    cstv[:, 39] = AM_FLOOR

    dsh = D // NCORES
    in_maps = []
    for r in range(NCORES):
        in_maps.append(dict(
            x_in=np.ascontiguousarray(x[r * T:(r + 1) * T]),
            wq_in=np.ascontiguousarray(wq[r * dsh:(r + 1) * dsh]),
            wk_in=np.ascontiguousarray(wk[r * dsh:(r + 1) * dsh]),
            wv_in=np.ascontiguousarray(wv[r * dsh:(r + 1) * dsh]),
            bqT_in=bqT,
            bk_in=np.ascontiguousarray(bk.reshape(1, 128)),
            bv_in=np.ascontiguousarray(bv.reshape(1, 128)),
            w1_in=np.ascontiguousarray(w1p[r * SH_ROWS:(r + 1) * SH_ROWS]),
            w2_in=np.ascontiguousarray(w2p[r * SH_ROWS:(r + 1) * SH_ROWS]),
            w3_in=np.ascontiguousarray(w3[r * dsh:(r + 1) * dsh]),
            idb_in=idb, idf_in=idf, cst_in=cstv,
        ))
    return in_maps


def kernel(**inputs):
    cfg = cfg_full()
    in_maps = shard_inputs(inputs, cfg)
    nc = build_program(cfg)
    res = run_bass_kernel_spmd(nc, in_maps, core_ids=list(range(NCORES)))
    out = np.concatenate(
        [np.ascontiguousarray(np.asarray(r["out"]).T) for r in res.results],
        axis=0)
    return out.reshape(4, 2048, 4096)
